# version 31
# baseline (speedup 1.0000x reference)
import numpy as np
import ml_dtypes
import concourse.bass as bass
import concourse.mybir as mybir
from concourse.bass_utils import run_bass_kernel_spmd

F32 = mybir.dt.float32
BF16 = mybir.dt.bfloat16
AF = mybir.ActivationFunctionType
ALU = mybir.AluOpType
AX = mybir.AxisListType

B, T, D = 4, 4096, 1024
HALF = 2048
H = 1024
NCH = 8


class Prog:

    def __init__(self):
        self.streams = {k: [] for k in ("sync", "tensor", "vector", "scalar", "gpsimd")}
        self.counts = {}

    def emit(self, eng, fn, waits=(), inc=None, inc_n=1):
        waits = tuple((s, v) for (s, v) in waits if v > 0)
        self.streams[eng].append((fn, waits, inc, inc_n))
        if inc is not None:
            self.counts[inc] = self.counts.get(inc, 0) + inc_n


def build_nc(debug=False):
    nc = bass.Bass("TRN2", target_bir_lowering=False, debug=False)

    x_own = nc.dram_tensor("x_own", [HALF, D], F32, kind="ExternalInput").ap()
    x_pre = nc.dram_tensor("x_pre", [HALF, D], F32, kind="ExternalInput").ap()
    Wr_h = nc.dram_tensor("Wr", [D, D], F32, kind="ExternalInput").ap()
    Wk_h = nc.dram_tensor("Wk", [D, D], F32, kind="ExternalInput").ap()
    Wv_h = nc.dram_tensor("Wv", [D, D], F32, kind="ExternalInput").ap()
    Wo_h = nc.dram_tensor("Wo", [D, D], F32, kind="ExternalInput").ap()
    Wf_h = nc.dram_tensor("Wf", [D, D], F32, kind="ExternalInput").ap()
    decay8_h = nc.dram_tensor("decay8", [128, 8], F32, kind="ExternalInput").ap()
    lnw8_h = nc.dram_tensor("lnw8", [128, 8], F32, kind="ExternalInput").ap()
    b_row_h = nc.dram_tensor("b_row", [1, D], F32, kind="ExternalInput").ap()
    ld_h = nc.dram_tensor("lora_down", [4, 100], F32, kind="ExternalInput").ap()
    lu_h = nc.dram_tensor("lora_up", [50, 4], F32, kind="ExternalInput").ap()
    da_h = nc.dram_tensor("down_aux", [100, D], F32, kind="ExternalInput").ap()
    ua_h = nc.dram_tensor("up_aux", [D, 50], F32, kind="ExternalInput").ap()
    mask_h = nc.dram_tensor("mask_col", [128, 1], F32, kind="ExternalInput").ap()
    t0_h = nc.dram_tensor("t0_col", [128, 1], F32, kind="ExternalInput").ap()
    ident_h = nc.dram_tensor("ident", [128, 128], BF16, kind="ExternalInput").ap()
    out_h = nc.dram_tensor("out", [HALF, D], F32, kind="ExternalOutput").ap()

    import contextlib
    with contextlib.ExitStack() as _stk:
        lnT = _stk.enter_context(nc.sbuf_tensor("lnT", [128, NCH * HALF], BF16))
        yA = _stk.enter_context(nc.sbuf_tensor("yA", [128, NCH * HALF], BF16))
        slotA = _stk.enter_context(nc.sbuf_tensor("slotA", [128, NCH * 1024], BF16))
        slotB = _stk.enter_context(nc.sbuf_tensor("slotB", [128, NCH * 1024], BF16))
        slotC = _stk.enter_context(nc.sbuf_tensor("slotC", [128, NCH * 1024], BF16))
        k_m = _stk.enter_context(nc.sbuf_tensor("k_m", [128, H], F32))
        u_m = _stk.enter_context(nc.sbuf_tensor("u_m", [128, H], F32))
        csbuf = _stk.enter_context(nc.sbuf_tensor("csbuf", [128, H], F32))
        rbuf = _stk.enter_context(nc.sbuf_tensor("rbuf", [128, H], F32))
        k_m1 = _stk.enter_context(nc.sbuf_tensor("k_m1", [128, H], F32))
        u_m1 = _stk.enter_context(nc.sbuf_tensor("u_m1", [128, H], F32))
        csbuf1 = _stk.enter_context(nc.sbuf_tensor("csbuf1", [128, H], F32))
        rbuf1 = _stk.enter_context(nc.sbuf_tensor("rbuf1", [128, H], F32))
        y2A = _stk.enter_context(nc.sbuf_tensor("y2A", [128, NCH * H], BF16))
        castbuf = _stk.enter_context(nc.sbuf_tensor("castbuf", [128, 1024], BF16))
        castbuf2 = _stk.enter_context(nc.sbuf_tensor("castbuf2", [128, 1024], BF16))
        xt0 = _stk.enter_context(nc.sbuf_tensor("xt0", [128, D], F32))
        xt1 = _stk.enter_context(nc.sbuf_tensor("xt1", [128, D], F32))
        x1t0 = _stk.enter_context(nc.sbuf_tensor("x1t0", [128, D], F32))
        x1t1 = _stk.enter_context(nc.sbuf_tensor("x1t1", [128, D], F32))
        x1tc = _stk.enter_context(nc.sbuf_tensor("x1tc", [128, D], F32))
        xtp = x1tc
        wstage = _stk.enter_context(nc.sbuf_tensor("wstage", [128, D], F32))
        iota = _stk.enter_context(nc.sbuf_tensor("iota", [128, H], F32))
        ones_t = _stk.enter_context(nc.sbuf_tensor("ones_t", [128, H], BF16))
        onesc = _stk.enter_context(nc.sbuf_tensor("onesc", [128, 1], BF16))
        b_bcast = _stk.enter_context(nc.sbuf_tensor("b_bcast", [128, D], F32))
        ident_t = _stk.enter_context(nc.sbuf_tensor("ident_t", [128, 128], BF16))
        small = _stk.enter_context(nc.sbuf_tensor("small", [128, 64], F32))
        bias_own = _stk.enter_context(nc.sbuf_tensor("bias_own", [128, 16], F32))
        bias_ownN = _stk.enter_context(nc.sbuf_tensor("bias_ownN", [128, 16], F32))
        bias_pre = _stk.enter_context(nc.sbuf_tensor("bias_pre", [128, 16], F32))
        bn0 = _stk.enter_context(nc.sbuf_tensor("bn0", [128, 12], F32))
        mvA = _stk.enter_context(nc.sbuf_tensor("mvA", [128, 2], F32))
        mvB = _stk.enter_context(nc.sbuf_tensor("mvB", [128, 2], F32))
        colsA = _stk.enter_context(nc.sbuf_tensor("colsA", [128, 4], F32))
        colsB = _stk.enter_context(nc.sbuf_tensor("colsB", [128, 4], F32))
        srow = x1t0
        invcols = _stk.enter_context(nc.sbuf_tensor("invcols", [128, 32], F32))
        ldT = _stk.enter_context(nc.sbuf_tensor("ldT", [100, 4], BF16))
        luT = _stk.enter_context(nc.sbuf_tensor("luT", [4, 50], BF16))
        uaT = _stk.enter_context(nc.sbuf_tensor("uaT", [50, D], BF16))
        m1 = _stk.enter_context(nc.sbuf_tensor("m1", [4, D], BF16))
        m2 = _stk.enter_context(nc.sbuf_tensor("m2", [50, D], BF16))
        psK = _stk.enter_context(nc.psum_tensor("psK", [128, 1024], F32))
        psV = _stk.enter_context(nc.psum_tensor("psV", [128, 1024], F32))
        psR = _stk.enter_context(nc.psum_tensor("psR", [128, 1024], F32))
        psTb = _stk.enter_context(nc.psum_tensor("psTb", [128, 1024], BF16))
        psS = _stk.enter_context(nc.psum_tensor("psS", [1, 512], F32))
        s_ld = _stk.enter_context(nc.semaphore("s_ld"))
        s_st = _stk.enter_context(nc.semaphore("s_st"))
        s_mm = _stk.enter_context(nc.semaphore("s_mm"))
        s_ve = _stk.enter_context(nc.semaphore("s_ve"))
        s_sc = _stk.enter_context(nc.semaphore("s_sc"))
        s_gp = _stk.enter_context(nc.semaphore("s_gp"))
        s_x0 = _stk.enter_context(nc.semaphore("s_x0"))
        s_x1 = _stk.enter_context(nc.semaphore("s_x1"))
        s_x2 = _stk.enter_context(nc.semaphore("s_x2"))
        s_w0 = _stk.enter_context(nc.semaphore("s_w0"))
        s_w1 = _stk.enter_context(nc.semaphore("s_w1"))
        st0 = _stk.enter_context(nc.semaphore("st0"))
        st1 = _stk.enter_context(nc.semaphore("st1"))
        st2 = _stk.enter_context(nc.semaphore("st2"))
        block = _stk.enter_context(nc.Block())
        p = Prog()
        SEM = {"ld": s_ld, "st": s_st, "mm": s_mm, "ve": s_ve, "sc": s_sc,
               "gp": s_gp, "x0": s_x0, "x1": s_x1, "x2": s_x2,
               "w0": s_w0, "w1": s_w1,
               "st0": st0, "st1": st1, "st2": st2}

        def c(name):
            return p.counts.get(SEM[name], 0)

        def E(eng, semname=None, waits=(), inc_n=1):
            def run(fn):
                p.emit(eng, fn,
                       waits=[(SEM[s], v) for (s, v) in waits],
                       inc=(SEM[semname] if semname else None), inc_n=inc_n)
            return run

        def dbg(name, ap):
            if not debug:
                return
            t = nc.dram_tensor("dbg_" + name, list(ap.shape), ap.dtype,
                               kind="ExternalOutput").ap()
            w = [(SEM[s], p.counts.get(SEM[s], 0)) for s in
                 ("mm", "ve", "sc", "gp")]
            p.emit("sync", (lambda e, t=t, ap=ap: e.dma_start(out=t, in_=ap)),
                   waits=w, inc=SEM["ld"], inc_n=16)
            ldv = c("ld")
            for engn in ("sync", "tensor", "vector", "scalar", "gpsimd"):
                p.emit(engn, (lambda e, v=ldv: e.wait_ge(s_ld, v)))

        def wtile(slot, kc, mc):
            return slot[:, kc * 1024 + mc * 128: kc * 1024 + (mc + 1) * 128]

        def wmov(slot, kc, j):
            return slot[:, kc * 1024 + j * 512: kc * 1024 + (j + 1) * 512]

        def lnT_mt(m, t0, n):
            return lnT[:, m * HALF + t0: m * HALF + t0 + n]

        def yA_mt(m, t0, n):
            return yA[:, m * HALF + t0: m * HALF + t0 + n]

        lnT3 = lnT.rearrange("p (m t) -> p m t", m=8)
        psTb3 = psTb.rearrange("p (m t) -> p m t", m=8)

        @E("sync", "ld", inc_n=16)
        def _(e): return e.dma_start(out=small[:, 8:16], in_=decay8_h)
        @E("sync", "ld", inc_n=16)
        def _(e): return e.dma_start(out=small[:, 0:1], in_=mask_h)
        @E("sync", "ld", inc_n=16)
        def _(e): return e.dma_start(out=small[:, 1:2], in_=t0_h)
        @E("sync", "ld", inc_n=16)
        def _(e): return e.dma_start(out=ident_t[:], in_=ident_h)
        @E("sync", "ld", inc_n=16)
        def _(e): return e.dma_start(out=small[:, 56:64], in_=lnw8_h)
        ld_base = c("ld")

        @E("gpsimd", "gp")
        def _(e): return e.iota(iota[:], pattern=[[1, H]], base=0,
                                channel_multiplier=0,
                                allow_small_or_imprecise_dtypes=True)
        @E("gpsimd", "gp")
        def _(e): return e.memset(ones_t[:], 1.0)
        @E("gpsimd", "gp")
        def _(e): return e.memset(onesc[:], 1.0)
        @E("gpsimd", "gp")
        def _(e): return e.memset(small[:, 32:40], 0.0)
        @E("gpsimd", "gp")
        def _(e): return e.memset(small[:, 3:4], 1e-5)
        @E("gpsimd", "gp")
        def _(e): return e.memset(small[:, 4:5], 1e-6)
        @E("gpsimd", "gp")
        def _(e): return e.memset(small[:, 5:6], 4e-6)
        gp_eps = c("gp")

        @E("scalar", "sc", waits=[("ld", ld_base)])
        def _(e): return e.activation(out=small[:, 40:48], in_=small[:, 8:16],
                                      func=AF.Tanh, scale=0.5)
        @E("vector", "ve", waits=[("sc", c("sc"))])
        def _(e):
            return e.tensor_scalar(small[:, 8:16], small[:, 40:48], 0.5, 0.5,
                                   ALU.mult, ALU.add)
        @E("vector", "ve", waits=[("ve", c("ve"))])
        def _(e): return e.tensor_scalar_max(small[:, 40:48], small[:, 8:16], 1e-7)
        ve_dec = c("ve")
        @E("scalar", "sc", waits=[("ve", ve_dec)])
        def _(e): return e.activation(out=small[:, 24:32], in_=small[:, 40:48],
                                      func=AF.Ln)
        sc_lnd = c("sc")
        @E("vector", "ve", waits=[("sc", sc_lnd)])
        def _(e): return e.tensor_scalar_mul(small[:, 16:24], small[:, 24:32], -1.0)
        @E("vector", "ve", waits=[("ve", c("ve"))])
        def _(e): return e.tensor_scalar_add(small[:, 2:3], small[:, 1:2], 1024.0)
        for h in range(2):
            @E("vector", "ve", waits=[("ve", c("ve"))])
            def _(e, h=h):
                return e.tensor_scalar(bias_own[:, h * 8:(h + 1) * 8],
                                       small[:, 16:24], small[:, 1 + h:2 + h],
                                       None, ALU.mult)
            @E("vector", "ve", waits=[("ve", c("ve"))])
            def _(e, h=h):
                return e.tensor_scalar_mul(bias_pre[:, h * 8:(h + 1) * 8],
                                           small[:, 16:24], float(h * 1024))
        @E("vector", "ve", waits=[("ve", c("ve"))])
        def _(e): return e.tensor_scalar_mul(bias_ownN[:], bias_own[:], -1.0)
        ve_bias = c("ve")

        wstate = {"cast": {}, "tp": {}}
        psTbg = {"sc": 0}

        def weight_round(src_h, slot, rc, stage, cbuf,
                         scale_lnw=False, m3_ps=None, m3_w=(), out_ve=None,
                         dma_w=()):
            sid, cid = id(stage), id(cbuf)
            ws = "w0" if stage is wstage else "w1"
            @E("sync", ws, inc_n=16,
               waits=[("ve", wstate["cast"].get(sid, 0))] + list(dma_w))
            def _(e, rc=rc, stage=stage):
                return e.dma_start(out=stage[:],
                                   in_=src_h[rc * 128:(rc + 1) * 128, :])
            ldv = c(ws)
            if m3_ps is not None:
                for j in range(2):
                    @E("tensor", "mm",
                       waits=(([(ws, ldv), ("sc", c("sc"))] + list(m3_w))
                              if j == 0 else ()))
                    def _(e, rc=rc, j=j, m3_ps=m3_ps):
                        return e.matmul(m3_ps[:, j * 512:(j + 1) * 512],
                                        uaT[:, rc * 128:(rc + 1) * 128],
                                        m2[:, j * 512:(j + 1) * 512],
                                        start=True, stop=True)
                @E("vector", "ve", waits=[("mm", c("mm")), (ws, ldv)])
                def _(e, stage=stage, m3_ps=m3_ps):
                    return e.tensor_add(stage[:], stage[:], m3_ps[:])
                if out_ve is not None:
                    out_ve.append(c("ve"))
                cast_w = [("ve", c("ve")), ("mm", wstate["tp"].get(cid, 0))]
            else:
                cast_w = [(ws, ldv), ("mm", wstate["tp"].get(cid, 0))]
            @E("vector", "ve", waits=cast_w)
            def _(e, stage=stage, cbuf=cbuf):
                return e.tensor_copy(cbuf[:], stage[:])
            wstate["cast"][sid] = vecast = c("ve")
            for kc in range(8):
                @E("tensor", "mm",
                   waits=([("ve", vecast), ("sc", psTbg["sc"])] if kc == 0 else ()))
                def _(e, kc=kc, cbuf=cbuf):
                    return e.transpose(psTb[:, kc * 128:(kc + 1) * 128],
                                       cbuf[:, kc * 128:(kc + 1) * 128],
                                       ident_t[:])
            wstate["tp"][cid] = mmt = c("mm")
            if scale_lnw:
                for kc in range(8):
                    @E("scalar", "sc", waits=([("mm", mmt)] if kc == 0 else ()))
                    def _(e, kc=kc, rc=rc, slot=slot):
                        return e.activation(
                            out=slot[:, kc * 1024 + rc * 128:
                                     kc * 1024 + (rc + 1) * 128],
                            in_=psTb[:, kc * 128:(kc + 1) * 128],
                            func=AF.Copy, scale=small[:, 56 + kc:57 + kc])
            else:
                slot3 = slot.rearrange("p (kc e) -> p kc e", kc=8)
                @E("scalar", "sc", waits=[("mm", mmt)])
                def _(e, rc=rc, slot3=slot3):
                    return e.copy(out=slot3[:, :, rc * 128:(rc + 1) * 128],
                                  in_=psTb3[:])
            psTbg["sc"] = c("sc")
            return c("sc")

        last_apply = {0: 0, 1: 0}
        last_bn = {0: 0, 1: 0}
        last_tp = {0: 0, 1: 0}
        last_ln = {0: 0, 1: 0}
        last_applyx = {0: 0, 1: 0, 2: 0}
        last_bnx = {0: 0, 1: 0, 2: 0}

        def ln_chunk_p1(src_h, i, tag="", use_castbuf=False):
            par = i % 2
            slot = i % 3
            xt = (xt0, xt1, xtp)[slot]
            mv = mvA if par == 0 else mvB
            cols = colsA if par == 0 else colsB
            if use_castbuf:
                lnc = castbuf[:] if par == 0 else castbuf2[:]
            else:
                lnc = yA[:, 1024:2048] if par == 0 else yA[:, 2048:3072]
            xs = ("x0", "x1", "x2")[slot]
            @E("sync", xs, inc_n=16,
               waits=[("sc", last_applyx[slot]), ("ve", last_bnx[slot])])
            def _(e, i=i, xt=xt):
                return e.dma_start(out=xt[:], in_=src_h[i * 128:(i + 1) * 128, :])
            ldx = c(xs)
            @E("vector", "ve", waits=[(xs, ldx), ("sc", last_ln[par])])
            def _(e, xt=xt): return e.bn_stats(bn0[:, 0:6], xt[:, 0:512])
            @E("vector", "ve")
            def _(e, xt=xt): return e.bn_stats(bn0[:, 6:12], xt[:, 512:1024])
            @E("vector", "ve", waits=[("ve", c("ve"))])
            def _(e, mv=mv): return e.bn_aggr(mv[:], bn0[:])
            last_bn[par] = last_bnx[slot] = ve_mv = c("ve")
            if i == 0 and tag:
                dbg(tag + "_x0", xt[:])
                dbg(tag + "_mv0", mv[:])
            @E("scalar", "sc", waits=[("ve", ve_mv), ("gp", gp_eps)])
            def _(e, mv=mv, cols=cols):
                return e.activation(out=cols[:, 0:1], in_=mv[:, 1:2],
                                    func=AF.Ln, bias=small[:, 3:4])
            last_ln[par] = c("sc")
            @E("scalar", "sc", waits=[("sc", c("sc"))])
            def _(e, cols=cols):
                return e.activation(out=cols[:, 1:2], in_=cols[:, 0:1],
                                    func=AF.Exp, scale=-0.5)
            sc_inv = c("sc")
            @E("vector", "ve", waits=[("sc", sc_inv)])
            def _(e, mv=mv, cols=cols):
                return e.scalar_tensor_tensor(cols[:, 2:3], mv[:, 0:1], -1.0,
                                              cols[:, 1:2], ALU.mult, ALU.mult)
            ve_negmi = c("ve")
            @E("scalar", "sc", waits=[("ve", ve_negmi), ("mm", last_tp[par])])
            def _(e, xt=xt, cols=cols, lnc=lnc):
                return e.activation(out=lnc, in_=xt[:], func=AF.Identity,
                                    bias=cols[:, 2:3], scale=cols[:, 1:2])
            last_apply[par] = last_applyx[slot] = sc_apply = c("sc")
            return dict(i=i, par=par, lnc=lnc, sc_apply=sc_apply)

        def ln_chunk_p2(ctx, ready):
            i, par, lnc = ctx["i"], ctx["par"], ctx["lnc"]
            for j in range(8):
                @E("tensor", "mm",
                   waits=([("sc", max(ctx["sc_apply"], psTbg["sc"]))]
                          if j == 0 else ()))
                def _(e, j=j, lnc=lnc):
                    return e.transpose(psTb[:, j * 128:(j + 1) * 128],
                                       lnc[:, j * 128:(j + 1) * 128], ident_t[:])
            last_tp[par] = mmt = c("mm")
            @E("scalar", "sc", waits=[("mm", mmt)])
            def _(e, i=i):
                return e.copy(out=lnT3[:, :, i * 128:(i + 1) * 128], in_=psTb3[:])
            psTbg["sc"] = c("sc")
            ready.append(c("sc"))

        def proj(slot, m, h, ps, w0=(), w1=()):
            cnt = 0
            for kc in range(8):
                for j in range(2):
                    w = list(w0) if cnt == 0 else (list(w1) if cnt == 1 else [])
                    cnt += 1
                    @E("tensor", "mm", waits=w)
                    def _(e, kc=kc, j=j, m=m, h=h, ps=ps, slot=slot):
                        return e.matmul(ps[:, j * 512:(j + 1) * 512],
                                        wtile(slot, kc, m),
                                        lnT_mt(kc, h * H + j * 512, 512),
                                        start=(kc == 0), stop=(kc == 7))
            return c("mm")

        BUFS = [(k_m, u_m, csbuf, rbuf), (k_m1, u_m1, csbuf1, rbuf1)]

        def prefix_iter(h, m, trk, lnready):
            par = (h * 8 + m) % 2
            kB, uB, cB, rB = BUFS[par]
            lnw = [("sc", max(trk["last_sck"], lnready[h * 8 + 3], trk["wkd"]))]
            lnw1 = [("sc", max(lnready[h * 8 + 7], trk["wvd"]))]
            mmk = proj(slotA, m, h, psK, w0=lnw, w1=lnw1)
            mmv = proj(slotB, m, h, psV, w0=[("ve", trk["last_kv"])])
            @E("scalar", "sc", waits=[("mm", mmk), ("gp", trk["gpu"][par]),
                                      ("ve", trk["red"][par])])
            def _(e, kB=kB): return e.copy(out=kB[:], in_=psK[:])
            trk["last_sck"] = sck = c("sc")
            @E("vector", "ve", waits=[("sc", sck), ("mm", mmv),
                                      ("gp", trk["gpu"][par])])
            def _(e, kB=kB, uB=uB): return e.tensor_mul(uB[:], kB[:], psV[:])
            trk["last_kv"] = c("ve")
            @E("scalar", "sc", waits=[("ve", ve_bias),
                                      ("ve", trk["min"][par])])
            def _(e, m=m, h=h, cB=cB):
                return e.activation(out=cB[:], in_=iota[:],
                                    func=AF.Identity,
                                    scale=small[:, 16 + m:17 + m],
                                    bias=bias_pre[:, h * 8 + m:h * 8 + m + 1])
            sc_c = c("sc")
            @E("vector", "ve", waits=[("sc", sc_c), ("gp", trk["gpu"][par])])
            def _(e, cB=cB, rB=rB):
                return e.tensor_scalar_min(rB[:], cB[:], 23.025850929940457)
            trk["min"][par] = ve_c = c("ve")
            @E("scalar", "sc", waits=[("ve", ve_c)])
            def _(e, cB=cB, rB=rB):
                return e.activation(out=cB[:], in_=rB[:], func=AF.Exp)
            sc_ce = c("sc")
            @E("gpsimd", "gp", waits=[("sc", sc_ce)])
            def _(e, kB=kB, uB=uB, cB=cB):
                return e.tensor_mul(kB[:], uB[:], cB[:])
            trk["gpu"][par] = gp_u = c("gp")
            @E("vector", "ve", waits=[("gp", gp_u)])
            def _(e, kB=kB, uB=uB):
                return e.tensor_reduce(uB[:, 0:1], kB[:], AX.X, ALU.add)
            dst = 32 + m if h == 0 else 8 + m
            @E("vector", "ve", waits=[("ve", c("ve"))])
            def _(e, dst=dst, uB=uB):
                return e.tensor_copy(small[:, dst:dst + 1], uB[:, 0:1])
            trk["red"][par] = c("ve")

        def carry_combine():
            @E("vector", "ve", waits=[("gp", c("gp")), ("ve", c("ve"))])
            def _(e):
                return e.tensor_add(small[:, 48:56], small[:, 32:40], small[:, 8:16])
            @E("vector", "ve", waits=[("ve", c("ve"))])
            def _(e):
                return e.tensor_scalar(small[:, 40:48], small[:, 48:56],
                                       small[:, 0:1], None, ALU.mult)

        last_srow_dma = {0: 0}
        sc_inv_h = {0: 0, 1: 0}

        def mid_iter(h, m, trk, lnready, post_clamp=None):
            par = (h * 8 + m) % 2
            kB, uB, cB, rB = BUFS[par]
            lnw = [("sc", max(trk["last_sck"], lnready[h * 8 + 3]))]
            lnw1 = [("sc", lnready[h * 8 + 7])]
            mmk = proj(slotA, m, h, psK, w0=lnw, w1=lnw1)
            mmv = proj(slotB, m, h, psV, w0=[("ve", trk["last_kv"])])
            mmr = proj(slotC, m, h, psR,
                       w0=[("sc", max(trk["last_th"], trk["wrd"]))])
            @E("scalar", "sc", waits=[("mm", mmk), ("ve", trk["yst"][par]),
                                      ("gp", trk["gpu"][par])])
            def _(e, kB=kB): return e.copy(out=kB[:], in_=psK[:])
            trk["last_sck"] = sck = c("sc")
            @E("vector", "ve", waits=[("sc", sck), ("mm", mmv),
                                      ("gp", trk["gpu"][par])])
            def _(e, kB=kB, uB=uB): return e.tensor_mul(uB[:], kB[:], psV[:])
            trk["last_kv"] = c("ve")
            @E("scalar", "sc", waits=[("ve", trk["state"][par])])
            def _(e, m=m, h=h, rB=rB):
                return e.activation(out=rB[:], in_=iota[:], func=AF.Exp,
                                    scale=small[:, 24 + m:25 + m],
                                    bias=bias_ownN[:, h * 8 + m:h * 8 + m + 1])
            @E("scalar", "sc", waits=[("ve", trk["yst"][par]),
                                      ("gp", trk["gpu"][par])])
            def _(e, m=m, h=h, cB=cB):
                return e.activation(out=cB[:], in_=iota[:], func=AF.Identity,
                                    scale=small[:, 16 + m:17 + m],
                                    bias=bias_own[:, h * 8 + m:h * 8 + m + 1])
            sc_c2 = c("sc")
            @E("vector", "ve", waits=[("sc", sc_c2), ("ve", c("ve"))])
            def _(e, kB=kB, cB=cB):
                return e.tensor_scalar_min(kB[:], cB[:], 23.025850929940457)
            ve_c = c("ve")
            @E("scalar", "sc", waits=[("ve", ve_c)])
            def _(e, kB=kB, cB=cB):
                return e.activation(out=cB[:], in_=kB[:], func=AF.Exp)
            sc_ce = c("sc")
            if post_clamp is not None:
                post_clamp()
            @E("gpsimd", "gp", waits=[("sc", sc_ce)])
            def _(e, kB=kB, uB=uB, cB=cB):
                return e.tensor_mul(kB[:], uB[:], cB[:])
            trk["gpu"][par] = gp_u = c("gp")
            init = (small[:, 40 + m:41 + m] if h == 0
                    else small[:, 48 + m:49 + m])
            @E("vector", "ve", waits=[("gp", gp_u), ("ve", c("ve"))])
            def _(e, init=init, uB=uB, kB=kB):
                return e.tensor_tensor_scan(out=uB[:], data0=ones_t[:],
                                            data1=kB[:], initial=init,
                                            op0=ALU.mult, op1=ALU.add)
            ve_scan = c("ve")
            if h == 0:
                @E("vector", "ve", waits=[("ve", c("ve"))])
                def _(e, m=m, uB=uB):
                    return e.tensor_copy(small[:, 48 + m:49 + m], uB[:, H - 1:H])
            @E("vector", "ve", waits=[("gp", c("gp"))])
            def _(e, kB=kB, uB=uB, rB=rB):
                return e.tensor_mul(kB[:], uB[:], rB[:])
            trk["state"][par] = c("ve")
            @E("scalar", "sc", waits=[("mm", mmr), ("ve", ve_scan)])
            def _(e, cB=cB): return e.activation(out=cB[:], in_=psR[:],
                                                 func=AF.Tanh, scale=0.5)
            trk["last_th"] = sc_r = c("sc")
            @E("vector", "ve", waits=[("sc", sc_r)])
            def _(e, m=m, h=h, kB=kB, cB=cB):
                return e.scalar_tensor_tensor(yA_mt(m, h * H, H), cB[:],
                                              1.0, kB[:], ALU.add, ALU.mult)
            trk["yst"][par] = c("ve")
            y2w = [("mm", trk["mm_stats"])] if h == 1 else []
            @E("vector", "ve", waits=y2w)
            def _(e, m=m, h=h):
                return e.tensor_mul(y2A[:, m * H:(m + 1) * H],
                                    yA_mt(m, h * H, H), yA_mt(m, h * H, H))
            trk["y2"] = c("ve")
            trk["y2l"][m] = c("ve")
            if m == 7:
                trk["y2_h" + str(h)] = c("ve")

        def mid_half_end(h, trk):
            ve_y2 = trk["y2"]
            sc_evs = []
            for j in range(2):
                for mc in range(8):
                    w = [("ve", trk["y2l"][mc])]
                    if mc == 0 and sc_evs:
                        w.append(("sc", sc_evs[-1]))
                    @E("tensor", "mm", waits=w)
                    def _(e, j=j, mc=mc):
                        return e.matmul(psS[:], onesc[:],
                                        y2A[:, mc * H + j * 512:
                                            mc * H + (j + 1) * 512],
                                        start=(mc == 0), stop=(mc == 7))
                mm_st = c("mm")
                @E("scalar", "sc", waits=[("mm", mm_st),
                                          ("ld", last_srow_dma[0])])
                def _(e, j=j):
                    return e.copy(out=srow[0:1, j * 512:(j + 1) * 512], in_=psS[:])
                sc_evs.append(c("sc"))
            trk["mm_stats"] = c("mm")
            sc_sr = sc_evs[-1]
            for i2 in range(8):
                @E("sync", "ld", inc_n=16,
                   waits=([("sc", sc_sr)] if i2 == 0 else ()))
                def _(e, i2=i2, h=h):
                    return e.dma_start(out=invcols[:, 16 * h + i2:16 * h + i2 + 1],
                                       in_=srow[0:1, i2 * 128:(i2 + 1) * 128])
            last_srow_dma[0] = ld_s = c("ld")
            @E("scalar", "sc", waits=[("ld", ld_s)])
            def _(e, h=h):
                return e.activation(out=invcols[:, 16 * h + 8:16 * h + 16],
                                    in_=invcols[:, 16 * h:16 * h + 8],
                                    func=AF.Ln, scale=1.0 / 1024.0,
                                    bias=small[:, 5:6])
            @E("scalar", "sc", waits=[("sc", c("sc"))])
            def _(e, h=h):
                return e.activation(out=invcols[:, 16 * h:16 * h + 8],
                                    in_=invcols[:, 16 * h + 8:16 * h + 16],
                                    func=AF.Exp, scale=-0.5)
            sc_inv_h[h] = c("sc")

        def part3(sc_swap, ld_he1, gp_all):
            @E("sync", "ld", inc_n=16, waits=[("ld", c("ld"))])
            def _(e): return e.dma_start(out=b_bcast[:],
                                         in_=b_row_h.broadcast_to((128, D)))
            ld_b = c("ld")

            last_x1 = {0: 0, 1: 0}
            last_dma = {0: 0, 1: 0}
            last_stt = {0: 0, 1: 0}
            st_l1 = {"l1": 0, "oa": 0}
            oring = [k_m, u_m, csbuf]
            oring_st = [0, 0, 0]
            oabuf = [rbuf, k_m1]
            last_sttx = {0: 0, 1: 0}
            ld_pre = {}
            pendB, pendC = [], []

            def stageB(ctx, mid_emit=None):
                ig, h, i = ctx["ig"], ctx["h"], ctx["i"]
                cb = ctx["cb"]
                for j in range(8):
                    @E("tensor", "mm",
                       waits=([("sc", max(ctx["sc_ap"], psTbg["sc"]))]
                              if j == 0 else ()))
                    def _(e, j=j, cb=cb):
                        return e.transpose(psTb[:, j * 128:(j + 1) * 128],
                                           cb[:, j * 128:(j + 1) * 128],
                                           ident_t[:])
                last_tp[ig % 2] = mmt = c("mm")
                @E("scalar", "sc", waits=[("mm", mmt)])
                def _(e, ig=ig):
                    return e.copy(out=lnT3[:, :, ig * 128:(ig + 1) * 128],
                                  in_=psTb3[:])
                psTbg["sc"] = st_l1["l1"] = sc_l1 = c("sc")
                if mid_emit is not None:
                    mid_emit()
                first = True
                for m in range(8):
                    for j in range(2):
                        w = [("sc", sc_l1), ("ve", st_l1["oa"])] if first else []
                        first = False
                        @E("tensor", "mm", waits=w)
                        def _(e, m=m, j=j, h=h, i=i):
                            return e.matmul(psV[:, j * 512:(j + 1) * 512],
                                            lnT_mt(m, h * H + i * 128, 128),
                                            wmov(slotB, m, j),
                                            start=(m == 0), stop=(m == 7))
                ctx["mm_ffn"] = c("mm")
                pendC.append(ctx)

            def stageC(ctx):
                ig, par = ctx["ig"], ctx["par"]
                x1t, oa, cols = ctx["x1t"], ctx["oa"], ctx["cols"]
                ot = oring[ig % 3]
                @E("vector", "ve", waits=[("mm", ctx["mm_ffn"]),
                                          ("sc", ctx["sc_inv"]),
                                          ("gp", gp_all)])
                def _(e, x1t=x1t, oa=oa, cols=cols):
                    return e.scalar_tensor_tensor(oa[:], psV[:], cols[:, 1:2],
                                                  x1t[:], ALU.mult, ALU.add)
                st_l1["oa"] = c("ve")
                @E("vector", "ve", waits=[("st" + str(ig % 3), oring_st[ig % 3]),
                                          ("ld", ld_b), ("gp", gp_all)])
                def _(e, ot=ot, oa=oa):
                    return e.tensor_add(ot[:], oa[:], b_bcast[:])
                last_x1[par] = ve_out = c("ve")
                sts = "st" + str(ig % 3)
                @E("sync", sts, inc_n=16, waits=[("ve", ve_out)])
                def _(e, ig=ig, ot=ot):
                    return e.dma_start(out=out_h[ig * 128:(ig + 1) * 128, :],
                                       in_=ot[:])
                oring_st[ig % 3] = c(sts)

            wo_mm = {}

            def emit_wo(ig):
                if ig > 15:
                    return
                h, i = ig // 8, ig % 8
                psWo = psK if ig % 2 == 0 else psR
                wpar = ig % 2
                first = True
                for m in range(8):
                    for j in range(2):
                        w = ([("ve", last_stt[wpar]), ("sc", sc_inv_h[h]),
                              ("sc", sc_swap)] if first else [])
                        first = False
                        @E("tensor", "mm", waits=w)
                        def _(e, m=m, j=j, h=h, i=i, psWo=psWo):
                            return e.matmul(psWo[:, j * 512:(j + 1) * 512],
                                            yA_mt(m, h * H + i * 128, 128),
                                            wmov(slotA, m, j),
                                            start=(m == 0), stop=(m == 7))
                wo_mm[ig] = c("mm")

            x1ring = [x1t0, x1t1, x1tc]
            last_apply_r = {0: 0, 1: 0, 2: last_applyx[2]}
            for h in range(2):
                for i in range(8):
                    ig = h * 8 + i
                    par = i % 2
                    xt = xt0 if par == 0 else xt1
                    x1t = x1ring[ig % 3]
                    mv = mvA if par == 0 else mvB
                    cols = colsA if par == 0 else colsB
                    cb = castbuf if ig % 2 == 0 else castbuf2
                    psWo = psK if ig % 2 == 0 else psR
                    wpar = ig % 2
                    if ig == 0:
                        emit_wo(0)
                        emit_wo(1)
                    else:
                        stageB(pendB.pop(0), mid_emit=lambda ig=ig: emit_wo(ig + 1))
                    mm_wo = wo_mm[ig]
                    for igp in ([ig, ig + 1] if ig == 0 else [ig + 1]):
                        if igp > 15:
                            continue
                        parp = igp % 8 % 2
                        xtp = xt0 if parp == 0 else xt1
                        xsp = "x0" if parp == 0 else "x1"
                        @E("sync", xsp, inc_n=16,
                           waits=[("ve", last_sttx[parp]),
                                  ("sc", sc_swap)])
                        def _(e, igp=igp, xtp=xtp):
                            return e.dma_start(
                                out=xtp[:],
                                in_=x_own[igp * 128:(igp + 1) * 128, :])
                        ld_pre[igp] = (xsp, c(xsp))
                    ld_x = ld_pre[ig]
                    @E("vector", "ve",
                       waits=[("mm", mm_wo), (ld_x[0], ld_x[1]),
                              ("ld", ld_he1), ("sc", last_apply_r[ig % 3])])
                    def _(e, h=h, i=i, xt=xt, x1t=x1t, psWo=psWo):
                        return e.scalar_tensor_tensor(
                            x1t[:], psWo[:], invcols[:, 16 * h + i:16 * h + i + 1],
                            xt[:], ALU.mult, ALU.add)
                    last_stt[wpar] = last_sttx[par] = c("ve")
                    @E("vector", "ve", waits=[("sc", last_ln[par])])
                    def _(e, x1t=x1t): return e.bn_stats(bn0[:, 0:6], x1t[:, 0:512])
                    @E("vector", "ve")
                    def _(e, x1t=x1t): return e.bn_stats(bn0[:, 6:12], x1t[:, 512:1024])
                    @E("vector", "ve", waits=[("ve", c("ve"))])
                    def _(e, mv=mv): return e.bn_aggr(mv[:], bn0[:])
                    last_bn[par] = ve_mv = c("ve")
                    @E("vector", "ve", waits=[("ve", ve_mv)])
                    def _(e, mv=mv, cols=cols):
                        return e.tensor_scalar_mul(cols[:, 2:3], mv[:, 0:1], -1.0)
                    ve_negm = c("ve")
                    @E("scalar", "sc", waits=[("ve", ve_negm),
                                              ("mm", last_tp[ig % 2])])
                    def _(e, x1t=x1t, cols=cols, cb=cb):
                        return e.activation(out=cb[:], in_=x1t[:],
                                            func=AF.Identity,
                                            bias=cols[:, 2:3], scale=1.0)
                    last_apply_r[ig % 3] = sc_ap = c("sc")
                    @E("scalar", "sc", waits=[("ve", ve_mv)])
                    def _(e, mv=mv, cols=cols):
                        return e.activation(out=cols[:, 0:1], in_=mv[:, 1:2],
                                            func=AF.Ln, bias=small[:, 3:4])
                    last_ln[par] = c("sc")
                    @E("scalar", "sc", waits=[("sc", c("sc"))])
                    def _(e, cols=cols):
                        return e.activation(out=cols[:, 1:2], in_=cols[:, 0:1],
                                            func=AF.Exp, scale=-0.5)
                    sc_inv_i = c("sc")
                    pendB.append(dict(ig=ig, h=h, i=i, par=par, x1t=x1t,
                                      oa=oabuf[ig % 2], cb=cb, sc_ap=sc_ap,
                                      sc_inv=sc_inv_i, cols=cols))
                    if pendC:
                        stageC(pendC.pop(0))
            while pendB:
                stageB(pendB.pop(0))
                if pendC:
                    stageC(pendC.pop(0))
            while pendC:
                stageC(pendC.pop(0))

        dbg("small_pro", small[:])
        rdy_pre = []
        rounds = ([(Wk_h, slotA, rc) for rc in range(8)]
                  + [(Wv_h, slotB, rc) for rc in range(8)]
                  + [(Wr_h, slotC, rc) for rc in range(8)])
        slot_done = {}
        ri = 0
        lsteps = []

        def _l_ld():
            @E("sync", "ld", inc_n=16, waits=[("ld", c("ld"))])
            def _(e): return e.dma_start(out=x1t1[0:4, 0:100], in_=ld_h)
            @E("vector", "ve", waits=[("ld", c("ld"))])
            def _(e): return e.tensor_copy(castbuf2[0:4, 0:100], x1t1[0:4, 0:100])
            @E("tensor", "mm", waits=[("ve", c("ve")), ("sc", psTbg["sc"])])
            def _(e): return e.transpose(psTb[0:100, 0:4], castbuf2[0:4, 0:100],
                                         ident_t[0:4, 0:4])
            @E("scalar", "sc", waits=[("mm", c("mm"))])
            def _(e): return e.copy(out=ldT[:], in_=psTb[0:100, 0:4])
            psTbg["sc"] = c("sc")
        lsteps.append(_l_ld)

        def _l_lu():
            @E("sync", "ld", inc_n=16, waits=[("ve", c("ve")), ("ld", c("ld"))])
            def _(e): return e.dma_start(out=x1t1[0:50, 0:4], in_=lu_h)
            @E("vector", "ve", waits=[("ld", c("ld")), ("mm", c("mm"))])
            def _(e): return e.tensor_copy(castbuf2[0:50, 0:4], x1t1[0:50, 0:4])
            @E("tensor", "mm", waits=[("ve", c("ve")), ("sc", psTbg["sc"])])
            def _(e): return e.transpose(psTb[0:4, 0:50], castbuf2[0:50, 0:4],
                                         ident_t[0:50, 0:50])
            @E("scalar", "sc", waits=[("mm", c("mm"))])
            def _(e): return e.copy(out=luT[:], in_=psTb[0:4, 0:50])
            psTbg["sc"] = c("sc")
        lsteps.append(_l_lu)

        def _l_ua(rc):
            @E("sync", "ld", inc_n=16, waits=[("ve", c("ve")), ("sc", c("sc")),
                                              ("ld", c("ld"))])
            def _(e, rc=rc):
                return e.dma_start(out=x1t1[:, 0:50],
                                   in_=ua_h[rc * 128:(rc + 1) * 128, :])
            @E("vector", "ve", waits=[("ld", c("ld")), ("mm", c("mm"))])
            def _(e): return e.tensor_copy(castbuf2[:, 0:50], x1t1[:, 0:50])
            @E("tensor", "mm", waits=[("ve", c("ve")), ("sc", psTbg["sc"])])
            def _(e): return e.transpose(psTb[0:50, 0:128], castbuf2[:, 0:50],
                                         ident_t[:])
            @E("scalar", "sc", waits=[("mm", c("mm"))])
            def _(e, rc=rc):
                return e.copy(out=uaT[:, rc * 128:(rc + 1) * 128],
                              in_=psTb[0:50, 0:128])
            psTbg["sc"] = c("sc")
        for _rc in range(8):
            lsteps.append(lambda rc=_rc: _l_ua(rc))

        def _l_m1():
            @E("sync", "ld", inc_n=16, waits=[("ve", c("ve")), ("ld", c("ld"))])
            def _(e): return e.dma_start(out=x1t1[0:100, :], in_=da_h)
            @E("vector", "ve", waits=[("ld", c("ld")), ("sc", c("sc")),
                                      ("mm", c("mm"))])
            def _(e): return e.tensor_copy(castbuf2[0:100, :], x1t1[0:100, :])
            ve_da = c("ve")
            for j in range(2):
                @E("tensor", "mm",
                   waits=([("ve", ve_da), ("sc", c("sc"))] if j == 0 else ()))
                def _(e, j=j):
                    return e.matmul(psR[0:4, j * 512:(j + 1) * 512], ldT[:],
                                    castbuf2[0:100, j * 512:(j + 1) * 512],
                                    start=True, stop=True)
            @E("scalar", "sc", waits=[("mm", c("mm"))])
            def _(e): return e.copy(out=m1[:], in_=psR[0:4, 0:1024])
        lsteps.append(_l_m1)

        def _l_m2():
            for j in range(2):
                @E("tensor", "mm", waits=[("sc", c("sc"))])
                def _(e, j=j):
                    return e.matmul(psR[0:50, j * 512:(j + 1) * 512], luT[:],
                                    m1[:, j * 512:(j + 1) * 512],
                                    start=True, stop=True)
            @E("scalar", "sc", waits=[("mm", c("mm"))])
            def _(e): return e.copy(out=m2[:], in_=psR[0:50, 0:1024])
            wstate["tp"][id(castbuf2)] = max(wstate["tp"].get(id(castbuf2), 0),
                                             c("mm"))
        lsteps.append(_l_m2)
        stages_pro = (wstage, k_m1)
        cbufs_pro = (castbuf, y2A[:, 0:1024])
        lnq = []
        for s in range(16):
            lnq.append(ln_chunk_p1(x_pre, s))
            if len(lnq) == 2:
                ln_chunk_p2(lnq.pop(0), rdy_pre)
            for _ in range(2):
                if ri < len(rounds):
                    srch, slot, rc = rounds[ri]
                    slot_done[id(slot)] = weight_round(
                        srch, slot, rc, stages_pro[ri % 2], cbufs_pro[ri % 2])
                    ri += 1
            if s >= 3 and lsteps:
                lsteps.pop(0)()
        while ri < len(rounds):
            srch, slot, rc = rounds[ri]
            slot_done[id(slot)] = weight_round(
                srch, slot, rc, stages_pro[ri % 2], cbufs_pro[ri % 2])
            ri += 1
        while lnq:
            ln_chunk_p2(lnq.pop(0), rdy_pre)
        dbg("lnTpre", lnT[:])

        while lsteps:
            lsteps.pop(0)()

        trk = {"sck": [0, 0], "kv": [0, 0], "gpu": [0, 0], "min": [0, 0],
               "red": [0, 0], "last_sck": slot_done.get(id(slotC), 0),
               "last_kv": 0,
               "wkd": slot_done.get(id(slotA), 0),
               "wvd": slot_done.get(id(slotB), 0),
               "wrd": slot_done.get(id(slotC), 0)}
        rdy_own = []
        for m in range(8):
            prefix_iter(0, m, trk, rdy_pre)
        for m in range(8):
            prefix_iter(1, m, trk, rdy_pre)
            for i2 in ([2 * m, 2 * m + 1] if m < 4 else []):
                lnq.append(ln_chunk_p1(x_own, i2, tag=("own" if i2 == 0 else "")))
                if len(lnq) == 2:
                    ln_chunk_p2(lnq.pop(0), rdy_own)
        while lnq:
            ln_chunk_p2(lnq.pop(0), rdy_own)
        carry_combine()
        dbg("small_prefix", small[:])

        trk.update({"state": [0, 0], "yst": [0, 0],
                    "th": [0, 0], "last_th": 0, "y2": 0, "mm_stats": 0,
                    "y2l": [0] * 8})
        for m in range(8):
            mid_iter(0, m, trk, rdy_own)
            for i2 in ([8 + 2 * m, 9 + 2 * m] if m < 4 else []):
                lnq.append(ln_chunk_p1(x_own, i2, use_castbuf=True))
                if len(lnq) == 2:
                    ln_chunk_p2(lnq.pop(0), rdy_own)
        while lnq:
            ln_chunk_p2(lnq.pop(0), rdy_own)
        mid_half_end(0, trk)
        stages_sw = (wstage, xt1)
        cbufs_sw = (castbuf, castbuf2)
        wstate["tp"][id(castbuf)] = max(wstate["tp"].get(id(castbuf), 0),
                                        last_tp[0])
        wstate["tp"][id(castbuf2)] = max(wstate["tp"].get(id(castbuf2), 0),
                                         last_tp[1])
        trk["y2l"] = [0] * 8
        swl = []
        for m in range(8):
            def rounds(m=m):
                weight_round(Wo_h, slotA, m, stages_sw[0], cbufs_sw[0],
                             scale_lnw=True)
                m3v = []
                sw = weight_round(Wf_h, slotB, m, stages_sw[1], cbufs_sw[1],
                                  m3_ps=psV, m3_w=[("ve", trk["last_kv"])],
                                  out_ve=m3v,
                                  dma_w=[("sc", last_apply[1]),
                                         ("ve", last_bn[1])])
                trk["last_kv"] = max(trk["last_kv"], m3v[0])
                swl.append(sw)
            mid_iter(1, m, trk, rdy_own, post_clamp=rounds)
        sc_swap = swl[-1]
        mid_half_end(1, trk)
        ld_he1 = c("ld")
        dbg("small_mid", small[:])
        dbg("lnTown", lnT[:])
        dbg("yA", yA[:])
        last_tp[0] = last_tp[1] = c("mm")
        part3(sc_swap, ld_he1, c("gp"))
        dbg("invcols", invcols[:])

        for s in ("st0", "st1", "st2"):
            fv = c(s)
            @E("sync", waits=[])
            def _(e, s=s, fv=fv): return e.wait_ge(SEM[s], fv)

        _OWN = {"vector": s_ve, "gpsimd": s_gp}

        def replay(key):
            own = _OWN.get(key)

            def body(eng):
                done = 0
                for fn, waits, inc, inc_n in p.streams[key]:
                    if own is not None and done > 0:
                        eng.wait_ge(own, done)
                    for (sem, v) in waits:
                        eng.wait_ge(sem, v)
                    ins = fn(eng)
                    if inc is not None:
                        ins.then_inc(inc, inc_n)
                        if inc is own:
                            done += inc_n
            return body

        block.sync(replay("sync"))
        block.tensor(replay("tensor"))
        block.vector(replay("vector"))
        block.scalar(replay("scalar"))
        block.gpsimd(replay("gpsimd"))

    return nc


_nc_cache = None


def kernel(**inputs):
    global _nc_cache
    x = np.ascontiguousarray(np.asarray(inputs["L0_attn_x"], dtype=np.float32))
    Wr = np.ascontiguousarray(np.asarray(inputs["L0_attn_Wr"], dtype=np.float32))
    Wk = np.ascontiguousarray(np.asarray(inputs["L0_attn_Wk"], dtype=np.float32))
    Wv = np.ascontiguousarray(np.asarray(inputs["L0_attn_Wv"], dtype=np.float32))
    Wo = np.ascontiguousarray(np.asarray(inputs["L0_attn_Wo"], dtype=np.float32))
    Wf = np.ascontiguousarray(np.asarray(inputs["L0_ffn_W"], dtype=np.float32))
    decay = np.asarray(inputs["L0_attn_decay"], dtype=np.float32)
    lnw = np.asarray(inputs["L0_attn_ln_w"], dtype=np.float32)
    bf = np.asarray(inputs["L0_ffn_b"], dtype=np.float32)
    ld = np.ascontiguousarray(np.asarray(inputs["L0_ffn_lora_down"], dtype=np.float32))
    lu = np.ascontiguousarray(np.asarray(inputs["L0_ffn_lora_up"], dtype=np.float32))
    da = np.ascontiguousarray(np.asarray(inputs["L0_ffn_down_aux"], dtype=np.float32))
    ua = np.ascontiguousarray(np.asarray(inputs["L0_ffn_up_aux"], dtype=np.float32))

    decay8 = np.ascontiguousarray(decay.reshape(8, 128).T)
    lnw8 = np.ascontiguousarray(lnw.reshape(8, 128).T)
    b_row = np.ascontiguousarray(bf.reshape(1, D))
    ident = np.eye(128, dtype=ml_dtypes.bfloat16)

    if _nc_cache is None:
        _nc_cache = build_nc()
    nc = _nc_cache

    in_maps = []
    for cid in range(8):
        b, h = cid // 2, cid % 2
        in_maps.append({
            "x_own": np.ascontiguousarray(x[b, h * HALF:(h + 1) * HALF]),
            "x_pre": np.ascontiguousarray(x[b, 0:HALF]),
            "Wr": Wr, "Wk": Wk, "Wv": Wv, "Wo": Wo, "Wf": Wf,
            "decay8": decay8, "lnw8": lnw8, "b_row": b_row,
            "lora_down": ld, "lora_up": lu, "down_aux": da, "up_aux": ua,
            "mask_col": np.full((128, 1), float(h), np.float32),
            "t0_col": np.full((128, 1), float(h * HALF), np.float32),
            "ident": ident,
        })

    res = run_bass_kernel_spmd(nc, in_maps, core_ids=list(range(8)))
    out = np.empty((B, T, D), np.float32)
    for cid in range(8):
        b, h = cid // 2, cid % 2
        out[b, h * HALF:(h + 1) * HALF] = res.results[cid]["out"]
    return out


# version 32
# speedup vs baseline: 1.0338x; 1.0338x over previous
import numpy as np
import ml_dtypes
import concourse.bass as bass
import concourse.mybir as mybir
from concourse.bass_utils import run_bass_kernel_spmd

F32 = mybir.dt.float32
BF16 = mybir.dt.bfloat16
AF = mybir.ActivationFunctionType
ALU = mybir.AluOpType
AX = mybir.AxisListType

B, T, D = 4, 4096, 1024
HALF = 2048
H = 1024
NCH = 8


class Prog:

    def __init__(self):
        self.streams = {k: [] for k in ("sync", "tensor", "vector", "scalar", "gpsimd")}
        self.counts = {}

    def emit(self, eng, fn, waits=(), inc=None, inc_n=1):
        waits = tuple((s, v) for (s, v) in waits if v > 0)
        self.streams[eng].append((fn, waits, inc, inc_n))
        if inc is not None:
            self.counts[inc] = self.counts.get(inc, 0) + inc_n


def build_nc(debug=False):
    nc = bass.Bass("TRN2", target_bir_lowering=False, debug=False)

    x_own = nc.dram_tensor("x_own", [HALF, D], F32, kind="ExternalInput").ap()
    x_pre = nc.dram_tensor("x_pre", [HALF, D], F32, kind="ExternalInput").ap()
    Wr_h = nc.dram_tensor("Wr", [D, D], F32, kind="ExternalInput").ap()
    Wk_h = nc.dram_tensor("Wk", [D, D], F32, kind="ExternalInput").ap()
    Wv_h = nc.dram_tensor("Wv", [D, D], F32, kind="ExternalInput").ap()
    Wo_h = nc.dram_tensor("Wo", [D, D], F32, kind="ExternalInput").ap()
    Wf_h = nc.dram_tensor("Wf", [D, D], F32, kind="ExternalInput").ap()
    decay8_h = nc.dram_tensor("decay8", [128, 8], F32, kind="ExternalInput").ap()
    lnw8_h = nc.dram_tensor("lnw8", [128, 8], F32, kind="ExternalInput").ap()
    b_row_h = nc.dram_tensor("b_row", [1, D], F32, kind="ExternalInput").ap()
    ld_h = nc.dram_tensor("lora_down", [4, 100], F32, kind="ExternalInput").ap()
    lu_h = nc.dram_tensor("lora_up", [50, 4], F32, kind="ExternalInput").ap()
    da_h = nc.dram_tensor("down_aux", [100, D], F32, kind="ExternalInput").ap()
    ua_h = nc.dram_tensor("up_aux", [D, 50], F32, kind="ExternalInput").ap()
    mask_h = nc.dram_tensor("mask_col", [128, 1], F32, kind="ExternalInput").ap()
    t0_h = nc.dram_tensor("t0_col", [128, 1], F32, kind="ExternalInput").ap()
    ident_h = nc.dram_tensor("ident", [128, 128], BF16, kind="ExternalInput").ap()
    out_h = nc.dram_tensor("out", [HALF, D], F32, kind="ExternalOutput").ap()

    import contextlib
    with contextlib.ExitStack() as _stk:
        lnT = _stk.enter_context(nc.sbuf_tensor("lnT", [128, NCH * HALF], BF16))
        yA = _stk.enter_context(nc.sbuf_tensor("yA", [128, NCH * HALF], BF16))
        slotA = _stk.enter_context(nc.sbuf_tensor("slotA", [128, NCH * 1024], BF16))
        slotB = _stk.enter_context(nc.sbuf_tensor("slotB", [128, NCH * 1024], BF16))
        slotC = _stk.enter_context(nc.sbuf_tensor("slotC", [128, NCH * 1024], BF16))
        k_m = _stk.enter_context(nc.sbuf_tensor("k_m", [128, H], F32))
        u_m = _stk.enter_context(nc.sbuf_tensor("u_m", [128, H], F32))
        csbuf = _stk.enter_context(nc.sbuf_tensor("csbuf", [128, H], F32))
        rbuf = _stk.enter_context(nc.sbuf_tensor("rbuf", [128, H], F32))
        k_m1 = _stk.enter_context(nc.sbuf_tensor("k_m1", [128, H], F32))
        u_m1 = _stk.enter_context(nc.sbuf_tensor("u_m1", [128, H], F32))
        csbuf1 = _stk.enter_context(nc.sbuf_tensor("csbuf1", [128, H], F32))
        rbuf1 = _stk.enter_context(nc.sbuf_tensor("rbuf1", [128, H], F32))
        y2A = _stk.enter_context(nc.sbuf_tensor("y2A", [128, NCH * H], BF16))
        castbuf = _stk.enter_context(nc.sbuf_tensor("castbuf", [128, 1024], BF16))
        castbuf2 = _stk.enter_context(nc.sbuf_tensor("castbuf2", [128, 1024], BF16))
        xt0 = _stk.enter_context(nc.sbuf_tensor("xt0", [128, D], F32))
        xt1 = _stk.enter_context(nc.sbuf_tensor("xt1", [128, D], F32))
        x1t0 = _stk.enter_context(nc.sbuf_tensor("x1t0", [128, D], F32))
        x1t1 = _stk.enter_context(nc.sbuf_tensor("x1t1", [128, D], F32))
        x1tc = _stk.enter_context(nc.sbuf_tensor("x1tc", [128, D], F32))
        xtp = x1tc
        wstage = _stk.enter_context(nc.sbuf_tensor("wstage", [128, D], F32))
        iota = _stk.enter_context(nc.sbuf_tensor("iota", [128, H], F32))
        ones_t = _stk.enter_context(nc.sbuf_tensor("ones_t", [128, H], BF16))
        onesc = _stk.enter_context(nc.sbuf_tensor("onesc", [128, 1], BF16))
        b_bcast = _stk.enter_context(nc.sbuf_tensor("b_bcast", [128, D], F32))
        ident_t = _stk.enter_context(nc.sbuf_tensor("ident_t", [128, 128], BF16))
        small = _stk.enter_context(nc.sbuf_tensor("small", [128, 64], F32))
        bias_own = _stk.enter_context(nc.sbuf_tensor("bias_own", [128, 16], F32))
        bias_ownN = _stk.enter_context(nc.sbuf_tensor("bias_ownN", [128, 16], F32))
        bias_pre = _stk.enter_context(nc.sbuf_tensor("bias_pre", [128, 16], F32))
        bn0 = _stk.enter_context(nc.sbuf_tensor("bn0", [128, 12], F32))
        bn0B = _stk.enter_context(nc.sbuf_tensor("bn0B", [128, 12], F32))
        mvA = _stk.enter_context(nc.sbuf_tensor("mvA", [128, 2], F32))
        mvB = _stk.enter_context(nc.sbuf_tensor("mvB", [128, 2], F32))
        colsA = _stk.enter_context(nc.sbuf_tensor("colsA", [128, 4], F32))
        colsB = _stk.enter_context(nc.sbuf_tensor("colsB", [128, 4], F32))
        srow = x1t0
        invcols = _stk.enter_context(nc.sbuf_tensor("invcols", [128, 32], F32))
        ldT = _stk.enter_context(nc.sbuf_tensor("ldT", [100, 4], BF16))
        luT = _stk.enter_context(nc.sbuf_tensor("luT", [4, 50], BF16))
        uaT = _stk.enter_context(nc.sbuf_tensor("uaT", [50, D], BF16))
        m1 = _stk.enter_context(nc.sbuf_tensor("m1", [4, D], BF16))
        m2 = _stk.enter_context(nc.sbuf_tensor("m2", [50, D], BF16))
        psK = _stk.enter_context(nc.psum_tensor("psK", [128, 1024], F32))
        psV = _stk.enter_context(nc.psum_tensor("psV", [128, 1024], F32))
        psR = _stk.enter_context(nc.psum_tensor("psR", [128, 1024], F32))
        psTb = _stk.enter_context(nc.psum_tensor("psTb", [128, 1024], BF16))
        psS = _stk.enter_context(nc.psum_tensor("psS", [1, 512], F32))
        s_ld = _stk.enter_context(nc.semaphore("s_ld"))
        s_st = _stk.enter_context(nc.semaphore("s_st"))
        s_mm = _stk.enter_context(nc.semaphore("s_mm"))
        s_ve = _stk.enter_context(nc.semaphore("s_ve"))
        s_sc = _stk.enter_context(nc.semaphore("s_sc"))
        s_gp = _stk.enter_context(nc.semaphore("s_gp"))
        s_x0 = _stk.enter_context(nc.semaphore("s_x0"))
        s_x1 = _stk.enter_context(nc.semaphore("s_x1"))
        s_x2 = _stk.enter_context(nc.semaphore("s_x2"))
        s_w0 = _stk.enter_context(nc.semaphore("s_w0"))
        s_w1 = _stk.enter_context(nc.semaphore("s_w1"))
        st0 = _stk.enter_context(nc.semaphore("st0"))
        st1 = _stk.enter_context(nc.semaphore("st1"))
        st2 = _stk.enter_context(nc.semaphore("st2"))
        block = _stk.enter_context(nc.Block())
        p = Prog()
        SEM = {"ld": s_ld, "st": s_st, "mm": s_mm, "ve": s_ve, "sc": s_sc,
               "gp": s_gp, "x0": s_x0, "x1": s_x1, "x2": s_x2,
               "w0": s_w0, "w1": s_w1,
               "st0": st0, "st1": st1, "st2": st2}

        def c(name):
            return p.counts.get(SEM[name], 0)

        def E(eng, semname=None, waits=(), inc_n=1):
            def run(fn):
                p.emit(eng, fn,
                       waits=[(SEM[s], v) for (s, v) in waits],
                       inc=(SEM[semname] if semname else None), inc_n=inc_n)
            return run

        def dbg(name, ap):
            if not debug:
                return
            t = nc.dram_tensor("dbg_" + name, list(ap.shape), ap.dtype,
                               kind="ExternalOutput").ap()
            w = [(SEM[s], p.counts.get(SEM[s], 0)) for s in
                 ("mm", "ve", "sc", "gp")]
            p.emit("sync", (lambda e, t=t, ap=ap: e.dma_start(out=t, in_=ap)),
                   waits=w, inc=SEM["ld"], inc_n=16)
            ldv = c("ld")
            for engn in ("sync", "tensor", "vector", "scalar", "gpsimd"):
                p.emit(engn, (lambda e, v=ldv: e.wait_ge(s_ld, v)))

        def wtile(slot, kc, mc):
            return slot[:, kc * 1024 + mc * 128: kc * 1024 + (mc + 1) * 128]

        def wmov(slot, kc, j):
            return slot[:, kc * 1024 + j * 512: kc * 1024 + (j + 1) * 512]

        def lnT_mt(m, t0, n):
            return lnT[:, m * HALF + t0: m * HALF + t0 + n]

        def yA_mt(m, t0, n):
            return yA[:, m * HALF + t0: m * HALF + t0 + n]

        lnT3 = lnT.rearrange("p (m t) -> p m t", m=8)
        psTb3 = psTb.rearrange("p (m t) -> p m t", m=8)

        @E("sync", "ld", inc_n=16)
        def _(e): return e.dma_start(out=small[:, 8:16], in_=decay8_h)
        @E("sync", "ld", inc_n=16)
        def _(e): return e.dma_start(out=small[:, 0:1], in_=mask_h)
        @E("sync", "ld", inc_n=16)
        def _(e): return e.dma_start(out=small[:, 1:2], in_=t0_h)
        @E("sync", "ld", inc_n=16)
        def _(e): return e.dma_start(out=ident_t[:], in_=ident_h)
        @E("sync", "ld", inc_n=16)
        def _(e): return e.dma_start(out=small[:, 56:64], in_=lnw8_h)
        ld_base = c("ld")

        @E("gpsimd", "gp")
        def _(e): return e.iota(iota[:], pattern=[[1, H]], base=0,
                                channel_multiplier=0,
                                allow_small_or_imprecise_dtypes=True)
        @E("gpsimd", "gp")
        def _(e): return e.memset(ones_t[:], 1.0)
        @E("gpsimd", "gp")
        def _(e): return e.memset(onesc[:], 1.0)
        @E("gpsimd", "gp")
        def _(e): return e.memset(small[:, 32:40], 0.0)
        @E("gpsimd", "gp")
        def _(e): return e.memset(small[:, 3:4], 1e-5)
        @E("gpsimd", "gp")
        def _(e): return e.memset(small[:, 4:5], 1e-6)
        @E("gpsimd", "gp")
        def _(e): return e.memset(small[:, 5:6], 4e-6)
        gp_eps = c("gp")

        @E("scalar", "sc", waits=[("ld", ld_base)])
        def _(e): return e.activation(out=small[:, 40:48], in_=small[:, 8:16],
                                      func=AF.Tanh, scale=0.5)
        @E("vector", "ve", waits=[("sc", c("sc"))])
        def _(e):
            return e.tensor_scalar(small[:, 8:16], small[:, 40:48], 0.5, 0.5,
                                   ALU.mult, ALU.add)
        @E("vector", "ve", waits=[("ve", c("ve"))])
        def _(e): return e.tensor_scalar_max(small[:, 40:48], small[:, 8:16], 1e-7)
        ve_dec = c("ve")
        @E("scalar", "sc", waits=[("ve", ve_dec)])
        def _(e): return e.activation(out=small[:, 24:32], in_=small[:, 40:48],
                                      func=AF.Ln)
        sc_lnd = c("sc")
        @E("vector", "ve", waits=[("sc", sc_lnd)])
        def _(e): return e.tensor_scalar_mul(small[:, 16:24], small[:, 24:32], -1.0)
        @E("vector", "ve", waits=[("ve", c("ve"))])
        def _(e): return e.tensor_scalar_add(small[:, 2:3], small[:, 1:2], 1024.0)
        for h in range(2):
            @E("vector", "ve", waits=[("ve", c("ve"))])
            def _(e, h=h):
                return e.tensor_scalar(bias_own[:, h * 8:(h + 1) * 8],
                                       small[:, 16:24], small[:, 1 + h:2 + h],
                                       None, ALU.mult)
            @E("vector", "ve", waits=[("ve", c("ve"))])
            def _(e, h=h):
                return e.tensor_scalar_mul(bias_pre[:, h * 8:(h + 1) * 8],
                                           small[:, 16:24], float(h * 1024))
        @E("vector", "ve", waits=[("ve", c("ve"))])
        def _(e): return e.tensor_scalar_mul(bias_ownN[:], bias_own[:], -1.0)
        ve_bias = c("ve")

        wstate = {"cast": {}, "tp": {}}
        psTbg = {"sc": 0}

        def weight_round(src_h, slot, rc, stage, cbuf,
                         scale_lnw=False, m3_ps=None, m3_w=(), out_ve=None,
                         dma_w=()):
            sid, cid = id(stage), id(cbuf)
            ws = "w0" if stage is wstage else "w1"
            @E("sync", ws, inc_n=16,
               waits=[("ve", wstate["cast"].get(sid, 0))] + list(dma_w))
            def _(e, rc=rc, stage=stage):
                return e.dma_start(out=stage[:],
                                   in_=src_h[rc * 128:(rc + 1) * 128, :])
            ldv = c(ws)
            if m3_ps is not None:
                for j in range(2):
                    @E("tensor", "mm",
                       waits=(([(ws, ldv), ("sc", c("sc"))] + list(m3_w))
                              if j == 0 else ()))
                    def _(e, rc=rc, j=j, m3_ps=m3_ps):
                        return e.matmul(m3_ps[:, j * 512:(j + 1) * 512],
                                        uaT[:, rc * 128:(rc + 1) * 128],
                                        m2[:, j * 512:(j + 1) * 512],
                                        start=True, stop=True)
                @E("vector", "ve", waits=[("mm", c("mm")), (ws, ldv)])
                def _(e, stage=stage, m3_ps=m3_ps):
                    return e.tensor_add(stage[:], stage[:], m3_ps[:])
                if out_ve is not None:
                    out_ve.append(c("ve"))
                cast_w = [("ve", c("ve")), ("mm", wstate["tp"].get(cid, 0))]
            else:
                cast_w = [(ws, ldv), ("mm", wstate["tp"].get(cid, 0))]
            @E("vector", "ve", waits=cast_w)
            def _(e, stage=stage, cbuf=cbuf):
                return e.tensor_copy(cbuf[:], stage[:])
            wstate["cast"][sid] = vecast = c("ve")
            for kc in range(8):
                @E("tensor", "mm",
                   waits=([("ve", vecast), ("sc", psTbg["sc"])] if kc == 0 else ()))
                def _(e, kc=kc, cbuf=cbuf):
                    return e.transpose(psTb[:, kc * 128:(kc + 1) * 128],
                                       cbuf[:, kc * 128:(kc + 1) * 128],
                                       ident_t[:])
            wstate["tp"][cid] = mmt = c("mm")
            if scale_lnw:
                for kc in range(8):
                    @E("scalar", "sc", waits=([("mm", mmt)] if kc == 0 else ()))
                    def _(e, kc=kc, rc=rc, slot=slot):
                        return e.activation(
                            out=slot[:, kc * 1024 + rc * 128:
                                     kc * 1024 + (rc + 1) * 128],
                            in_=psTb[:, kc * 128:(kc + 1) * 128],
                            func=AF.Copy, scale=small[:, 56 + kc:57 + kc])
            else:
                slot3 = slot.rearrange("p (kc e) -> p kc e", kc=8)
                @E("scalar", "sc", waits=[("mm", mmt)])
                def _(e, rc=rc, slot3=slot3):
                    return e.copy(out=slot3[:, :, rc * 128:(rc + 1) * 128],
                                  in_=psTb3[:])
            psTbg["sc"] = c("sc")
            return c("sc")

        last_apply = {0: 0, 1: 0}
        last_bn = {0: 0, 1: 0}
        last_tp = {0: 0, 1: 0}
        last_ln = {0: 0, 1: 0}
        last_applyx = {0: 0, 1: 0, 2: 0}
        last_bnx = {0: 0, 1: 0, 2: 0}

        def ln_chunk_p1(src_h, i, tag="", use_castbuf=False):
            par = i % 2
            slot = i % 3
            xt = (xt0, xt1, xtp)[slot]
            mv = mvA if par == 0 else mvB
            cols = colsA if par == 0 else colsB
            if use_castbuf:
                lnc = castbuf[:] if par == 0 else castbuf2[:]
            else:
                lnc = yA[:, 1024:2048] if par == 0 else yA[:, 2048:3072]
            xs = ("x0", "x1", "x2")[slot]
            @E("sync", xs, inc_n=16,
               waits=[("sc", last_applyx[slot]), ("ve", last_bnx[slot])])
            def _(e, i=i, xt=xt):
                return e.dma_start(out=xt[:], in_=src_h[i * 128:(i + 1) * 128, :])
            ldx = c(xs)
            bnp = bn0 if par == 0 else bn0B
            @E("vector", "ve", waits=[(xs, ldx), ("sc", last_ln[par])])
            def _(e, xt=xt, bnp=bnp): return e.bn_stats(bnp[:, 0:6], xt[:, 0:512])
            @E("vector", "ve")
            def _(e, xt=xt, bnp=bnp): return e.bn_stats(bnp[:, 6:12], xt[:, 512:1024])
            @E("vector", "ve", waits=[("ve", c("ve"))])
            def _(e, mv=mv, bnp=bnp): return e.bn_aggr(mv[:], bnp[:])
            last_bn[par] = last_bnx[slot] = ve_mv = c("ve")
            if i == 0 and tag:
                dbg(tag + "_x0", xt[:])
                dbg(tag + "_mv0", mv[:])
            ctx = dict(i=i, par=par, lnc=lnc)

            def finish(par=par, slot=slot, mv=mv, cols=cols, xt=xt, lnc=lnc,
                       ve_mv=ve_mv, ctx=ctx):
                @E("scalar", "sc", waits=[("ve", ve_mv), ("gp", gp_eps)])
                def _(e, mv=mv, cols=cols):
                    return e.activation(out=cols[:, 0:1], in_=mv[:, 1:2],
                                        func=AF.Ln, bias=small[:, 3:4])
                last_ln[par] = c("sc")
                @E("scalar", "sc", waits=[("sc", c("sc"))])
                def _(e, cols=cols):
                    return e.activation(out=cols[:, 1:2], in_=cols[:, 0:1],
                                        func=AF.Exp, scale=-0.5)
                sc_inv = c("sc")
                @E("vector", "ve", waits=[("sc", sc_inv)])
                def _(e, mv=mv, cols=cols):
                    return e.scalar_tensor_tensor(cols[:, 2:3], mv[:, 0:1], -1.0,
                                                  cols[:, 1:2], ALU.mult, ALU.mult)
                ve_negmi = c("ve")
                @E("scalar", "sc", waits=[("ve", ve_negmi), ("mm", last_tp[par])])
                def _(e, xt=xt, cols=cols, lnc=lnc):
                    return e.activation(out=lnc, in_=xt[:], func=AF.Identity,
                                        bias=cols[:, 2:3], scale=cols[:, 1:2])
                last_apply[par] = last_applyx[slot] = ctx["sc_apply"] = c("sc")
            ctx["finish"] = finish
            return ctx

        def ln_chunk_p2(ctx, ready):
            i, par, lnc = ctx["i"], ctx["par"], ctx["lnc"]
            for j in range(8):
                @E("tensor", "mm",
                   waits=([("sc", max(ctx["sc_apply"], psTbg["sc"]))]
                          if j == 0 else ()))
                def _(e, j=j, lnc=lnc):
                    return e.transpose(psTb[:, j * 128:(j + 1) * 128],
                                       lnc[:, j * 128:(j + 1) * 128], ident_t[:])
            last_tp[par] = mmt = c("mm")
            @E("scalar", "sc", waits=[("mm", mmt)])
            def _(e, i=i):
                return e.copy(out=lnT3[:, :, i * 128:(i + 1) * 128], in_=psTb3[:])
            psTbg["sc"] = c("sc")
            ready.append(c("sc"))

        def proj(slot, m, h, ps, w0=(), w1=()):
            cnt = 0
            for kc in range(8):
                for j in range(2):
                    w = list(w0) if cnt == 0 else (list(w1) if cnt == 1 else [])
                    cnt += 1
                    @E("tensor", "mm", waits=w)
                    def _(e, kc=kc, j=j, m=m, h=h, ps=ps, slot=slot):
                        return e.matmul(ps[:, j * 512:(j + 1) * 512],
                                        wtile(slot, kc, m),
                                        lnT_mt(kc, h * H + j * 512, 512),
                                        start=(kc == 0), stop=(kc == 7))
            return c("mm")

        BUFS = [(k_m, u_m, csbuf, rbuf), (k_m1, u_m1, csbuf1, rbuf1)]

        def prefix_iter(h, m, trk, lnready):
            par = (h * 8 + m) % 2
            kB, uB, cB, rB = BUFS[par]
            lnw = [("sc", max(trk["last_sck"], lnready[h * 8 + 3], trk["wkd"]))]
            lnw1 = [("sc", max(lnready[h * 8 + 7], trk["wvd"]))]
            mmk = proj(slotA, m, h, psK, w0=lnw, w1=lnw1)
            mmv = proj(slotB, m, h, psV, w0=[("ve", trk["last_kv"])])
            @E("scalar", "sc", waits=[("mm", mmk), ("gp", trk["gpu"][par]),
                                      ("ve", trk["red"][par])])
            def _(e, kB=kB): return e.copy(out=kB[:], in_=psK[:])
            trk["last_sck"] = sck = c("sc")
            @E("vector", "ve", waits=[("sc", sck), ("mm", mmv),
                                      ("gp", trk["gpu"][par])])
            def _(e, kB=kB, uB=uB): return e.tensor_mul(uB[:], kB[:], psV[:])
            trk["last_kv"] = c("ve")
            @E("scalar", "sc", waits=[("ve", ve_bias),
                                      ("ve", trk["min"][par])])
            def _(e, m=m, h=h, cB=cB):
                return e.activation(out=cB[:], in_=iota[:],
                                    func=AF.Identity,
                                    scale=small[:, 16 + m:17 + m],
                                    bias=bias_pre[:, h * 8 + m:h * 8 + m + 1])
            sc_c = c("sc")
            @E("vector", "ve", waits=[("sc", sc_c), ("gp", trk["gpu"][par])])
            def _(e, cB=cB, rB=rB):
                return e.tensor_scalar_min(rB[:], cB[:], 23.025850929940457)
            trk["min"][par] = ve_c = c("ve")
            @E("scalar", "sc", waits=[("ve", ve_c)])
            def _(e, cB=cB, rB=rB):
                return e.activation(out=cB[:], in_=rB[:], func=AF.Exp)
            sc_ce = c("sc")
            @E("gpsimd", "gp", waits=[("sc", sc_ce)])
            def _(e, kB=kB, uB=uB, cB=cB):
                return e.tensor_mul(kB[:], uB[:], cB[:])
            trk["gpu"][par] = gp_u = c("gp")
            @E("vector", "ve", waits=[("gp", gp_u)])
            def _(e, kB=kB, uB=uB):
                return e.tensor_reduce(uB[:, 0:1], kB[:], AX.X, ALU.add)
            dst = 32 + m if h == 0 else 8 + m
            @E("vector", "ve", waits=[("ve", c("ve"))])
            def _(e, dst=dst, uB=uB):
                return e.tensor_copy(small[:, dst:dst + 1], uB[:, 0:1])
            trk["red"][par] = c("ve")

        def carry_combine():
            @E("vector", "ve", waits=[("gp", c("gp")), ("ve", c("ve"))])
            def _(e):
                return e.tensor_add(small[:, 48:56], small[:, 32:40], small[:, 8:16])
            @E("vector", "ve", waits=[("ve", c("ve"))])
            def _(e):
                return e.tensor_scalar(small[:, 40:48], small[:, 48:56],
                                       small[:, 0:1], None, ALU.mult)

        last_srow_dma = {0: 0}
        sc_inv_h = {0: 0, 1: 0}

        def mid_iter(h, m, trk, lnready, post_clamp=None):
            par = (h * 8 + m) % 2
            kB, uB, cB, rB = BUFS[par]
            lnw = [("sc", max(trk["last_sck"], lnready[h * 8 + 3]))]
            lnw1 = [("sc", lnready[h * 8 + 7])]
            mmk = proj(slotA, m, h, psK, w0=lnw, w1=lnw1)
            mmv = proj(slotB, m, h, psV, w0=[("ve", trk["last_kv"])])
            mmr = proj(slotC, m, h, psR,
                       w0=[("sc", max(trk["last_th"], trk["wrd"]))])
            @E("scalar", "sc", waits=[("mm", mmk), ("ve", trk["yst"][par]),
                                      ("gp", trk["gpu"][par])])
            def _(e, kB=kB): return e.copy(out=kB[:], in_=psK[:])
            trk["last_sck"] = sck = c("sc")
            @E("vector", "ve", waits=[("sc", sck), ("mm", mmv),
                                      ("gp", trk["gpu"][par])])
            def _(e, kB=kB, uB=uB): return e.tensor_mul(uB[:], kB[:], psV[:])
            trk["last_kv"] = c("ve")
            @E("scalar", "sc", waits=[("ve", trk["state"][par])])
            def _(e, m=m, h=h, rB=rB):
                return e.activation(out=rB[:], in_=iota[:], func=AF.Exp,
                                    scale=small[:, 24 + m:25 + m],
                                    bias=bias_ownN[:, h * 8 + m:h * 8 + m + 1])
            @E("scalar", "sc", waits=[("ve", trk["yst"][par]),
                                      ("gp", trk["gpu"][par])])
            def _(e, m=m, h=h, cB=cB):
                return e.activation(out=cB[:], in_=iota[:], func=AF.Identity,
                                    scale=small[:, 16 + m:17 + m],
                                    bias=bias_own[:, h * 8 + m:h * 8 + m + 1])
            sc_c2 = c("sc")
            @E("vector", "ve", waits=[("sc", sc_c2), ("ve", c("ve"))])
            def _(e, kB=kB, cB=cB):
                return e.tensor_scalar_min(kB[:], cB[:], 23.025850929940457)
            ve_c = c("ve")
            @E("scalar", "sc", waits=[("ve", ve_c)])
            def _(e, kB=kB, cB=cB):
                return e.activation(out=cB[:], in_=kB[:], func=AF.Exp)
            sc_ce = c("sc")
            if post_clamp is not None:
                post_clamp()
            @E("gpsimd", "gp", waits=[("sc", sc_ce)])
            def _(e, kB=kB, uB=uB, cB=cB):
                return e.tensor_mul(kB[:], uB[:], cB[:])
            trk["gpu"][par] = gp_u = c("gp")
            init = (small[:, 40 + m:41 + m] if h == 0
                    else small[:, 48 + m:49 + m])
            @E("vector", "ve", waits=[("gp", gp_u), ("ve", c("ve"))])
            def _(e, init=init, uB=uB, kB=kB):
                return e.tensor_tensor_scan(out=uB[:], data0=ones_t[:],
                                            data1=kB[:], initial=init,
                                            op0=ALU.mult, op1=ALU.add)
            ve_scan = c("ve")
            if h == 0:
                @E("vector", "ve", waits=[("ve", c("ve"))])
                def _(e, m=m, uB=uB):
                    return e.tensor_copy(small[:, 48 + m:49 + m], uB[:, H - 1:H])
            @E("vector", "ve", waits=[("gp", c("gp"))])
            def _(e, kB=kB, uB=uB, rB=rB):
                return e.tensor_mul(kB[:], uB[:], rB[:])
            trk["state"][par] = c("ve")
            @E("scalar", "sc", waits=[("mm", mmr), ("ve", ve_scan)])
            def _(e, cB=cB): return e.activation(out=cB[:], in_=psR[:],
                                                 func=AF.Tanh, scale=0.5)
            trk["last_th"] = sc_r = c("sc")
            @E("vector", "ve", waits=[("sc", sc_r)])
            def _(e, m=m, h=h, kB=kB, cB=cB):
                return e.scalar_tensor_tensor(yA_mt(m, h * H, H), cB[:],
                                              1.0, kB[:], ALU.add, ALU.mult)
            trk["yst"][par] = c("ve")
            y2w = [("mm", trk["mm_stats"])] if h == 1 else []
            @E("vector", "ve", waits=y2w)
            def _(e, m=m, h=h):
                return e.tensor_mul(y2A[:, m * H:(m + 1) * H],
                                    yA_mt(m, h * H, H), yA_mt(m, h * H, H))
            trk["y2"] = c("ve")
            trk["y2l"][m] = c("ve")
            if m == 7:
                trk["y2_h" + str(h)] = c("ve")

        def mid_half_end(h, trk):
            ve_y2 = trk["y2"]
            sc_evs = []
            for j in range(2):
                for mc in range(8):
                    w = [("ve", trk["y2l"][mc])]
                    if mc == 0 and sc_evs:
                        w.append(("sc", sc_evs[-1]))
                    @E("tensor", "mm", waits=w)
                    def _(e, j=j, mc=mc):
                        return e.matmul(psS[:], onesc[:],
                                        y2A[:, mc * H + j * 512:
                                            mc * H + (j + 1) * 512],
                                        start=(mc == 0), stop=(mc == 7))
                mm_st = c("mm")
                @E("scalar", "sc", waits=[("mm", mm_st),
                                          ("ld", last_srow_dma[0])])
                def _(e, j=j):
                    return e.copy(out=srow[0:1, j * 512:(j + 1) * 512], in_=psS[:])
                sc_evs.append(c("sc"))
            trk["mm_stats"] = c("mm")
            sc_sr = sc_evs[-1]
            for i2 in range(8):
                @E("sync", "ld", inc_n=16,
                   waits=([("sc", sc_sr)] if i2 == 0 else ()))
                def _(e, i2=i2, h=h):
                    return e.dma_start(out=invcols[:, 16 * h + i2:16 * h + i2 + 1],
                                       in_=srow[0:1, i2 * 128:(i2 + 1) * 128])
            last_srow_dma[0] = ld_s = c("ld")
            @E("scalar", "sc", waits=[("ld", ld_s)])
            def _(e, h=h):
                return e.activation(out=invcols[:, 16 * h + 8:16 * h + 16],
                                    in_=invcols[:, 16 * h:16 * h + 8],
                                    func=AF.Ln, scale=1.0 / 1024.0,
                                    bias=small[:, 5:6])
            @E("scalar", "sc", waits=[("sc", c("sc"))])
            def _(e, h=h):
                return e.activation(out=invcols[:, 16 * h:16 * h + 8],
                                    in_=invcols[:, 16 * h + 8:16 * h + 16],
                                    func=AF.Exp, scale=-0.5)
            sc_inv_h[h] = c("sc")

        def part3(sc_swap, ld_he1, gp_all):
            @E("sync", "ld", inc_n=16, waits=[("ld", c("ld"))])
            def _(e): return e.dma_start(out=b_bcast[:],
                                         in_=b_row_h.broadcast_to((128, D)))
            ld_b = c("ld")

            last_x1 = {0: 0, 1: 0}
            last_dma = {0: 0, 1: 0}
            last_stt = {0: 0, 1: 0}
            st_l1 = {"l1": 0, "oa": 0}
            oring = [k_m, u_m, csbuf]
            oring_st = [0, 0, 0]
            oabuf = [rbuf, k_m1]
            last_sttx = {0: 0, 1: 0}
            ld_pre = {}
            pendB, pendC = [], []

            def stageB(ctx, mid_emit=None):
                ig, h, i = ctx["ig"], ctx["h"], ctx["i"]
                cb = ctx["cb"]
                for j in range(8):
                    @E("tensor", "mm",
                       waits=([("sc", max(ctx["sc_ap"], psTbg["sc"]))]
                              if j == 0 else ()))
                    def _(e, j=j, cb=cb):
                        return e.transpose(psTb[:, j * 128:(j + 1) * 128],
                                           cb[:, j * 128:(j + 1) * 128],
                                           ident_t[:])
                last_tp[ig % 2] = mmt = c("mm")
                @E("scalar", "sc", waits=[("mm", mmt)])
                def _(e, ig=ig):
                    return e.copy(out=lnT3[:, :, ig * 128:(ig + 1) * 128],
                                  in_=psTb3[:])
                psTbg["sc"] = st_l1["l1"] = sc_l1 = c("sc")
                if mid_emit is not None:
                    mid_emit()
                first = True
                for m in range(8):
                    for j in range(2):
                        w = [("sc", sc_l1), ("ve", st_l1["oa"])] if first else []
                        first = False
                        @E("tensor", "mm", waits=w)
                        def _(e, m=m, j=j, h=h, i=i):
                            return e.matmul(psV[:, j * 512:(j + 1) * 512],
                                            lnT_mt(m, h * H + i * 128, 128),
                                            wmov(slotB, m, j),
                                            start=(m == 0), stop=(m == 7))
                ctx["mm_ffn"] = c("mm")
                pendC.append(ctx)

            def stageC(ctx):
                ig, par = ctx["ig"], ctx["par"]
                x1t, oa, cols = ctx["x1t"], ctx["oa"], ctx["cols"]
                ot = oring[ig % 3]
                @E("vector", "ve", waits=[("mm", ctx["mm_ffn"]),
                                          ("sc", ctx["sc_inv"]),
                                          ("gp", gp_all)])
                def _(e, x1t=x1t, oa=oa, cols=cols):
                    return e.scalar_tensor_tensor(oa[:], psV[:], cols[:, 1:2],
                                                  x1t[:], ALU.mult, ALU.add)
                st_l1["oa"] = c("ve")
                @E("vector", "ve", waits=[("st" + str(ig % 3), oring_st[ig % 3]),
                                          ("ld", ld_b), ("gp", gp_all)])
                def _(e, ot=ot, oa=oa):
                    return e.tensor_add(ot[:], oa[:], b_bcast[:])
                last_x1[par] = ve_out = c("ve")
                sts = "st" + str(ig % 3)
                @E("sync", sts, inc_n=16, waits=[("ve", ve_out)])
                def _(e, ig=ig, ot=ot):
                    return e.dma_start(out=out_h[ig * 128:(ig + 1) * 128, :],
                                       in_=ot[:])
                oring_st[ig % 3] = c(sts)

            wo_mm = {}

            def emit_wo(ig):
                if ig > 15:
                    return
                h, i = ig // 8, ig % 8
                psWo = psK if ig % 2 == 0 else psR
                wpar = ig % 2
                first = True
                for m in range(8):
                    for j in range(2):
                        w = ([("ve", last_stt[wpar]), ("sc", sc_inv_h[h]),
                              ("sc", sc_swap)] if first else [])
                        first = False
                        @E("tensor", "mm", waits=w)
                        def _(e, m=m, j=j, h=h, i=i, psWo=psWo):
                            return e.matmul(psWo[:, j * 512:(j + 1) * 512],
                                            yA_mt(m, h * H + i * 128, 128),
                                            wmov(slotA, m, j),
                                            start=(m == 0), stop=(m == 7))
                wo_mm[ig] = c("mm")

            x1ring = [x1t0, x1t1, x1tc]
            last_apply_r = {0: 0, 1: 0, 2: last_applyx[2]}
            for h in range(2):
                for i in range(8):
                    ig = h * 8 + i
                    par = i % 2
                    xt = xt0 if par == 0 else xt1
                    x1t = x1ring[ig % 3]
                    mv = mvA if par == 0 else mvB
                    cols = colsA if par == 0 else colsB
                    cb = castbuf if ig % 2 == 0 else castbuf2
                    psWo = psK if ig % 2 == 0 else psR
                    wpar = ig % 2
                    if ig == 0:
                        emit_wo(0)
                        emit_wo(1)
                    else:
                        stageB(pendB.pop(0), mid_emit=lambda ig=ig: emit_wo(ig + 1))
                    mm_wo = wo_mm[ig]
                    for igp in ([ig, ig + 1] if ig == 0 else [ig + 1]):
                        if igp > 15:
                            continue
                        parp = igp % 8 % 2
                        xtp = xt0 if parp == 0 else xt1
                        xsp = "x0" if parp == 0 else "x1"
                        @E("sync", xsp, inc_n=16,
                           waits=[("ve", last_sttx[parp]),
                                  ("sc", sc_swap)])
                        def _(e, igp=igp, xtp=xtp):
                            return e.dma_start(
                                out=xtp[:],
                                in_=x_own[igp * 128:(igp + 1) * 128, :])
                        ld_pre[igp] = (xsp, c(xsp))
                    ld_x = ld_pre[ig]
                    @E("vector", "ve",
                       waits=[("mm", mm_wo), (ld_x[0], ld_x[1]),
                              ("ld", ld_he1), ("sc", last_apply_r[ig % 3])])
                    def _(e, h=h, i=i, xt=xt, x1t=x1t, psWo=psWo):
                        return e.scalar_tensor_tensor(
                            x1t[:], psWo[:], invcols[:, 16 * h + i:16 * h + i + 1],
                            xt[:], ALU.mult, ALU.add)
                    last_stt[wpar] = last_sttx[par] = c("ve")
                    @E("vector", "ve", waits=[("sc", last_ln[par])])
                    def _(e, x1t=x1t): return e.bn_stats(bn0[:, 0:6], x1t[:, 0:512])
                    @E("vector", "ve")
                    def _(e, x1t=x1t): return e.bn_stats(bn0[:, 6:12], x1t[:, 512:1024])
                    @E("vector", "ve", waits=[("ve", c("ve"))])
                    def _(e, mv=mv): return e.bn_aggr(mv[:], bn0[:])
                    last_bn[par] = ve_mv = c("ve")
                    @E("vector", "ve", waits=[("ve", ve_mv)])
                    def _(e, mv=mv, cols=cols):
                        return e.tensor_scalar_mul(cols[:, 2:3], mv[:, 0:1], -1.0)
                    ve_negm = c("ve")
                    @E("scalar", "sc", waits=[("ve", ve_negm),
                                              ("mm", last_tp[ig % 2])])
                    def _(e, x1t=x1t, cols=cols, cb=cb):
                        return e.activation(out=cb[:], in_=x1t[:],
                                            func=AF.Identity,
                                            bias=cols[:, 2:3], scale=1.0)
                    last_apply_r[ig % 3] = sc_ap = c("sc")
                    @E("scalar", "sc", waits=[("ve", ve_mv)])
                    def _(e, mv=mv, cols=cols):
                        return e.activation(out=cols[:, 0:1], in_=mv[:, 1:2],
                                            func=AF.Ln, bias=small[:, 3:4])
                    last_ln[par] = c("sc")
                    @E("scalar", "sc", waits=[("sc", c("sc"))])
                    def _(e, cols=cols):
                        return e.activation(out=cols[:, 1:2], in_=cols[:, 0:1],
                                            func=AF.Exp, scale=-0.5)
                    sc_inv_i = c("sc")
                    pendB.append(dict(ig=ig, h=h, i=i, par=par, x1t=x1t,
                                      oa=oabuf[ig % 2], cb=cb, sc_ap=sc_ap,
                                      sc_inv=sc_inv_i, cols=cols))
                    if pendC:
                        stageC(pendC.pop(0))
            while pendB:
                stageB(pendB.pop(0))
                if pendC:
                    stageC(pendC.pop(0))
            while pendC:
                stageC(pendC.pop(0))

        dbg("small_pro", small[:])
        rdy_pre = []
        rounds = ([(Wk_h, slotA, rc) for rc in range(8)]
                  + [(Wv_h, slotB, rc) for rc in range(8)]
                  + [(Wr_h, slotC, rc) for rc in range(8)])
        slot_done = {}
        ri = 0
        stages_pro = (wstage, k_m1)
        cbufs_pro = (castbuf, y2A[:, 0:1024])
        lnq = []
        pend1 = []

        def push_chunk(ctx, rdy):
            pend1.append(ctx)
            if len(pend1) == 2:
                pc = pend1.pop(0)
                pc["finish"]()
                lnq.append(pc)
            if len(lnq) == 2:
                ln_chunk_p2(lnq.pop(0), rdy)

        def flush_chunks(rdy):
            while pend1:
                pc = pend1.pop(0)
                pc["finish"]()
                lnq.append(pc)
                if len(lnq) == 2:
                    ln_chunk_p2(lnq.pop(0), rdy)
            while lnq:
                ln_chunk_p2(lnq.pop(0), rdy)

        for s in range(16):
            push_chunk(ln_chunk_p1(x_pre, s), rdy_pre)
            for _ in range(2):
                if ri < len(rounds):
                    srch, slot, rc = rounds[ri]
                    slot_done[id(slot)] = weight_round(
                        srch, slot, rc, stages_pro[ri % 2], cbufs_pro[ri % 2])
                    ri += 1
        while ri < len(rounds):
            srch, slot, rc = rounds[ri]
            slot_done[id(slot)] = weight_round(
                srch, slot, rc, stages_pro[ri % 2], cbufs_pro[ri % 2])
            ri += 1
        flush_chunks(rdy_pre)
        dbg("lnTpre", lnT[:])

        @E("sync", "ld", inc_n=16,
           waits=[("ve", wstate["cast"].get(id(wstage), 0)),
                  ("ld", c("ld")), ("w0", c("w0")), ("w1", c("w1"))])
        def _(e): return e.dma_start(out=wstage[0:4, 0:100], in_=ld_h)
        @E("vector", "ve", waits=[("ld", c("ld")),
                                  ("mm", wstate["tp"].get(id(castbuf), 0))])
        def _(e): return e.tensor_copy(castbuf[0:4, 0:100], wstage[0:4, 0:100])
        @E("tensor", "mm", waits=[("ve", c("ve")), ("sc", c("sc"))])
        def _(e): return e.transpose(psTb[0:100, 0:4], castbuf[0:4, 0:100],
                                     ident_t[0:4, 0:4])
        @E("scalar", "sc", waits=[("mm", c("mm"))])
        def _(e): return e.copy(out=ldT[:], in_=psTb[0:100, 0:4])

        @E("sync", "ld", inc_n=16, waits=[("ve", c("ve")), ("ld", c("ld"))])
        def _(e): return e.dma_start(out=wstage[0:50, 0:4], in_=lu_h)
        @E("vector", "ve", waits=[("ld", c("ld")), ("sc", c("sc"))])
        def _(e): return e.tensor_copy(castbuf[0:50, 0:4], wstage[0:50, 0:4])
        @E("tensor", "mm", waits=[("ve", c("ve")), ("sc", c("sc"))])
        def _(e): return e.transpose(psTb[0:4, 0:50], castbuf[0:50, 0:4],
                                     ident_t[0:50, 0:50])
        @E("scalar", "sc", waits=[("mm", c("mm"))])
        def _(e): return e.copy(out=luT[:], in_=psTb[0:4, 0:50])

        for rc in range(8):
            @E("sync", "ld", inc_n=16, waits=[("ve", c("ve")), ("sc", c("sc")),
                                              ("ld", c("ld"))])
            def _(e, rc=rc):
                return e.dma_start(out=wstage[:, 0:50],
                                   in_=ua_h[rc * 128:(rc + 1) * 128, :])
            @E("vector", "ve", waits=[("ld", c("ld")), ("mm", c("mm"))])
            def _(e): return e.tensor_copy(castbuf[:, 0:50], wstage[:, 0:50])
            @E("tensor", "mm", waits=[("ve", c("ve")), ("sc", c("sc"))])
            def _(e): return e.transpose(psTb[0:50, 0:128], castbuf[:, 0:50],
                                         ident_t[:])
            @E("scalar", "sc", waits=[("mm", c("mm"))])
            def _(e, rc=rc):
                return e.copy(out=uaT[:, rc * 128:(rc + 1) * 128],
                              in_=psTb[0:50, 0:128])

        @E("sync", "ld", inc_n=16, waits=[("ve", c("ve")), ("ld", c("ld"))])
        def _(e): return e.dma_start(out=wstage[0:100, :], in_=da_h)
        @E("vector", "ve", waits=[("ld", c("ld")), ("sc", c("sc")), ("mm", c("mm"))])
        def _(e): return e.tensor_copy(castbuf[0:100, :], wstage[0:100, :])
        ve_da = c("ve")
        for j in range(2):
            @E("tensor", "mm", waits=([("ve", ve_da), ("sc", c("sc"))] if j == 0 else ()))
            def _(e, j=j):
                return e.matmul(psR[0:4, j * 512:(j + 1) * 512], ldT[:],
                                castbuf[0:100, j * 512:(j + 1) * 512],
                                start=True, stop=True)
        @E("scalar", "sc", waits=[("mm", c("mm"))])
        def _(e): return e.copy(out=m1[:], in_=psR[0:4, 0:1024])
        for j in range(2):
            @E("tensor", "mm", waits=[("sc", c("sc"))])
            def _(e, j=j):
                return e.matmul(psR[0:50, j * 512:(j + 1) * 512], luT[:],
                                m1[:, j * 512:(j + 1) * 512], start=True, stop=True)
        @E("scalar", "sc", waits=[("mm", c("mm"))])
        def _(e): return e.copy(out=m2[:], in_=psR[0:50, 0:1024])
        wstate["cast"][id(wstage)] = max(wstate["cast"].get(id(wstage), 0), c("ve"))
        wstate["tp"][id(castbuf)] = max(wstate["tp"].get(id(castbuf), 0), c("mm"))

        trk = {"sck": [0, 0], "kv": [0, 0], "gpu": [0, 0], "min": [0, 0],
               "red": [0, 0], "last_sck": slot_done.get(id(slotC), 0),
               "last_kv": 0,
               "wkd": slot_done.get(id(slotA), 0),
               "wvd": slot_done.get(id(slotB), 0),
               "wrd": slot_done.get(id(slotC), 0)}
        rdy_own = []
        for m in range(8):
            prefix_iter(0, m, trk, rdy_pre)
        for m in range(8):
            prefix_iter(1, m, trk, rdy_pre)
            for i2 in ([2 * m, 2 * m + 1] if m < 4 else []):
                push_chunk(ln_chunk_p1(x_own, i2,
                                       tag=("own" if i2 == 0 else "")), rdy_own)
        flush_chunks(rdy_own)
        carry_combine()
        dbg("small_prefix", small[:])

        trk.update({"state": [0, 0], "yst": [0, 0],
                    "th": [0, 0], "last_th": 0, "y2": 0, "mm_stats": 0,
                    "y2l": [0] * 8})
        for m in range(8):
            mid_iter(0, m, trk, rdy_own)
            for i2 in ([8 + 2 * m, 9 + 2 * m] if m < 4 else []):
                push_chunk(ln_chunk_p1(x_own, i2, use_castbuf=True), rdy_own)
        flush_chunks(rdy_own)
        mid_half_end(0, trk)
        stages_sw = (wstage, xt1)
        cbufs_sw = (castbuf, castbuf2)
        wstate["tp"][id(castbuf)] = max(wstate["tp"].get(id(castbuf), 0),
                                        last_tp[0])
        wstate["tp"][id(castbuf2)] = max(wstate["tp"].get(id(castbuf2), 0),
                                         last_tp[1])
        trk["y2l"] = [0] * 8
        swl = []
        for m in range(8):
            def rounds(m=m):
                weight_round(Wo_h, slotA, m, stages_sw[0], cbufs_sw[0],
                             scale_lnw=True)
                m3v = []
                sw = weight_round(Wf_h, slotB, m, stages_sw[1], cbufs_sw[1],
                                  m3_ps=psV, m3_w=[("ve", trk["last_kv"])],
                                  out_ve=m3v,
                                  dma_w=[("sc", last_apply[1]),
                                         ("ve", last_bn[1])])
                trk["last_kv"] = max(trk["last_kv"], m3v[0])
                swl.append(sw)
            mid_iter(1, m, trk, rdy_own, post_clamp=rounds)
        sc_swap = swl[-1]
        mid_half_end(1, trk)
        ld_he1 = c("ld")
        dbg("small_mid", small[:])
        dbg("lnTown", lnT[:])
        dbg("yA", yA[:])
        last_tp[0] = last_tp[1] = c("mm")
        part3(sc_swap, ld_he1, c("gp"))
        dbg("invcols", invcols[:])

        for s in ("st0", "st1", "st2"):
            fv = c(s)
            @E("sync", waits=[])
            def _(e, s=s, fv=fv): return e.wait_ge(SEM[s], fv)

        _OWN = {"vector": s_ve, "gpsimd": s_gp}

        def replay(key):
            own = _OWN.get(key)

            def body(eng):
                done = 0
                for fn, waits, inc, inc_n in p.streams[key]:
                    if own is not None and done > 0:
                        eng.wait_ge(own, done)
                    for (sem, v) in waits:
                        eng.wait_ge(sem, v)
                    ins = fn(eng)
                    if inc is not None:
                        ins.then_inc(inc, inc_n)
                        if inc is own:
                            done += inc_n
            return body

        block.sync(replay("sync"))
        block.tensor(replay("tensor"))
        block.vector(replay("vector"))
        block.scalar(replay("scalar"))
        block.gpsimd(replay("gpsimd"))

    return nc


_nc_cache = None


def kernel(**inputs):
    global _nc_cache
    x = np.ascontiguousarray(np.asarray(inputs["L0_attn_x"], dtype=np.float32))
    Wr = np.ascontiguousarray(np.asarray(inputs["L0_attn_Wr"], dtype=np.float32))
    Wk = np.ascontiguousarray(np.asarray(inputs["L0_attn_Wk"], dtype=np.float32))
    Wv = np.ascontiguousarray(np.asarray(inputs["L0_attn_Wv"], dtype=np.float32))
    Wo = np.ascontiguousarray(np.asarray(inputs["L0_attn_Wo"], dtype=np.float32))
    Wf = np.ascontiguousarray(np.asarray(inputs["L0_ffn_W"], dtype=np.float32))
    decay = np.asarray(inputs["L0_attn_decay"], dtype=np.float32)
    lnw = np.asarray(inputs["L0_attn_ln_w"], dtype=np.float32)
    bf = np.asarray(inputs["L0_ffn_b"], dtype=np.float32)
    ld = np.ascontiguousarray(np.asarray(inputs["L0_ffn_lora_down"], dtype=np.float32))
    lu = np.ascontiguousarray(np.asarray(inputs["L0_ffn_lora_up"], dtype=np.float32))
    da = np.ascontiguousarray(np.asarray(inputs["L0_ffn_down_aux"], dtype=np.float32))
    ua = np.ascontiguousarray(np.asarray(inputs["L0_ffn_up_aux"], dtype=np.float32))

    decay8 = np.ascontiguousarray(decay.reshape(8, 128).T)
    lnw8 = np.ascontiguousarray(lnw.reshape(8, 128).T)
    b_row = np.ascontiguousarray(bf.reshape(1, D))
    ident = np.eye(128, dtype=ml_dtypes.bfloat16)

    if _nc_cache is None:
        _nc_cache = build_nc()
    nc = _nc_cache

    in_maps = []
    for cid in range(8):
        b, h = cid // 2, cid % 2
        in_maps.append({
            "x_own": np.ascontiguousarray(x[b, h * HALF:(h + 1) * HALF]),
            "x_pre": np.ascontiguousarray(x[b, 0:HALF]),
            "Wr": Wr, "Wk": Wk, "Wv": Wv, "Wo": Wo, "Wf": Wf,
            "decay8": decay8, "lnw8": lnw8, "b_row": b_row,
            "lora_down": ld, "lora_up": lu, "down_aux": da, "up_aux": ua,
            "mask_col": np.full((128, 1), float(h), np.float32),
            "t0_col": np.full((128, 1), float(h * HALF), np.float32),
            "ident": ident,
        })

    res = run_bass_kernel_spmd(nc, in_maps, core_ids=list(range(8)))
    out = np.empty((B, T, D), np.float32)
    for cid in range(8):
        b, h = cid // 2, cid % 2
        out[b, h * HALF:(h + 1) * HALF] = res.results[cid]["out"]
    return out


# version 33 (final)
# speedup vs baseline: 1.0414x; 1.0073x over previous
import numpy as np
import ml_dtypes
import concourse.bass as bass
import concourse.mybir as mybir
from concourse.bass_utils import run_bass_kernel_spmd

F32 = mybir.dt.float32
BF16 = mybir.dt.bfloat16
AF = mybir.ActivationFunctionType
ALU = mybir.AluOpType
AX = mybir.AxisListType

B, T, D = 4, 4096, 1024
HALF = 2048
H = 1024
NCH = 8


class Prog:

    def __init__(self):
        self.streams = {k: [] for k in ("sync", "tensor", "vector", "scalar", "gpsimd")}
        self.counts = {}

    def emit(self, eng, fn, waits=(), inc=None, inc_n=1):
        waits = tuple((s, v) for (s, v) in waits if v > 0)
        self.streams[eng].append((fn, waits, inc, inc_n))
        if inc is not None:
            self.counts[inc] = self.counts.get(inc, 0) + inc_n


def build_nc(debug=False):
    nc = bass.Bass("TRN2", target_bir_lowering=False, debug=False)

    x_own = nc.dram_tensor("x_own", [HALF, D], F32, kind="ExternalInput").ap()
    x_pre = nc.dram_tensor("x_pre", [HALF, D], F32, kind="ExternalInput").ap()
    Wr_h = nc.dram_tensor("Wr", [D, D], F32, kind="ExternalInput").ap()
    Wk_h = nc.dram_tensor("Wk", [D, D], F32, kind="ExternalInput").ap()
    Wv_h = nc.dram_tensor("Wv", [D, D], F32, kind="ExternalInput").ap()
    Wo_h = nc.dram_tensor("Wo", [D, D], F32, kind="ExternalInput").ap()
    Wf_h = nc.dram_tensor("Wf", [D, D], F32, kind="ExternalInput").ap()
    decay8_h = nc.dram_tensor("decay8", [128, 8], F32, kind="ExternalInput").ap()
    lnw8_h = nc.dram_tensor("lnw8", [128, 8], F32, kind="ExternalInput").ap()
    b_row_h = nc.dram_tensor("b_row", [1, D], F32, kind="ExternalInput").ap()
    ld_h = nc.dram_tensor("lora_down", [4, 100], F32, kind="ExternalInput").ap()
    lu_h = nc.dram_tensor("lora_up", [50, 4], F32, kind="ExternalInput").ap()
    da_h = nc.dram_tensor("down_aux", [100, D], F32, kind="ExternalInput").ap()
    ua_h = nc.dram_tensor("up_aux", [D, 50], F32, kind="ExternalInput").ap()
    mask_h = nc.dram_tensor("mask_col", [128, 1], F32, kind="ExternalInput").ap()
    t0_h = nc.dram_tensor("t0_col", [128, 1], F32, kind="ExternalInput").ap()
    ident_h = nc.dram_tensor("ident", [128, 128], BF16, kind="ExternalInput").ap()
    out_h = nc.dram_tensor("out", [HALF, D], F32, kind="ExternalOutput").ap()

    import contextlib
    with contextlib.ExitStack() as _stk:
        lnT = _stk.enter_context(nc.sbuf_tensor("lnT", [128, NCH * HALF], BF16))
        yA = _stk.enter_context(nc.sbuf_tensor("yA", [128, NCH * HALF], BF16))
        slotA = _stk.enter_context(nc.sbuf_tensor("slotA", [128, NCH * 1024], BF16))
        slotB = _stk.enter_context(nc.sbuf_tensor("slotB", [128, NCH * 1024], BF16))
        slotC = _stk.enter_context(nc.sbuf_tensor("slotC", [128, NCH * 1024], BF16))
        k_m = _stk.enter_context(nc.sbuf_tensor("k_m", [128, H], F32))
        u_m = _stk.enter_context(nc.sbuf_tensor("u_m", [128, H], F32))
        csbuf = _stk.enter_context(nc.sbuf_tensor("csbuf", [128, H], F32))
        rbuf = _stk.enter_context(nc.sbuf_tensor("rbuf", [128, H], F32))
        k_m1 = _stk.enter_context(nc.sbuf_tensor("k_m1", [128, H], F32))
        u_m1 = _stk.enter_context(nc.sbuf_tensor("u_m1", [128, H], F32))
        csbuf1 = _stk.enter_context(nc.sbuf_tensor("csbuf1", [128, H], F32))
        rbuf1 = _stk.enter_context(nc.sbuf_tensor("rbuf1", [128, H], F32))
        y2A = _stk.enter_context(nc.sbuf_tensor("y2A", [128, NCH * H], BF16))
        castbuf = _stk.enter_context(nc.sbuf_tensor("castbuf", [128, 1024], BF16))
        castbuf2 = _stk.enter_context(nc.sbuf_tensor("castbuf2", [128, 1024], BF16))
        xt0 = _stk.enter_context(nc.sbuf_tensor("xt0", [128, D], F32))
        xt1 = _stk.enter_context(nc.sbuf_tensor("xt1", [128, D], F32))
        x1t0 = _stk.enter_context(nc.sbuf_tensor("x1t0", [128, D], F32))
        x1t1 = _stk.enter_context(nc.sbuf_tensor("x1t1", [128, D], F32))
        x1tc = _stk.enter_context(nc.sbuf_tensor("x1tc", [128, D], F32))
        xtp = x1tc
        wstage = _stk.enter_context(nc.sbuf_tensor("wstage", [128, D], F32))
        iota = _stk.enter_context(nc.sbuf_tensor("iota", [128, H], F32))
        ones_t = _stk.enter_context(nc.sbuf_tensor("ones_t", [128, H], BF16))
        onesc = _stk.enter_context(nc.sbuf_tensor("onesc", [128, 1], BF16))
        b_bcast = _stk.enter_context(nc.sbuf_tensor("b_bcast", [128, D], F32))
        ident_t = _stk.enter_context(nc.sbuf_tensor("ident_t", [128, 128], BF16))
        small = _stk.enter_context(nc.sbuf_tensor("small", [128, 64], F32))
        bias_own = _stk.enter_context(nc.sbuf_tensor("bias_own", [128, 16], F32))
        bias_ownN = _stk.enter_context(nc.sbuf_tensor("bias_ownN", [128, 16], F32))
        bias_pre = _stk.enter_context(nc.sbuf_tensor("bias_pre", [128, 16], F32))
        bn0 = _stk.enter_context(nc.sbuf_tensor("bn0", [128, 12], F32))
        bn0B = _stk.enter_context(nc.sbuf_tensor("bn0B", [128, 12], F32))
        mvA = _stk.enter_context(nc.sbuf_tensor("mvA", [128, 2], F32))
        mvB = _stk.enter_context(nc.sbuf_tensor("mvB", [128, 2], F32))
        colsA = _stk.enter_context(nc.sbuf_tensor("colsA", [128, 4], F32))
        colsB = _stk.enter_context(nc.sbuf_tensor("colsB", [128, 4], F32))
        srow = x1t0
        invcols = _stk.enter_context(nc.sbuf_tensor("invcols", [128, 32], F32))
        ldT = _stk.enter_context(nc.sbuf_tensor("ldT", [100, 4], BF16))
        luT = _stk.enter_context(nc.sbuf_tensor("luT", [4, 50], BF16))
        uaT = _stk.enter_context(nc.sbuf_tensor("uaT", [50, D], BF16))
        m1 = _stk.enter_context(nc.sbuf_tensor("m1", [4, D], BF16))
        m2 = _stk.enter_context(nc.sbuf_tensor("m2", [50, D], BF16))
        psK = _stk.enter_context(nc.psum_tensor("psK", [128, 1024], F32))
        psV = _stk.enter_context(nc.psum_tensor("psV", [128, 1024], F32))
        psR = _stk.enter_context(nc.psum_tensor("psR", [128, 1024], F32))
        psTb = _stk.enter_context(nc.psum_tensor("psTb", [128, 1024], BF16))
        psS = _stk.enter_context(nc.psum_tensor("psS", [1, 512], F32))
        s_ld = _stk.enter_context(nc.semaphore("s_ld"))
        s_st = _stk.enter_context(nc.semaphore("s_st"))
        s_mm = _stk.enter_context(nc.semaphore("s_mm"))
        s_ve = _stk.enter_context(nc.semaphore("s_ve"))
        s_sc = _stk.enter_context(nc.semaphore("s_sc"))
        s_gp = _stk.enter_context(nc.semaphore("s_gp"))
        s_x0 = _stk.enter_context(nc.semaphore("s_x0"))
        s_x1 = _stk.enter_context(nc.semaphore("s_x1"))
        s_x2 = _stk.enter_context(nc.semaphore("s_x2"))
        s_w0 = _stk.enter_context(nc.semaphore("s_w0"))
        s_w1 = _stk.enter_context(nc.semaphore("s_w1"))
        st0 = _stk.enter_context(nc.semaphore("st0"))
        st1 = _stk.enter_context(nc.semaphore("st1"))
        st2 = _stk.enter_context(nc.semaphore("st2"))
        block = _stk.enter_context(nc.Block())
        p = Prog()
        SEM = {"ld": s_ld, "st": s_st, "mm": s_mm, "ve": s_ve, "sc": s_sc,
               "gp": s_gp, "x0": s_x0, "x1": s_x1, "x2": s_x2,
               "w0": s_w0, "w1": s_w1,
               "st0": st0, "st1": st1, "st2": st2}

        def c(name):
            return p.counts.get(SEM[name], 0)

        def E(eng, semname=None, waits=(), inc_n=1):
            def run(fn):
                p.emit(eng, fn,
                       waits=[(SEM[s], v) for (s, v) in waits],
                       inc=(SEM[semname] if semname else None), inc_n=inc_n)
            return run

        def dbg(name, ap):
            if not debug:
                return
            t = nc.dram_tensor("dbg_" + name, list(ap.shape), ap.dtype,
                               kind="ExternalOutput").ap()
            w = [(SEM[s], p.counts.get(SEM[s], 0)) for s in
                 ("mm", "ve", "sc", "gp")]
            p.emit("sync", (lambda e, t=t, ap=ap: e.dma_start(out=t, in_=ap)),
                   waits=w, inc=SEM["ld"], inc_n=16)
            ldv = c("ld")
            for engn in ("sync", "tensor", "vector", "scalar", "gpsimd"):
                p.emit(engn, (lambda e, v=ldv: e.wait_ge(s_ld, v)))

        def wtile(slot, kc, mc):
            return slot[:, kc * 1024 + mc * 128: kc * 1024 + (mc + 1) * 128]

        def wmov(slot, kc, j):
            return slot[:, kc * 1024 + j * 512: kc * 1024 + (j + 1) * 512]

        def lnT_mt(m, t0, n):
            return lnT[:, m * HALF + t0: m * HALF + t0 + n]

        def yA_mt(m, t0, n):
            return yA[:, m * HALF + t0: m * HALF + t0 + n]

        lnT3 = lnT.rearrange("p (m t) -> p m t", m=8)
        psTb3 = psTb.rearrange("p (m t) -> p m t", m=8)

        @E("sync", "ld", inc_n=16)
        def _(e): return e.dma_start(out=small[:, 8:16], in_=decay8_h)
        @E("sync", "ld", inc_n=16)
        def _(e): return e.dma_start(out=small[:, 0:1], in_=mask_h)
        @E("sync", "ld", inc_n=16)
        def _(e): return e.dma_start(out=small[:, 1:2], in_=t0_h)
        @E("sync", "ld", inc_n=16)
        def _(e): return e.dma_start(out=ident_t[:], in_=ident_h)
        @E("sync", "ld", inc_n=16)
        def _(e): return e.dma_start(out=small[:, 56:64], in_=lnw8_h)
        ld_base = c("ld")

        @E("gpsimd", "gp")
        def _(e): return e.iota(iota[:], pattern=[[1, H]], base=0,
                                channel_multiplier=0,
                                allow_small_or_imprecise_dtypes=True)
        @E("gpsimd", "gp")
        def _(e): return e.memset(ones_t[:], 1.0)
        @E("gpsimd", "gp")
        def _(e): return e.memset(onesc[:], 1.0)
        @E("gpsimd", "gp")
        def _(e): return e.memset(small[:, 32:40], 0.0)
        @E("gpsimd", "gp")
        def _(e): return e.memset(small[:, 3:4], 1e-5)
        @E("gpsimd", "gp")
        def _(e): return e.memset(small[:, 4:5], 1e-6)
        @E("gpsimd", "gp")
        def _(e): return e.memset(small[:, 5:6], 4e-6)
        gp_eps = c("gp")

        @E("scalar", "sc", waits=[("ld", ld_base)])
        def _(e): return e.activation(out=small[:, 40:48], in_=small[:, 8:16],
                                      func=AF.Tanh, scale=0.5)
        @E("vector", "ve", waits=[("sc", c("sc"))])
        def _(e):
            return e.tensor_scalar(small[:, 8:16], small[:, 40:48], 0.5, 0.5,
                                   ALU.mult, ALU.add)
        @E("vector", "ve", waits=[("ve", c("ve"))])
        def _(e): return e.tensor_scalar_max(small[:, 40:48], small[:, 8:16], 1e-7)
        ve_dec = c("ve")
        @E("scalar", "sc", waits=[("ve", ve_dec)])
        def _(e): return e.activation(out=small[:, 24:32], in_=small[:, 40:48],
                                      func=AF.Ln)
        sc_lnd = c("sc")
        @E("vector", "ve", waits=[("sc", sc_lnd)])
        def _(e): return e.tensor_scalar_mul(small[:, 16:24], small[:, 24:32], -1.0)
        @E("vector", "ve", waits=[("ve", c("ve"))])
        def _(e): return e.tensor_scalar_add(small[:, 2:3], small[:, 1:2], 1024.0)
        for h in range(2):
            @E("vector", "ve", waits=[("ve", c("ve"))])
            def _(e, h=h):
                return e.tensor_scalar(bias_own[:, h * 8:(h + 1) * 8],
                                       small[:, 16:24], small[:, 1 + h:2 + h],
                                       None, ALU.mult)
            @E("vector", "ve", waits=[("ve", c("ve"))])
            def _(e, h=h):
                return e.tensor_scalar_mul(bias_pre[:, h * 8:(h + 1) * 8],
                                           small[:, 16:24], float(h * 1024))
        @E("vector", "ve", waits=[("ve", c("ve"))])
        def _(e): return e.tensor_scalar_mul(bias_ownN[:], bias_own[:], -1.0)
        ve_bias = c("ve")

        wstate = {"cast": {}, "tp": {}}
        psTbg = {"sc": 0}

        def weight_round(src_h, slot, rc, stage, cbuf,
                         scale_lnw=False, m3_ps=None, m3_w=(), out_ve=None,
                         dma_w=()):
            sid, cid = id(stage), id(cbuf)
            ws = "w0" if stage is wstage else "w1"
            @E("sync", ws, inc_n=16,
               waits=[("ve", wstate["cast"].get(sid, 0))] + list(dma_w))
            def _(e, rc=rc, stage=stage):
                return e.dma_start(out=stage[:],
                                   in_=src_h[rc * 128:(rc + 1) * 128, :])
            ldv = c(ws)
            if m3_ps is not None:
                for j in range(2):
                    @E("tensor", "mm",
                       waits=(([(ws, ldv), ("sc", c("sc"))] + list(m3_w))
                              if j == 0 else ()))
                    def _(e, rc=rc, j=j, m3_ps=m3_ps):
                        return e.matmul(m3_ps[:, j * 512:(j + 1) * 512],
                                        uaT[:, rc * 128:(rc + 1) * 128],
                                        m2[:, j * 512:(j + 1) * 512],
                                        start=True, stop=True)
                @E("vector", "ve", waits=[("mm", c("mm")), (ws, ldv)])
                def _(e, stage=stage, m3_ps=m3_ps):
                    return e.tensor_add(stage[:], stage[:], m3_ps[:])
                if out_ve is not None:
                    out_ve.append(c("ve"))
                cast_w = [("ve", c("ve")), ("mm", wstate["tp"].get(cid, 0))]
            else:
                cast_w = [(ws, ldv), ("mm", wstate["tp"].get(cid, 0))]
            @E("vector", "ve", waits=cast_w)
            def _(e, stage=stage, cbuf=cbuf):
                return e.tensor_copy(cbuf[:], stage[:])
            wstate["cast"][sid] = vecast = c("ve")
            for kc in range(8):
                @E("tensor", "mm",
                   waits=([("ve", vecast), ("sc", psTbg["sc"])] if kc == 0 else ()))
                def _(e, kc=kc, cbuf=cbuf):
                    return e.transpose(psTb[:, kc * 128:(kc + 1) * 128],
                                       cbuf[:, kc * 128:(kc + 1) * 128],
                                       ident_t[:])
            wstate["tp"][cid] = mmt = c("mm")
            if scale_lnw:
                for kc in range(8):
                    @E("scalar", "sc", waits=([("mm", mmt)] if kc == 0 else ()))
                    def _(e, kc=kc, rc=rc, slot=slot):
                        return e.activation(
                            out=slot[:, kc * 1024 + rc * 128:
                                     kc * 1024 + (rc + 1) * 128],
                            in_=psTb[:, kc * 128:(kc + 1) * 128],
                            func=AF.Copy, scale=small[:, 56 + kc:57 + kc])
            else:
                slot3 = slot.rearrange("p (kc e) -> p kc e", kc=8)
                @E("scalar", "sc", waits=[("mm", mmt)])
                def _(e, rc=rc, slot3=slot3):
                    return e.copy(out=slot3[:, :, rc * 128:(rc + 1) * 128],
                                  in_=psTb3[:])
            psTbg["sc"] = c("sc")
            return c("sc")

        last_apply = {0: 0, 1: 0}
        last_bn = {0: 0, 1: 0}
        last_tp = {0: 0, 1: 0}
        last_ln = {0: 0, 1: 0}
        last_applyx = {0: 0, 1: 0, 2: 0}
        last_bnx = {0: 0, 1: 0, 2: 0}

        def ln_chunk_p1(src_h, i, tag="", use_castbuf=False):
            par = i % 2
            slot = i % 3
            xt = (xt0, xt1, xtp)[slot]
            mv = mvA if par == 0 else mvB
            cols = colsA if par == 0 else colsB
            if use_castbuf:
                lnc = castbuf[:] if par == 0 else castbuf2[:]
            else:
                lnc = yA[:, 1024:2048] if par == 0 else yA[:, 2048:3072]
            xs = ("x0", "x1", "x2")[slot]
            @E("sync", xs, inc_n=16,
               waits=[("sc", last_applyx[slot]), ("ve", last_bnx[slot])])
            def _(e, i=i, xt=xt):
                return e.dma_start(out=xt[:], in_=src_h[i * 128:(i + 1) * 128, :])
            ldx = c(xs)
            bnp = bn0 if par == 0 else bn0B
            @E("vector", "ve", waits=[(xs, ldx), ("sc", last_ln[par])])
            def _(e, xt=xt, bnp=bnp): return e.bn_stats(bnp[:, 0:6], xt[:, 0:512])
            @E("vector", "ve")
            def _(e, xt=xt, bnp=bnp): return e.bn_stats(bnp[:, 6:12], xt[:, 512:1024])
            @E("vector", "ve", waits=[("ve", c("ve"))])
            def _(e, mv=mv, bnp=bnp): return e.bn_aggr(mv[:], bnp[:])
            last_bn[par] = last_bnx[slot] = ve_mv = c("ve")
            if i == 0 and tag:
                dbg(tag + "_x0", xt[:])
                dbg(tag + "_mv0", mv[:])
            ctx = dict(i=i, par=par, lnc=lnc)

            def finish(par=par, slot=slot, mv=mv, cols=cols, xt=xt, lnc=lnc,
                       ve_mv=ve_mv, ctx=ctx):
                @E("scalar", "sc", waits=[("ve", ve_mv), ("gp", gp_eps)])
                def _(e, mv=mv, cols=cols):
                    return e.activation(out=cols[:, 0:1], in_=mv[:, 1:2],
                                        func=AF.Ln, bias=small[:, 3:4])
                last_ln[par] = c("sc")
                @E("scalar", "sc", waits=[("sc", c("sc"))])
                def _(e, cols=cols):
                    return e.activation(out=cols[:, 1:2], in_=cols[:, 0:1],
                                        func=AF.Exp, scale=-0.5)
                sc_inv = c("sc")
                @E("vector", "ve", waits=[("sc", sc_inv)])
                def _(e, mv=mv, cols=cols):
                    return e.scalar_tensor_tensor(cols[:, 2:3], mv[:, 0:1], -1.0,
                                                  cols[:, 1:2], ALU.mult, ALU.mult)
                ve_negmi = c("ve")
                @E("scalar", "sc", waits=[("ve", ve_negmi), ("mm", last_tp[par])])
                def _(e, xt=xt, cols=cols, lnc=lnc):
                    return e.activation(out=lnc, in_=xt[:], func=AF.Identity,
                                        bias=cols[:, 2:3], scale=cols[:, 1:2])
                last_apply[par] = last_applyx[slot] = ctx["sc_apply"] = c("sc")
            ctx["finish"] = finish
            return ctx

        def ln_chunk_p2(ctx, ready):
            i, par, lnc = ctx["i"], ctx["par"], ctx["lnc"]
            for j in range(8):
                @E("tensor", "mm",
                   waits=([("sc", max(ctx["sc_apply"], psTbg["sc"]))]
                          if j == 0 else ()))
                def _(e, j=j, lnc=lnc):
                    return e.transpose(psTb[:, j * 128:(j + 1) * 128],
                                       lnc[:, j * 128:(j + 1) * 128], ident_t[:])
            last_tp[par] = mmt = c("mm")
            @E("scalar", "sc", waits=[("mm", mmt)])
            def _(e, i=i):
                return e.copy(out=lnT3[:, :, i * 128:(i + 1) * 128], in_=psTb3[:])
            psTbg["sc"] = c("sc")
            ready.append(c("sc"))

        def proj(slot, m, h, ps, w0=(), w1=()):
            cnt = 0
            for kc in range(8):
                for j in range(2):
                    w = list(w0) if cnt == 0 else (list(w1) if cnt == 1 else [])
                    cnt += 1
                    @E("tensor", "mm", waits=w)
                    def _(e, kc=kc, j=j, m=m, h=h, ps=ps, slot=slot):
                        return e.matmul(ps[:, j * 512:(j + 1) * 512],
                                        wtile(slot, kc, m),
                                        lnT_mt(kc, h * H + j * 512, 512),
                                        start=(kc == 0), stop=(kc == 7))
            return c("mm")

        BUFS = [(k_m, u_m, csbuf, rbuf), (k_m1, u_m1, csbuf1, rbuf1)]

        def prefix_iter(h, m, trk, lnready):
            par = (h * 8 + m) % 2
            kB, uB, cB, rB = BUFS[par]
            lnw = [("sc", max(trk["last_sck"], lnready[h * 8 + 3], trk["wkd"]))]
            lnw1 = [("sc", max(lnready[h * 8 + 7], trk["wvd"]))]
            mmk = proj(slotA, m, h, psK, w0=lnw, w1=lnw1)
            mmv = proj(slotB, m, h, psV, w0=[("ve", trk["last_kv"])])
            @E("scalar", "sc", waits=[("mm", mmk), ("gp", trk["gpu"][par]),
                                      ("ve", trk["red"][par])])
            def _(e, kB=kB): return e.copy(out=kB[:], in_=psK[:])
            trk["last_sck"] = sck = c("sc")
            @E("vector", "ve", waits=[("sc", sck), ("mm", mmv),
                                      ("gp", trk["gpu"][par])])
            def _(e, kB=kB, uB=uB): return e.tensor_mul(uB[:], kB[:], psV[:])
            trk["last_kv"] = c("ve")
            @E("scalar", "sc", waits=[("ve", ve_bias),
                                      ("ve", trk["min"][par])])
            def _(e, m=m, h=h, cB=cB):
                return e.activation(out=cB[:], in_=iota[:],
                                    func=AF.Identity,
                                    scale=small[:, 16 + m:17 + m],
                                    bias=bias_pre[:, h * 8 + m:h * 8 + m + 1])
            sc_c = c("sc")
            @E("vector", "ve", waits=[("sc", sc_c), ("gp", trk["gpu"][par])])
            def _(e, cB=cB, rB=rB):
                return e.tensor_scalar_min(rB[:], cB[:], 23.025850929940457)
            trk["min"][par] = ve_c = c("ve")
            @E("scalar", "sc", waits=[("ve", ve_c)])
            def _(e, cB=cB, rB=rB):
                return e.activation(out=cB[:], in_=rB[:], func=AF.Exp)
            sc_ce = c("sc")
            @E("gpsimd", "gp", waits=[("sc", sc_ce)])
            def _(e, kB=kB, uB=uB, cB=cB):
                return e.tensor_mul(kB[:], uB[:], cB[:])
            trk["gpu"][par] = gp_u = c("gp")
            @E("vector", "ve", waits=[("gp", gp_u)])
            def _(e, kB=kB, uB=uB):
                return e.tensor_reduce(uB[:, 0:1], kB[:], AX.X, ALU.add)
            dst = 32 + m if h == 0 else 8 + m
            @E("vector", "ve", waits=[("ve", c("ve"))])
            def _(e, dst=dst, uB=uB):
                return e.tensor_copy(small[:, dst:dst + 1], uB[:, 0:1])
            trk["red"][par] = c("ve")

        def carry_combine():
            @E("vector", "ve", waits=[("gp", c("gp")), ("ve", c("ve"))])
            def _(e):
                return e.tensor_add(small[:, 48:56], small[:, 32:40], small[:, 8:16])
            @E("vector", "ve", waits=[("ve", c("ve"))])
            def _(e):
                return e.tensor_scalar(small[:, 40:48], small[:, 48:56],
                                       small[:, 0:1], None, ALU.mult)

        last_srow_dma = {0: 0}
        sc_inv_h = {0: 0, 1: 0}

        def mid_iter(h, m, trk, lnready, post_clamp=None):
            par = (h * 8 + m) % 2
            kB, uB, cB, rB = BUFS[par]
            lnw = [("sc", max(trk["last_sck"], lnready[h * 8 + 3]))]
            lnw1 = [("sc", lnready[h * 8 + 7])]
            mmk = proj(slotA, m, h, psK, w0=lnw, w1=lnw1)
            mmv = proj(slotB, m, h, psV, w0=[("ve", trk["last_kv"])])
            mmr = proj(slotC, m, h, psR,
                       w0=[("sc", max(trk["last_th"], trk["wrd"]))])
            @E("scalar", "sc", waits=[("mm", mmk), ("ve", trk["yst"][par]),
                                      ("gp", trk["gpu"][par])])
            def _(e, kB=kB): return e.copy(out=kB[:], in_=psK[:])
            trk["last_sck"] = sck = c("sc")
            @E("vector", "ve", waits=[("sc", sck), ("mm", mmv),
                                      ("gp", trk["gpu"][par])])
            def _(e, kB=kB, uB=uB): return e.tensor_mul(uB[:], kB[:], psV[:])
            trk["last_kv"] = c("ve")
            @E("scalar", "sc", waits=[("ve", trk["state"][par])])
            def _(e, m=m, h=h, rB=rB):
                return e.activation(out=rB[:], in_=iota[:], func=AF.Exp,
                                    scale=small[:, 24 + m:25 + m],
                                    bias=bias_ownN[:, h * 8 + m:h * 8 + m + 1])
            @E("scalar", "sc", waits=[("ve", trk["yst"][par]),
                                      ("gp", trk["gpu"][par])])
            def _(e, m=m, h=h, cB=cB):
                return e.activation(out=cB[:], in_=iota[:], func=AF.Identity,
                                    scale=small[:, 16 + m:17 + m],
                                    bias=bias_own[:, h * 8 + m:h * 8 + m + 1])
            sc_c2 = c("sc")
            @E("vector", "ve", waits=[("sc", sc_c2), ("ve", c("ve"))])
            def _(e, kB=kB, cB=cB):
                return e.tensor_scalar_min(kB[:], cB[:], 23.025850929940457)
            ve_c = c("ve")
            @E("scalar", "sc", waits=[("ve", ve_c)])
            def _(e, kB=kB, cB=cB):
                return e.activation(out=cB[:], in_=kB[:], func=AF.Exp)
            sc_ce = c("sc")
            if post_clamp is not None:
                post_clamp()
            @E("gpsimd", "gp", waits=[("sc", sc_ce)])
            def _(e, kB=kB, uB=uB, cB=cB):
                return e.tensor_mul(kB[:], uB[:], cB[:])
            trk["gpu"][par] = gp_u = c("gp")
            init = (small[:, 40 + m:41 + m] if h == 0
                    else small[:, 48 + m:49 + m])
            @E("vector", "ve", waits=[("gp", gp_u), ("ve", c("ve"))])
            def _(e, init=init, uB=uB, kB=kB):
                return e.tensor_tensor_scan(out=uB[:], data0=ones_t[:],
                                            data1=kB[:], initial=init,
                                            op0=ALU.mult, op1=ALU.add)
            ve_scan = c("ve")
            if h == 0:
                @E("vector", "ve", waits=[("ve", c("ve"))])
                def _(e, m=m, uB=uB):
                    return e.tensor_copy(small[:, 48 + m:49 + m], uB[:, H - 1:H])
            @E("vector", "ve", waits=[("gp", c("gp"))])
            def _(e, kB=kB, uB=uB, rB=rB):
                return e.tensor_mul(kB[:], uB[:], rB[:])
            trk["state"][par] = c("ve")
            @E("scalar", "sc", waits=[("mm", mmr), ("ve", ve_scan)])
            def _(e, cB=cB): return e.activation(out=cB[:], in_=psR[:],
                                                 func=AF.Tanh, scale=0.5)
            trk["last_th"] = sc_r = c("sc")
            @E("vector", "ve", waits=[("sc", sc_r)])
            def _(e, m=m, h=h, kB=kB, cB=cB):
                return e.scalar_tensor_tensor(yA_mt(m, h * H, H), cB[:],
                                              1.0, kB[:], ALU.add, ALU.mult)
            trk["yst"][par] = c("ve")
            y2w = [("mm", trk["mm_stats"])] if h == 1 else []
            @E("vector", "ve", waits=y2w)
            def _(e, m=m, h=h):
                return e.tensor_mul(y2A[:, m * H:(m + 1) * H],
                                    yA_mt(m, h * H, H), yA_mt(m, h * H, H))
            trk["y2"] = c("ve")
            trk["y2l"][m] = c("ve")
            if m == 7:
                trk["y2_h" + str(h)] = c("ve")

        def mid_half_end(h, trk):
            ve_y2 = trk["y2"]
            sc_evs = []
            for j in range(2):
                for mc in range(8):
                    w = [("ve", trk["y2l"][mc])]
                    if mc == 0 and sc_evs:
                        w.append(("sc", sc_evs[-1]))
                    @E("tensor", "mm", waits=w)
                    def _(e, j=j, mc=mc):
                        return e.matmul(psS[:], onesc[:],
                                        y2A[:, mc * H + j * 512:
                                            mc * H + (j + 1) * 512],
                                        start=(mc == 0), stop=(mc == 7))
                mm_st = c("mm")
                @E("scalar", "sc", waits=[("mm", mm_st),
                                          ("ld", last_srow_dma[0])])
                def _(e, j=j):
                    return e.copy(out=srow[0:1, j * 512:(j + 1) * 512], in_=psS[:])
                sc_evs.append(c("sc"))
            trk["mm_stats"] = c("mm")
            sc_sr = sc_evs[-1]
            for i2 in range(8):
                @E("sync", "ld", inc_n=16,
                   waits=([("sc", sc_sr)] if i2 == 0 else ()))
                def _(e, i2=i2, h=h):
                    return e.dma_start(out=invcols[:, 16 * h + i2:16 * h + i2 + 1],
                                       in_=srow[0:1, i2 * 128:(i2 + 1) * 128])
            last_srow_dma[0] = ld_s = c("ld")
            @E("scalar", "sc", waits=[("ld", ld_s)])
            def _(e, h=h):
                return e.activation(out=invcols[:, 16 * h + 8:16 * h + 16],
                                    in_=invcols[:, 16 * h:16 * h + 8],
                                    func=AF.Ln, scale=1.0 / 1024.0,
                                    bias=small[:, 5:6])
            @E("scalar", "sc", waits=[("sc", c("sc"))])
            def _(e, h=h):
                return e.activation(out=invcols[:, 16 * h:16 * h + 8],
                                    in_=invcols[:, 16 * h + 8:16 * h + 16],
                                    func=AF.Exp, scale=-0.5)
            sc_inv_h[h] = c("sc")

        def part3(sc_swap, ld_he1, gp_all, sc_mid):
            @E("sync", "ld", inc_n=16, waits=[("ld", c("ld"))])
            def _(e): return e.dma_start(out=b_bcast[:],
                                         in_=b_row_h.broadcast_to((128, D)))
            ld_b = c("ld")

            last_x1 = {0: 0, 1: 0}
            last_dma = {0: 0, 1: 0}
            last_stt = {0: 0, 1: 0}
            st_l1 = {"l1": 0, "oa": 0}
            oring = [k_m, u_m, csbuf]
            oring_st = [0, 0, 0]
            oabuf = [rbuf, k_m1]
            last_sttx = {0: 0, 1: 0}
            ld_pre = {}
            pendB, pendC = [], []

            def stageB(ctx, mid_emit=None):
                ig, h, i = ctx["ig"], ctx["h"], ctx["i"]
                cb = ctx["cb"]
                for j in range(8):
                    @E("tensor", "mm",
                       waits=([("sc", max(ctx["sc_ap"], psTbg["sc"]))]
                              if j == 0 else ()))
                    def _(e, j=j, cb=cb):
                        return e.transpose(psTb[:, j * 128:(j + 1) * 128],
                                           cb[:, j * 128:(j + 1) * 128],
                                           ident_t[:])
                last_tp[ig % 2] = mmt = c("mm")
                @E("scalar", "sc", waits=[("mm", mmt)])
                def _(e, ig=ig):
                    return e.copy(out=lnT3[:, :, ig * 128:(ig + 1) * 128],
                                  in_=psTb3[:])
                psTbg["sc"] = st_l1["l1"] = sc_l1 = c("sc")
                if mid_emit is not None:
                    mid_emit()
                first = True
                for m in range(8):
                    for j in range(2):
                        w = [("sc", sc_l1), ("ve", st_l1["oa"])] if first else []
                        first = False
                        @E("tensor", "mm", waits=w)
                        def _(e, m=m, j=j, h=h, i=i):
                            return e.matmul(psV[:, j * 512:(j + 1) * 512],
                                            lnT_mt(m, h * H + i * 128, 128),
                                            wmov(slotB, m, j),
                                            start=(m == 0), stop=(m == 7))
                ctx["mm_ffn"] = c("mm")
                pendC.append(ctx)

            def stageC(ctx):
                ig, par = ctx["ig"], ctx["par"]
                x1t, oa, cols = ctx["x1t"], ctx["oa"], ctx["cols"]
                ot = oring[ig % 3]
                @E("vector", "ve", waits=[("mm", ctx["mm_ffn"]),
                                          ("sc", ctx["sc_inv"]),
                                          ("gp", gp_all)])
                def _(e, x1t=x1t, oa=oa, cols=cols):
                    return e.scalar_tensor_tensor(oa[:], psV[:], cols[:, 1:2],
                                                  x1t[:], ALU.mult, ALU.add)
                st_l1["oa"] = c("ve")
                @E("vector", "ve", waits=[("st" + str(ig % 3), oring_st[ig % 3]),
                                          ("ld", ld_b), ("gp", gp_all)])
                def _(e, ot=ot, oa=oa):
                    return e.tensor_add(ot[:], oa[:], b_bcast[:])
                last_x1[par] = ve_out = c("ve")
                sts = "st" + str(ig % 3)
                @E("sync", sts, inc_n=16, waits=[("ve", ve_out)])
                def _(e, ig=ig, ot=ot):
                    return e.dma_start(out=out_h[ig * 128:(ig + 1) * 128, :],
                                       in_=ot[:])
                oring_st[ig % 3] = c(sts)

            wo_mm = {}

            def emit_wo(ig):
                if ig > 15:
                    return
                h, i = ig // 8, ig % 8
                psWo = psK if ig % 2 == 0 else psR
                wpar = ig % 2
                first = True
                for m in range(8):
                    for j in range(2):
                        w = ([("ve", last_stt[wpar]), ("sc", sc_mid),
                              ("sc", sc_swap)] if first else [])
                        first = False
                        @E("tensor", "mm", waits=w)
                        def _(e, m=m, j=j, h=h, i=i, psWo=psWo):
                            return e.matmul(psWo[:, j * 512:(j + 1) * 512],
                                            yA_mt(m, h * H + i * 128, 128),
                                            wmov(slotA, m, j),
                                            start=(m == 0), stop=(m == 7))
                wo_mm[ig] = c("mm")

            x1ring = [x1t0, x1t1, x1tc]
            last_apply_r = {0: 0, 1: 0, 2: last_applyx[2]}
            for h in range(2):
                for i in range(8):
                    ig = h * 8 + i
                    par = i % 2
                    xt = xt0 if par == 0 else xt1
                    x1t = x1ring[ig % 3]
                    mv = mvA if par == 0 else mvB
                    cols = colsA if par == 0 else colsB
                    cb = castbuf if ig % 2 == 0 else castbuf2
                    psWo = psK if ig % 2 == 0 else psR
                    wpar = ig % 2
                    if ig == 0:
                        emit_wo(0)
                        emit_wo(1)
                    else:
                        stageB(pendB.pop(0), mid_emit=lambda ig=ig: emit_wo(ig + 1))
                    mm_wo = wo_mm[ig]
                    for igp in ([ig, ig + 1] if ig == 0 else [ig + 1]):
                        if igp > 15:
                            continue
                        parp = igp % 8 % 2
                        xtp = xt0 if parp == 0 else xt1
                        xsp = "x0" if parp == 0 else "x1"
                        @E("sync", xsp, inc_n=16,
                           waits=[("ve", last_sttx[parp]),
                                  ("sc", sc_swap)])
                        def _(e, igp=igp, xtp=xtp):
                            return e.dma_start(
                                out=xtp[:],
                                in_=x_own[igp * 128:(igp + 1) * 128, :])
                        ld_pre[igp] = (xsp, c(xsp))
                    ld_x = ld_pre[ig]
                    @E("vector", "ve",
                       waits=[("mm", mm_wo), (ld_x[0], ld_x[1]),
                              ("ld", ld_he1), ("sc", sc_inv_h[h]),
                              ("sc", last_apply_r[ig % 3])])
                    def _(e, h=h, i=i, xt=xt, x1t=x1t, psWo=psWo):
                        return e.scalar_tensor_tensor(
                            x1t[:], psWo[:], invcols[:, 16 * h + i:16 * h + i + 1],
                            xt[:], ALU.mult, ALU.add)
                    last_stt[wpar] = last_sttx[par] = c("ve")
                    @E("vector", "ve", waits=[("sc", last_ln[par])])
                    def _(e, x1t=x1t): return e.bn_stats(bn0[:, 0:6], x1t[:, 0:512])
                    @E("vector", "ve")
                    def _(e, x1t=x1t): return e.bn_stats(bn0[:, 6:12], x1t[:, 512:1024])
                    @E("vector", "ve", waits=[("ve", c("ve"))])
                    def _(e, mv=mv): return e.bn_aggr(mv[:], bn0[:])
                    last_bn[par] = ve_mv = c("ve")
                    @E("vector", "ve", waits=[("ve", ve_mv)])
                    def _(e, mv=mv, cols=cols):
                        return e.tensor_scalar_mul(cols[:, 2:3], mv[:, 0:1], -1.0)
                    ve_negm = c("ve")
                    @E("scalar", "sc", waits=[("ve", ve_negm),
                                              ("mm", last_tp[ig % 2])])
                    def _(e, x1t=x1t, cols=cols, cb=cb):
                        return e.activation(out=cb[:], in_=x1t[:],
                                            func=AF.Identity,
                                            bias=cols[:, 2:3], scale=1.0)
                    last_apply_r[ig % 3] = sc_ap = c("sc")
                    @E("scalar", "sc", waits=[("ve", ve_mv)])
                    def _(e, mv=mv, cols=cols):
                        return e.activation(out=cols[:, 0:1], in_=mv[:, 1:2],
                                            func=AF.Ln, bias=small[:, 3:4])
                    last_ln[par] = c("sc")
                    @E("scalar", "sc", waits=[("sc", c("sc"))])
                    def _(e, cols=cols):
                        return e.activation(out=cols[:, 1:2], in_=cols[:, 0:1],
                                            func=AF.Exp, scale=-0.5)
                    sc_inv_i = c("sc")
                    pendB.append(dict(ig=ig, h=h, i=i, par=par, x1t=x1t,
                                      oa=oabuf[ig % 2], cb=cb, sc_ap=sc_ap,
                                      sc_inv=sc_inv_i, cols=cols))
                    if pendC:
                        stageC(pendC.pop(0))
            while pendB:
                stageB(pendB.pop(0))
                if pendC:
                    stageC(pendC.pop(0))
            while pendC:
                stageC(pendC.pop(0))

        dbg("small_pro", small[:])
        rdy_pre = []
        rounds = ([(Wk_h, slotA, rc) for rc in range(8)]
                  + [(Wv_h, slotB, rc) for rc in range(8)]
                  + [(Wr_h, slotC, rc) for rc in range(8)])
        slot_done = {}
        ri = 0
        stages_pro = (wstage, k_m1)
        cbufs_pro = (castbuf, y2A[:, 0:1024])
        lnq = []
        pend1 = []

        def push_chunk(ctx, rdy):
            pend1.append(ctx)
            if len(pend1) == 2:
                pc = pend1.pop(0)
                pc["finish"]()
                lnq.append(pc)
            if len(lnq) == 2:
                ln_chunk_p2(lnq.pop(0), rdy)

        def flush_chunks(rdy):
            while pend1:
                pc = pend1.pop(0)
                pc["finish"]()
                lnq.append(pc)
                if len(lnq) == 2:
                    ln_chunk_p2(lnq.pop(0), rdy)
            while lnq:
                ln_chunk_p2(lnq.pop(0), rdy)

        for s in range(16):
            push_chunk(ln_chunk_p1(x_pre, s), rdy_pre)
            for _ in range(2):
                if ri < len(rounds):
                    srch, slot, rc = rounds[ri]
                    slot_done[id(slot)] = weight_round(
                        srch, slot, rc, stages_pro[ri % 2], cbufs_pro[ri % 2])
                    ri += 1
        while ri < len(rounds):
            srch, slot, rc = rounds[ri]
            slot_done[id(slot)] = weight_round(
                srch, slot, rc, stages_pro[ri % 2], cbufs_pro[ri % 2])
            ri += 1
        flush_chunks(rdy_pre)
        dbg("lnTpre", lnT[:])

        @E("sync", "ld", inc_n=16,
           waits=[("ve", wstate["cast"].get(id(wstage), 0)),
                  ("ld", c("ld")), ("w0", c("w0")), ("w1", c("w1"))])
        def _(e): return e.dma_start(out=wstage[0:4, 0:100], in_=ld_h)
        @E("vector", "ve", waits=[("ld", c("ld")),
                                  ("mm", wstate["tp"].get(id(castbuf), 0))])
        def _(e): return e.tensor_copy(castbuf[0:4, 0:100], wstage[0:4, 0:100])
        @E("tensor", "mm", waits=[("ve", c("ve")), ("sc", c("sc"))])
        def _(e): return e.transpose(psTb[0:100, 0:4], castbuf[0:4, 0:100],
                                     ident_t[0:4, 0:4])
        @E("scalar", "sc", waits=[("mm", c("mm"))])
        def _(e): return e.copy(out=ldT[:], in_=psTb[0:100, 0:4])

        @E("sync", "ld", inc_n=16, waits=[("ve", c("ve")), ("ld", c("ld"))])
        def _(e): return e.dma_start(out=wstage[0:50, 0:4], in_=lu_h)
        @E("vector", "ve", waits=[("ld", c("ld")), ("sc", c("sc"))])
        def _(e): return e.tensor_copy(castbuf[0:50, 0:4], wstage[0:50, 0:4])
        @E("tensor", "mm", waits=[("ve", c("ve")), ("sc", c("sc"))])
        def _(e): return e.transpose(psTb[0:4, 0:50], castbuf[0:50, 0:4],
                                     ident_t[0:50, 0:50])
        @E("scalar", "sc", waits=[("mm", c("mm"))])
        def _(e): return e.copy(out=luT[:], in_=psTb[0:4, 0:50])

        for rc in range(8):
            @E("sync", "ld", inc_n=16, waits=[("ve", c("ve")), ("sc", c("sc")),
                                              ("ld", c("ld"))])
            def _(e, rc=rc):
                return e.dma_start(out=wstage[:, 0:50],
                                   in_=ua_h[rc * 128:(rc + 1) * 128, :])
            @E("vector", "ve", waits=[("ld", c("ld")), ("mm", c("mm"))])
            def _(e): return e.tensor_copy(castbuf[:, 0:50], wstage[:, 0:50])
            @E("tensor", "mm", waits=[("ve", c("ve")), ("sc", c("sc"))])
            def _(e): return e.transpose(psTb[0:50, 0:128], castbuf[:, 0:50],
                                         ident_t[:])
            @E("scalar", "sc", waits=[("mm", c("mm"))])
            def _(e, rc=rc):
                return e.copy(out=uaT[:, rc * 128:(rc + 1) * 128],
                              in_=psTb[0:50, 0:128])

        @E("sync", "ld", inc_n=16, waits=[("ve", c("ve")), ("ld", c("ld"))])
        def _(e): return e.dma_start(out=wstage[0:100, :], in_=da_h)
        @E("vector", "ve", waits=[("ld", c("ld")), ("sc", c("sc")), ("mm", c("mm"))])
        def _(e): return e.tensor_copy(castbuf[0:100, :], wstage[0:100, :])
        ve_da = c("ve")
        for j in range(2):
            @E("tensor", "mm", waits=([("ve", ve_da), ("sc", c("sc"))] if j == 0 else ()))
            def _(e, j=j):
                return e.matmul(psR[0:4, j * 512:(j + 1) * 512], ldT[:],
                                castbuf[0:100, j * 512:(j + 1) * 512],
                                start=True, stop=True)
        @E("scalar", "sc", waits=[("mm", c("mm"))])
        def _(e): return e.copy(out=m1[:], in_=psR[0:4, 0:1024])
        for j in range(2):
            @E("tensor", "mm", waits=[("sc", c("sc"))])
            def _(e, j=j):
                return e.matmul(psR[0:50, j * 512:(j + 1) * 512], luT[:],
                                m1[:, j * 512:(j + 1) * 512], start=True, stop=True)
        @E("scalar", "sc", waits=[("mm", c("mm"))])
        def _(e): return e.copy(out=m2[:], in_=psR[0:50, 0:1024])
        wstate["cast"][id(wstage)] = max(wstate["cast"].get(id(wstage), 0), c("ve"))
        wstate["tp"][id(castbuf)] = max(wstate["tp"].get(id(castbuf), 0), c("mm"))

        trk = {"sck": [0, 0], "kv": [0, 0], "gpu": [0, 0], "min": [0, 0],
               "red": [0, 0], "last_sck": slot_done.get(id(slotC), 0),
               "last_kv": 0,
               "wkd": slot_done.get(id(slotA), 0),
               "wvd": slot_done.get(id(slotB), 0),
               "wrd": slot_done.get(id(slotC), 0)}
        rdy_own = []
        for m in range(8):
            prefix_iter(0, m, trk, rdy_pre)
        for m in range(8):
            prefix_iter(1, m, trk, rdy_pre)
            for i2 in ([2 * m, 2 * m + 1] if m < 4 else []):
                push_chunk(ln_chunk_p1(x_own, i2,
                                       tag=("own" if i2 == 0 else "")), rdy_own)
        flush_chunks(rdy_own)
        carry_combine()
        dbg("small_prefix", small[:])

        trk.update({"state": [0, 0], "yst": [0, 0],
                    "th": [0, 0], "last_th": 0, "y2": 0, "mm_stats": 0,
                    "y2l": [0] * 8})
        for m in range(8):
            mid_iter(0, m, trk, rdy_own)
            for i2 in ([8 + 2 * m, 9 + 2 * m] if m < 4 else []):
                push_chunk(ln_chunk_p1(x_own, i2, use_castbuf=True), rdy_own)
        flush_chunks(rdy_own)
        mid_half_end(0, trk)
        stages_sw = (wstage, xt1)
        cbufs_sw = (castbuf, castbuf2)
        wstate["tp"][id(castbuf)] = max(wstate["tp"].get(id(castbuf), 0),
                                        last_tp[0])
        wstate["tp"][id(castbuf2)] = max(wstate["tp"].get(id(castbuf2), 0),
                                         last_tp[1])
        trk["y2l"] = [0] * 8
        swl = []
        for m in range(8):
            def rounds(m=m):
                weight_round(Wo_h, slotA, m, stages_sw[0], cbufs_sw[0],
                             scale_lnw=True)
                m3v = []
                sw = weight_round(Wf_h, slotB, m, stages_sw[1], cbufs_sw[1],
                                  m3_ps=psV, m3_w=[("ve", trk["last_kv"])],
                                  out_ve=m3v,
                                  dma_w=[("sc", last_apply[1]),
                                         ("ve", last_bn[1])])
                trk["last_kv"] = max(trk["last_kv"], m3v[0])
                swl.append(sw)
            mid_iter(1, m, trk, rdy_own, post_clamp=rounds)
        sc_swap = swl[-1]
        mid_half_end(1, trk)
        ld_he1 = c("ld")
        dbg("small_mid", small[:])
        dbg("lnTown", lnT[:])
        dbg("yA", yA[:])
        last_tp[0] = last_tp[1] = c("mm")
        part3(sc_swap, ld_he1, c("gp"),
              max(trk["last_sck"], trk["last_th"]))
        dbg("invcols", invcols[:])

        for s in ("st0", "st1", "st2"):
            fv = c(s)
            @E("sync", waits=[])
            def _(e, s=s, fv=fv): return e.wait_ge(SEM[s], fv)

        _OWN = {"vector": s_ve, "gpsimd": s_gp}

        def replay(key):
            own = _OWN.get(key)

            def body(eng):
                done = 0
                for fn, waits, inc, inc_n in p.streams[key]:
                    if own is not None and done > 0:
                        eng.wait_ge(own, done)
                    for (sem, v) in waits:
                        eng.wait_ge(sem, v)
                    ins = fn(eng)
                    if inc is not None:
                        ins.then_inc(inc, inc_n)
                        if inc is own:
                            done += inc_n
            return body

        block.sync(replay("sync"))
        block.tensor(replay("tensor"))
        block.vector(replay("vector"))
        block.scalar(replay("scalar"))
        block.gpsimd(replay("gpsimd"))

    return nc


_nc_cache = None


def kernel(**inputs):
    global _nc_cache
    x = np.ascontiguousarray(np.asarray(inputs["L0_attn_x"], dtype=np.float32))
    Wr = np.ascontiguousarray(np.asarray(inputs["L0_attn_Wr"], dtype=np.float32))
    Wk = np.ascontiguousarray(np.asarray(inputs["L0_attn_Wk"], dtype=np.float32))
    Wv = np.ascontiguousarray(np.asarray(inputs["L0_attn_Wv"], dtype=np.float32))
    Wo = np.ascontiguousarray(np.asarray(inputs["L0_attn_Wo"], dtype=np.float32))
    Wf = np.ascontiguousarray(np.asarray(inputs["L0_ffn_W"], dtype=np.float32))
    decay = np.asarray(inputs["L0_attn_decay"], dtype=np.float32)
    lnw = np.asarray(inputs["L0_attn_ln_w"], dtype=np.float32)
    bf = np.asarray(inputs["L0_ffn_b"], dtype=np.float32)
    ld = np.ascontiguousarray(np.asarray(inputs["L0_ffn_lora_down"], dtype=np.float32))
    lu = np.ascontiguousarray(np.asarray(inputs["L0_ffn_lora_up"], dtype=np.float32))
    da = np.ascontiguousarray(np.asarray(inputs["L0_ffn_down_aux"], dtype=np.float32))
    ua = np.ascontiguousarray(np.asarray(inputs["L0_ffn_up_aux"], dtype=np.float32))

    decay8 = np.ascontiguousarray(decay.reshape(8, 128).T)
    lnw8 = np.ascontiguousarray(lnw.reshape(8, 128).T)
    b_row = np.ascontiguousarray(bf.reshape(1, D))
    ident = np.eye(128, dtype=ml_dtypes.bfloat16)

    if _nc_cache is None:
        _nc_cache = build_nc()
    nc = _nc_cache

    in_maps = []
    for cid in range(8):
        b, h = cid // 2, cid % 2
        in_maps.append({
            "x_own": np.ascontiguousarray(x[b, h * HALF:(h + 1) * HALF]),
            "x_pre": np.ascontiguousarray(x[b, 0:HALF]),
            "Wr": Wr, "Wk": Wk, "Wv": Wv, "Wo": Wo, "Wf": Wf,
            "decay8": decay8, "lnw8": lnw8, "b_row": b_row,
            "lora_down": ld, "lora_up": lu, "down_aux": da, "up_aux": ua,
            "mask_col": np.full((128, 1), float(h), np.float32),
            "t0_col": np.full((128, 1), float(h * HALF), np.float32),
            "ident": ident,
        })

    res = run_bass_kernel_spmd(nc, in_maps, core_ids=list(range(8)))
    out = np.empty((B, T, D), np.float32)
    for cid in range(8):
        b, h = cid // 2, cid % 2
        out[b, h * HALF:(h + 1) * HALF] = res.results[cid]["out"]
    return out


# version 34
# speedup vs baseline: 1.0415x; 1.0002x over previous
import numpy as np
import ml_dtypes
import concourse.bass as bass
import concourse.mybir as mybir
from concourse.bass_utils import run_bass_kernel_spmd

F32 = mybir.dt.float32
BF16 = mybir.dt.bfloat16
AF = mybir.ActivationFunctionType
ALU = mybir.AluOpType
AX = mybir.AxisListType

B, T, D = 4, 4096, 1024
HALF = 2048
H = 1024
NCH = 8


class Prog:

    def __init__(self):
        self.streams = {k: [] for k in ("sync", "tensor", "vector", "scalar", "gpsimd")}
        self.counts = {}

    def emit(self, eng, fn, waits=(), inc=None, inc_n=1):
        waits = tuple((s, v) for (s, v) in waits if v > 0)
        self.streams[eng].append((fn, waits, inc, inc_n))
        if inc is not None:
            self.counts[inc] = self.counts.get(inc, 0) + inc_n


def build_nc(debug=False):
    nc = bass.Bass("TRN2", target_bir_lowering=False, debug=False)

    x_own = nc.dram_tensor("x_own", [HALF, D], F32, kind="ExternalInput").ap()
    x_pre = nc.dram_tensor("x_pre", [HALF, D], F32, kind="ExternalInput").ap()
    Wr_h = nc.dram_tensor("Wr", [D, D], F32, kind="ExternalInput").ap()
    Wk_h = nc.dram_tensor("Wk", [D, D], F32, kind="ExternalInput").ap()
    Wv_h = nc.dram_tensor("Wv", [D, D], F32, kind="ExternalInput").ap()
    Wo_h = nc.dram_tensor("Wo", [D, D], F32, kind="ExternalInput").ap()
    Wf_h = nc.dram_tensor("Wf", [D, D], F32, kind="ExternalInput").ap()
    decay8_h = nc.dram_tensor("decay8", [128, 8], F32, kind="ExternalInput").ap()
    lnw8_h = nc.dram_tensor("lnw8", [128, 8], F32, kind="ExternalInput").ap()
    b_row_h = nc.dram_tensor("b_row", [1, D], F32, kind="ExternalInput").ap()
    ld_h = nc.dram_tensor("lora_down", [4, 100], F32, kind="ExternalInput").ap()
    lu_h = nc.dram_tensor("lora_up", [50, 4], F32, kind="ExternalInput").ap()
    da_h = nc.dram_tensor("down_aux", [100, D], F32, kind="ExternalInput").ap()
    ua_h = nc.dram_tensor("up_aux", [D, 50], F32, kind="ExternalInput").ap()
    mask_h = nc.dram_tensor("mask_col", [128, 1], F32, kind="ExternalInput").ap()
    t0_h = nc.dram_tensor("t0_col", [128, 1], F32, kind="ExternalInput").ap()
    ident_h = nc.dram_tensor("ident", [128, 128], BF16, kind="ExternalInput").ap()
    out_h = nc.dram_tensor("out", [HALF, D], F32, kind="ExternalOutput").ap()

    import contextlib
    with contextlib.ExitStack() as _stk:
        lnT = _stk.enter_context(nc.sbuf_tensor("lnT", [128, NCH * HALF], BF16))
        yA = _stk.enter_context(nc.sbuf_tensor("yA", [128, NCH * HALF], BF16))
        slotA = _stk.enter_context(nc.sbuf_tensor("slotA", [128, NCH * 1024], BF16))
        slotB = _stk.enter_context(nc.sbuf_tensor("slotB", [128, NCH * 1024], BF16))
        slotC = _stk.enter_context(nc.sbuf_tensor("slotC", [128, NCH * 1024], BF16))
        k_m = _stk.enter_context(nc.sbuf_tensor("k_m", [128, H], F32))
        u_m = _stk.enter_context(nc.sbuf_tensor("u_m", [128, H], F32))
        csbuf = _stk.enter_context(nc.sbuf_tensor("csbuf", [128, H], F32))
        rbuf = _stk.enter_context(nc.sbuf_tensor("rbuf", [128, H], F32))
        k_m1 = _stk.enter_context(nc.sbuf_tensor("k_m1", [128, H], F32))
        u_m1 = _stk.enter_context(nc.sbuf_tensor("u_m1", [128, H], F32))
        csbuf1 = _stk.enter_context(nc.sbuf_tensor("csbuf1", [128, H], F32))
        rbuf1 = _stk.enter_context(nc.sbuf_tensor("rbuf1", [128, H], F32))
        y2A = _stk.enter_context(nc.sbuf_tensor("y2A", [128, NCH * H], BF16))
        castbuf = _stk.enter_context(nc.sbuf_tensor("castbuf", [128, 1024], BF16))
        castbuf2 = _stk.enter_context(nc.sbuf_tensor("castbuf2", [128, 1024], BF16))
        xt0 = _stk.enter_context(nc.sbuf_tensor("xt0", [128, D], F32))
        xt1 = _stk.enter_context(nc.sbuf_tensor("xt1", [128, D], F32))
        x1t0 = _stk.enter_context(nc.sbuf_tensor("x1t0", [128, D], F32))
        x1t1 = _stk.enter_context(nc.sbuf_tensor("x1t1", [128, D], F32))
        x1tc = _stk.enter_context(nc.sbuf_tensor("x1tc", [128, D], F32))
        xtp = x1tc
        wstage = _stk.enter_context(nc.sbuf_tensor("wstage", [128, D], F32))
        iota = _stk.enter_context(nc.sbuf_tensor("iota", [128, H], F32))
        ones_t = _stk.enter_context(nc.sbuf_tensor("ones_t", [128, H], BF16))
        onesc = _stk.enter_context(nc.sbuf_tensor("onesc", [128, 1], BF16))
        b_bcast = _stk.enter_context(nc.sbuf_tensor("b_bcast", [128, D], F32))
        ident_t = _stk.enter_context(nc.sbuf_tensor("ident_t", [128, 128], BF16))
        small = _stk.enter_context(nc.sbuf_tensor("small", [128, 64], F32))
        bias_own = _stk.enter_context(nc.sbuf_tensor("bias_own", [128, 16], F32))
        bias_ownN = _stk.enter_context(nc.sbuf_tensor("bias_ownN", [128, 16], F32))
        bias_pre = _stk.enter_context(nc.sbuf_tensor("bias_pre", [128, 16], F32))
        bn0 = _stk.enter_context(nc.sbuf_tensor("bn0", [128, 12], F32))
        bn0B = _stk.enter_context(nc.sbuf_tensor("bn0B", [128, 12], F32))
        mvA = _stk.enter_context(nc.sbuf_tensor("mvA", [128, 2], F32))
        mvB = _stk.enter_context(nc.sbuf_tensor("mvB", [128, 2], F32))
        colsA = _stk.enter_context(nc.sbuf_tensor("colsA", [128, 4], F32))
        colsB = _stk.enter_context(nc.sbuf_tensor("colsB", [128, 4], F32))
        srow = x1t0
        invcols = _stk.enter_context(nc.sbuf_tensor("invcols", [128, 32], F32))
        ldT = _stk.enter_context(nc.sbuf_tensor("ldT", [100, 4], BF16))
        luT = _stk.enter_context(nc.sbuf_tensor("luT", [4, 50], BF16))
        uaT = _stk.enter_context(nc.sbuf_tensor("uaT", [50, D], BF16))
        m1 = _stk.enter_context(nc.sbuf_tensor("m1", [4, D], BF16))
        m2 = _stk.enter_context(nc.sbuf_tensor("m2", [50, D], BF16))
        psK = _stk.enter_context(nc.psum_tensor("psK", [128, 1024], F32))
        psV = _stk.enter_context(nc.psum_tensor("psV", [128, 1024], F32))
        psR = _stk.enter_context(nc.psum_tensor("psR", [128, 1024], F32))
        psTb = _stk.enter_context(nc.psum_tensor("psTb", [128, 1024], BF16))
        psS = _stk.enter_context(nc.psum_tensor("psS", [1, 512], F32))
        s_ld = _stk.enter_context(nc.semaphore("s_ld"))
        s_st = _stk.enter_context(nc.semaphore("s_st"))
        s_mm = _stk.enter_context(nc.semaphore("s_mm"))
        s_ve = _stk.enter_context(nc.semaphore("s_ve"))
        s_sc = _stk.enter_context(nc.semaphore("s_sc"))
        s_gp = _stk.enter_context(nc.semaphore("s_gp"))
        s_x0 = _stk.enter_context(nc.semaphore("s_x0"))
        s_x1 = _stk.enter_context(nc.semaphore("s_x1"))
        s_x2 = _stk.enter_context(nc.semaphore("s_x2"))
        s_w0 = _stk.enter_context(nc.semaphore("s_w0"))
        s_w1 = _stk.enter_context(nc.semaphore("s_w1"))
        st0 = _stk.enter_context(nc.semaphore("st0"))
        st1 = _stk.enter_context(nc.semaphore("st1"))
        st2 = _stk.enter_context(nc.semaphore("st2"))
        block = _stk.enter_context(nc.Block())
        p = Prog()
        SEM = {"ld": s_ld, "st": s_st, "mm": s_mm, "ve": s_ve, "sc": s_sc,
               "gp": s_gp, "x0": s_x0, "x1": s_x1, "x2": s_x2,
               "w0": s_w0, "w1": s_w1,
               "st0": st0, "st1": st1, "st2": st2}

        def c(name):
            return p.counts.get(SEM[name], 0)

        def E(eng, semname=None, waits=(), inc_n=1):
            def run(fn):
                p.emit(eng, fn,
                       waits=[(SEM[s], v) for (s, v) in waits],
                       inc=(SEM[semname] if semname else None), inc_n=inc_n)
            return run

        def dbg(name, ap):
            if not debug:
                return
            t = nc.dram_tensor("dbg_" + name, list(ap.shape), ap.dtype,
                               kind="ExternalOutput").ap()
            w = [(SEM[s], p.counts.get(SEM[s], 0)) for s in
                 ("mm", "ve", "sc", "gp")]
            p.emit("sync", (lambda e, t=t, ap=ap: e.dma_start(out=t, in_=ap)),
                   waits=w, inc=SEM["ld"], inc_n=16)
            ldv = c("ld")
            for engn in ("sync", "tensor", "vector", "scalar", "gpsimd"):
                p.emit(engn, (lambda e, v=ldv: e.wait_ge(s_ld, v)))

        def wtile(slot, kc, mc):
            return slot[:, kc * 1024 + mc * 128: kc * 1024 + (mc + 1) * 128]

        def wmov(slot, kc, j):
            return slot[:, kc * 1024 + j * 512: kc * 1024 + (j + 1) * 512]

        def lnT_mt(m, t0, n):
            return lnT[:, m * HALF + t0: m * HALF + t0 + n]

        def yA_mt(m, t0, n):
            return yA[:, m * HALF + t0: m * HALF + t0 + n]

        lnT3 = lnT.rearrange("p (m t) -> p m t", m=8)
        psTb3 = psTb.rearrange("p (m t) -> p m t", m=8)

        @E("sync", "ld", inc_n=16)
        def _(e): return e.dma_start(out=small[:, 8:16], in_=decay8_h)
        @E("sync", "ld", inc_n=16)
        def _(e): return e.dma_start(out=small[:, 0:1], in_=mask_h)
        @E("sync", "ld", inc_n=16)
        def _(e): return e.dma_start(out=small[:, 1:2], in_=t0_h)
        @E("sync", "ld", inc_n=16)
        def _(e): return e.dma_start(out=ident_t[:], in_=ident_h)
        @E("sync", "ld", inc_n=16)
        def _(e): return e.dma_start(out=small[:, 56:64], in_=lnw8_h)
        ld_base = c("ld")

        @E("gpsimd", "gp")
        def _(e): return e.iota(iota[:], pattern=[[1, H]], base=0,
                                channel_multiplier=0,
                                allow_small_or_imprecise_dtypes=True)
        @E("gpsimd", "gp")
        def _(e): return e.memset(ones_t[:], 1.0)
        @E("gpsimd", "gp")
        def _(e): return e.memset(onesc[:], 1.0)
        @E("gpsimd", "gp")
        def _(e): return e.memset(small[:, 32:40], 0.0)
        @E("gpsimd", "gp")
        def _(e): return e.memset(small[:, 3:4], 1e-5)
        @E("gpsimd", "gp")
        def _(e): return e.memset(small[:, 4:5], 1e-6)
        @E("gpsimd", "gp")
        def _(e): return e.memset(small[:, 5:6], 4e-6)
        gp_eps = c("gp")

        @E("scalar", "sc", waits=[("ld", ld_base)])
        def _(e): return e.activation(out=small[:, 40:48], in_=small[:, 8:16],
                                      func=AF.Tanh, scale=0.5)
        @E("vector", "ve", waits=[("sc", c("sc"))])
        def _(e):
            return e.tensor_scalar(small[:, 8:16], small[:, 40:48], 0.5, 0.5,
                                   ALU.mult, ALU.add)
        @E("vector", "ve", waits=[("ve", c("ve"))])
        def _(e): return e.tensor_scalar_max(small[:, 40:48], small[:, 8:16], 1e-7)
        ve_dec = c("ve")
        @E("scalar", "sc", waits=[("ve", ve_dec)])
        def _(e): return e.activation(out=small[:, 24:32], in_=small[:, 40:48],
                                      func=AF.Ln)
        sc_lnd = c("sc")
        @E("vector", "ve", waits=[("sc", sc_lnd)])
        def _(e): return e.tensor_scalar_mul(small[:, 16:24], small[:, 24:32], -1.0)
        @E("vector", "ve", waits=[("ve", c("ve"))])
        def _(e): return e.tensor_scalar_add(small[:, 2:3], small[:, 1:2], 1024.0)
        for h in range(2):
            @E("vector", "ve", waits=[("ve", c("ve"))])
            def _(e, h=h):
                return e.tensor_scalar(bias_own[:, h * 8:(h + 1) * 8],
                                       small[:, 16:24], small[:, 1 + h:2 + h],
                                       None, ALU.mult)
            @E("vector", "ve", waits=[("ve", c("ve"))])
            def _(e, h=h):
                return e.tensor_scalar_mul(bias_pre[:, h * 8:(h + 1) * 8],
                                           small[:, 16:24], float(h * 1024))
        @E("vector", "ve", waits=[("ve", c("ve"))])
        def _(e): return e.tensor_scalar_mul(bias_ownN[:], bias_own[:], -1.0)
        ve_bias = c("ve")

        wstate = {"cast": {}, "tp": {}}
        psTbg = {"sc": 0}

        def weight_round(src_h, slot, rc, stage, cbuf,
                         scale_lnw=False, m3_ps=None, m3_w=(), out_ve=None,
                         dma_w=()):
            sid, cid = id(stage), id(cbuf)
            ws = "w0" if stage is wstage else "w1"
            @E("sync", ws, inc_n=16,
               waits=[("ve", wstate["cast"].get(sid, 0))] + list(dma_w))
            def _(e, rc=rc, stage=stage):
                return e.dma_start(out=stage[:],
                                   in_=src_h[rc * 128:(rc + 1) * 128, :])
            ldv = c(ws)
            if m3_ps is not None:
                for j in range(2):
                    @E("tensor", "mm",
                       waits=(([(ws, ldv), ("sc", c("sc"))] + list(m3_w))
                              if j == 0 else ()))
                    def _(e, rc=rc, j=j, m3_ps=m3_ps):
                        return e.matmul(m3_ps[:, j * 512:(j + 1) * 512],
                                        uaT[:, rc * 128:(rc + 1) * 128],
                                        m2[:, j * 512:(j + 1) * 512],
                                        start=True, stop=True)
                @E("vector", "ve", waits=[("mm", c("mm")), (ws, ldv)])
                def _(e, stage=stage, m3_ps=m3_ps):
                    return e.tensor_add(stage[:], stage[:], m3_ps[:])
                if out_ve is not None:
                    out_ve.append(c("ve"))
                cast_w = [("ve", c("ve")), ("mm", wstate["tp"].get(cid, 0))]
            else:
                cast_w = [(ws, ldv), ("mm", wstate["tp"].get(cid, 0))]
            @E("vector", "ve", waits=cast_w)
            def _(e, stage=stage, cbuf=cbuf):
                return e.tensor_copy(cbuf[:], stage[:])
            wstate["cast"][sid] = vecast = c("ve")
            for kc in range(8):
                @E("tensor", "mm",
                   waits=([("ve", vecast), ("sc", psTbg["sc"])] if kc == 0 else ()))
                def _(e, kc=kc, cbuf=cbuf):
                    return e.transpose(psTb[:, kc * 128:(kc + 1) * 128],
                                       cbuf[:, kc * 128:(kc + 1) * 128],
                                       ident_t[:])
            wstate["tp"][cid] = mmt = c("mm")
            if scale_lnw:
                for kc in range(8):
                    @E("scalar", "sc", waits=([("mm", mmt)] if kc == 0 else ()))
                    def _(e, kc=kc, rc=rc, slot=slot):
                        return e.activation(
                            out=slot[:, kc * 1024 + rc * 128:
                                     kc * 1024 + (rc + 1) * 128],
                            in_=psTb[:, kc * 128:(kc + 1) * 128],
                            func=AF.Copy, scale=small[:, 56 + kc:57 + kc])
            else:
                slot3 = slot.rearrange("p (kc e) -> p kc e", kc=8)
                @E("scalar", "sc", waits=[("mm", mmt)])
                def _(e, rc=rc, slot3=slot3):
                    return e.copy(out=slot3[:, :, rc * 128:(rc + 1) * 128],
                                  in_=psTb3[:])
            psTbg["sc"] = c("sc")
            return c("sc")

        last_apply = {0: 0, 1: 0}
        last_bn = {0: 0, 1: 0}
        last_tp = {0: 0, 1: 0}
        last_ln = {0: 0, 1: 0}
        last_applyx = {0: 0, 1: 0, 2: 0}
        last_bnx = {0: 0, 1: 0, 2: 0}

        def ln_chunk_p1(src_h, i, tag="", use_castbuf=False):
            par = i % 2
            slot = i % 3
            xt = (xt0, xt1, xtp)[slot]
            mv = mvA if par == 0 else mvB
            cols = colsA if par == 0 else colsB
            if use_castbuf:
                lnc = castbuf[:] if par == 0 else castbuf2[:]
            else:
                lnc = yA[:, 1024:2048] if par == 0 else yA[:, 2048:3072]
            xs = ("x0", "x1", "x2")[slot]
            @E("sync", xs, inc_n=16,
               waits=[("sc", last_applyx[slot]), ("ve", last_bnx[slot])])
            def _(e, i=i, xt=xt):
                return e.dma_start(out=xt[:], in_=src_h[i * 128:(i + 1) * 128, :])
            ldx = c(xs)
            bnp = bn0 if par == 0 else bn0B
            @E("vector", "ve", waits=[(xs, ldx), ("sc", last_ln[par])])
            def _(e, xt=xt, bnp=bnp): return e.bn_stats(bnp[:, 0:6], xt[:, 0:512])
            @E("vector", "ve")
            def _(e, xt=xt, bnp=bnp): return e.bn_stats(bnp[:, 6:12], xt[:, 512:1024])
            @E("vector", "ve", waits=[("ve", c("ve"))])
            def _(e, mv=mv, bnp=bnp): return e.bn_aggr(mv[:], bnp[:])
            last_bn[par] = last_bnx[slot] = ve_mv = c("ve")
            if i == 0 and tag:
                dbg(tag + "_x0", xt[:])
                dbg(tag + "_mv0", mv[:])
            ctx = dict(i=i, par=par, lnc=lnc)

            def finish(par=par, slot=slot, mv=mv, cols=cols, xt=xt, lnc=lnc,
                       ve_mv=ve_mv, ctx=ctx):
                @E("scalar", "sc", waits=[("ve", ve_mv), ("gp", gp_eps)])
                def _(e, mv=mv, cols=cols):
                    return e.activation(out=cols[:, 0:1], in_=mv[:, 1:2],
                                        func=AF.Ln, bias=small[:, 3:4])
                last_ln[par] = c("sc")
                @E("scalar", "sc", waits=[("sc", c("sc"))])
                def _(e, cols=cols):
                    return e.activation(out=cols[:, 1:2], in_=cols[:, 0:1],
                                        func=AF.Exp, scale=-0.5)
                sc_inv = c("sc")
                @E("vector", "ve", waits=[("sc", sc_inv)])
                def _(e, mv=mv, cols=cols):
                    return e.scalar_tensor_tensor(cols[:, 2:3], mv[:, 0:1], -1.0,
                                                  cols[:, 1:2], ALU.mult, ALU.mult)
                ve_negmi = c("ve")
                @E("scalar", "sc", waits=[("ve", ve_negmi), ("mm", last_tp[par])])
                def _(e, xt=xt, cols=cols, lnc=lnc):
                    return e.activation(out=lnc, in_=xt[:], func=AF.Identity,
                                        bias=cols[:, 2:3], scale=cols[:, 1:2])
                last_apply[par] = last_applyx[slot] = ctx["sc_apply"] = c("sc")
            ctx["finish"] = finish
            return ctx

        def ln_chunk_p2(ctx, ready):
            i, par, lnc = ctx["i"], ctx["par"], ctx["lnc"]
            for j in range(8):
                @E("tensor", "mm",
                   waits=([("sc", max(ctx["sc_apply"], psTbg["sc"]))]
                          if j == 0 else ()))
                def _(e, j=j, lnc=lnc):
                    return e.transpose(psTb[:, j * 128:(j + 1) * 128],
                                       lnc[:, j * 128:(j + 1) * 128], ident_t[:])
            last_tp[par] = mmt = c("mm")
            @E("scalar", "sc", waits=[("mm", mmt)])
            def _(e, i=i):
                return e.copy(out=lnT3[:, :, i * 128:(i + 1) * 128], in_=psTb3[:])
            psTbg["sc"] = c("sc")
            ready.append(c("sc"))

        def proj(slot, m, h, ps, w0=(), w1=()):
            cnt = 0
            for kc in range(8):
                for j in range(2):
                    w = list(w0) if cnt == 0 else (list(w1) if cnt == 1 else [])
                    cnt += 1
                    @E("tensor", "mm", waits=w)
                    def _(e, kc=kc, j=j, m=m, h=h, ps=ps, slot=slot):
                        return e.matmul(ps[:, j * 512:(j + 1) * 512],
                                        wtile(slot, kc, m),
                                        lnT_mt(kc, h * H + j * 512, 512),
                                        start=(kc == 0), stop=(kc == 7))
            return c("mm")

        BUFS = [(k_m, u_m, csbuf, rbuf), (k_m1, u_m1, csbuf1, rbuf1)]

        def prefix_iter(h, m, trk, lnready):
            par = (h * 8 + m) % 2
            kB, uB, cB, rB = BUFS[par]
            lnw = [("sc", max(trk["last_sck"], lnready[h * 8 + 3], trk["wkd"]))]
            lnw1 = [("sc", max(lnready[h * 8 + 7], trk["wvd"]))]
            mmk = proj(slotA, m, h, psK, w0=lnw, w1=lnw1)
            mmv = proj(slotB, m, h, psV, w0=[("ve", trk["last_kv"])])
            @E("scalar", "sc", waits=[("mm", mmk), ("gp", trk["gpu"][par]),
                                      ("ve", trk["red"][par])])
            def _(e, kB=kB): return e.copy(out=kB[:], in_=psK[:])
            trk["last_sck"] = sck = c("sc")
            @E("vector", "ve", waits=[("sc", sck), ("mm", mmv),
                                      ("gp", trk["gpu"][par])])
            def _(e, kB=kB, uB=uB): return e.tensor_mul(uB[:], kB[:], psV[:])
            trk["last_kv"] = c("ve")
            @E("scalar", "sc", waits=[("ve", ve_bias),
                                      ("ve", trk["min"][par])])
            def _(e, m=m, h=h, cB=cB):
                return e.activation(out=cB[:], in_=iota[:],
                                    func=AF.Identity,
                                    scale=small[:, 16 + m:17 + m],
                                    bias=bias_pre[:, h * 8 + m:h * 8 + m + 1])
            sc_c = c("sc")
            @E("vector", "ve", waits=[("sc", sc_c), ("gp", trk["gpu"][par])])
            def _(e, cB=cB, rB=rB):
                return e.tensor_scalar_min(rB[:], cB[:], 23.025850929940457)
            trk["min"][par] = ve_c = c("ve")
            @E("scalar", "sc", waits=[("ve", ve_c)])
            def _(e, cB=cB, rB=rB):
                return e.activation(out=cB[:], in_=rB[:], func=AF.Exp)
            sc_ce = c("sc")
            @E("gpsimd", "gp", waits=[("sc", sc_ce)])
            def _(e, kB=kB, uB=uB, cB=cB):
                return e.tensor_mul(kB[:], uB[:], cB[:])
            trk["gpu"][par] = gp_u = c("gp")
            @E("vector", "ve", waits=[("gp", gp_u)])
            def _(e, kB=kB, uB=uB):
                return e.tensor_reduce(uB[:, 0:1], kB[:], AX.X, ALU.add)
            dst = 32 + m if h == 0 else 8 + m
            @E("vector", "ve", waits=[("ve", c("ve"))])
            def _(e, dst=dst, uB=uB):
                return e.tensor_copy(small[:, dst:dst + 1], uB[:, 0:1])
            trk["red"][par] = c("ve")

        def carry_combine():
            @E("vector", "ve", waits=[("gp", c("gp")), ("ve", c("ve"))])
            def _(e):
                return e.tensor_add(small[:, 48:56], small[:, 32:40], small[:, 8:16])
            @E("vector", "ve", waits=[("ve", c("ve"))])
            def _(e):
                return e.tensor_scalar(small[:, 40:48], small[:, 48:56],
                                       small[:, 0:1], None, ALU.mult)

        last_srow_dma = {0: 0}
        sc_inv_h = {0: 0, 1: 0}

        def mid_iter(h, m, trk, lnready, post_clamp=None):
            par = (h * 8 + m) % 2
            kB, uB, cB, rB = BUFS[par]
            lnw = [("sc", max(trk["last_sck"], lnready[h * 8 + 3]))]
            lnw1 = [("sc", lnready[h * 8 + 7])]
            mmk = proj(slotA, m, h, psK, w0=lnw, w1=lnw1)
            mmv = proj(slotB, m, h, psV, w0=[("ve", trk["last_kv"])])
            mmr = proj(slotC, m, h, psR,
                       w0=[("sc", max(trk["last_th"], trk["wrd"]))])
            @E("scalar", "sc", waits=[("mm", mmk), ("ve", trk["yst"][par]),
                                      ("gp", trk["gpu"][par])])
            def _(e, kB=kB): return e.copy(out=kB[:], in_=psK[:])
            trk["last_sck"] = sck = c("sc")
            @E("vector", "ve", waits=[("sc", sck), ("mm", mmv),
                                      ("gp", trk["gpu"][par])])
            def _(e, kB=kB, uB=uB): return e.tensor_mul(uB[:], kB[:], psV[:])
            trk["last_kv"] = c("ve")
            @E("scalar", "sc", waits=[("ve", trk["yst"][par]),
                                      ("gp", trk["gpu"][par])])
            def _(e, m=m, h=h, cB=cB):
                return e.activation(out=cB[:], in_=iota[:], func=AF.Identity,
                                    scale=small[:, 16 + m:17 + m],
                                    bias=bias_own[:, h * 8 + m:h * 8 + m + 1])
            sc_c2 = c("sc")
            @E("scalar", "sc", waits=[("ve", trk["state"][par])])
            def _(e, m=m, h=h, rB=rB):
                return e.activation(out=rB[:], in_=iota[:], func=AF.Exp,
                                    scale=small[:, 24 + m:25 + m],
                                    bias=bias_ownN[:, h * 8 + m:h * 8 + m + 1])
            @E("vector", "ve", waits=[("sc", sc_c2), ("ve", c("ve"))])
            def _(e, kB=kB, cB=cB):
                return e.tensor_scalar_min(kB[:], cB[:], 23.025850929940457)
            ve_c = c("ve")
            @E("scalar", "sc", waits=[("ve", ve_c)])
            def _(e, kB=kB, cB=cB):
                return e.activation(out=cB[:], in_=kB[:], func=AF.Exp)
            sc_ce = c("sc")
            if post_clamp is not None:
                post_clamp()
            @E("gpsimd", "gp", waits=[("sc", sc_ce)])
            def _(e, kB=kB, uB=uB, cB=cB):
                return e.tensor_mul(kB[:], uB[:], cB[:])
            trk["gpu"][par] = gp_u = c("gp")
            init = (small[:, 40 + m:41 + m] if h == 0
                    else small[:, 48 + m:49 + m])
            @E("vector", "ve", waits=[("gp", gp_u), ("ve", c("ve"))])
            def _(e, init=init, uB=uB, kB=kB):
                return e.tensor_tensor_scan(out=uB[:], data0=ones_t[:],
                                            data1=kB[:], initial=init,
                                            op0=ALU.mult, op1=ALU.add)
            ve_scan = c("ve")
            if h == 0:
                @E("vector", "ve", waits=[("ve", c("ve"))])
                def _(e, m=m, uB=uB):
                    return e.tensor_copy(small[:, 48 + m:49 + m], uB[:, H - 1:H])
            @E("vector", "ve", waits=[("gp", c("gp"))])
            def _(e, kB=kB, uB=uB, rB=rB):
                return e.tensor_mul(kB[:], uB[:], rB[:])
            trk["state"][par] = c("ve")
            @E("scalar", "sc", waits=[("mm", mmr), ("ve", ve_scan)])
            def _(e, cB=cB): return e.activation(out=cB[:], in_=psR[:],
                                                 func=AF.Tanh, scale=0.5)
            trk["last_th"] = sc_r = c("sc")
            @E("vector", "ve", waits=[("sc", sc_r)])
            def _(e, m=m, h=h, kB=kB, cB=cB):
                return e.scalar_tensor_tensor(yA_mt(m, h * H, H), cB[:],
                                              1.0, kB[:], ALU.add, ALU.mult)
            trk["yst"][par] = c("ve")
            y2w = [("mm", trk["mm_stats"])] if h == 1 else []
            @E("vector", "ve", waits=y2w)
            def _(e, m=m, h=h):
                return e.tensor_mul(y2A[:, m * H:(m + 1) * H],
                                    yA_mt(m, h * H, H), yA_mt(m, h * H, H))
            trk["y2"] = c("ve")
            trk["y2l"][m] = c("ve")
            if m == 7:
                trk["y2_h" + str(h)] = c("ve")

        def mid_half_end(h, trk):
            ve_y2 = trk["y2"]
            sc_evs = []
            for j in range(2):
                for mc in range(8):
                    w = [("ve", trk["y2l"][mc])]
                    if mc == 0 and sc_evs:
                        w.append(("sc", sc_evs[-1]))
                    @E("tensor", "mm", waits=w)
                    def _(e, j=j, mc=mc):
                        return e.matmul(psS[:], onesc[:],
                                        y2A[:, mc * H + j * 512:
                                            mc * H + (j + 1) * 512],
                                        start=(mc == 0), stop=(mc == 7))
                mm_st = c("mm")
                @E("scalar", "sc", waits=[("mm", mm_st),
                                          ("ld", last_srow_dma[0])])
                def _(e, j=j):
                    return e.copy(out=srow[0:1, j * 512:(j + 1) * 512], in_=psS[:])
                sc_evs.append(c("sc"))
            trk["mm_stats"] = c("mm")
            sc_sr = sc_evs[-1]
            for i2 in range(8):
                @E("sync", "ld", inc_n=16,
                   waits=([("sc", sc_sr)] if i2 == 0 else ()))
                def _(e, i2=i2, h=h):
                    return e.dma_start(out=invcols[:, 16 * h + i2:16 * h + i2 + 1],
                                       in_=srow[0:1, i2 * 128:(i2 + 1) * 128])
            last_srow_dma[0] = ld_s = c("ld")
            @E("scalar", "sc", waits=[("ld", ld_s)])
            def _(e, h=h):
                return e.activation(out=invcols[:, 16 * h + 8:16 * h + 16],
                                    in_=invcols[:, 16 * h:16 * h + 8],
                                    func=AF.Ln, scale=1.0 / 1024.0,
                                    bias=small[:, 5:6])
            @E("scalar", "sc", waits=[("sc", c("sc"))])
            def _(e, h=h):
                return e.activation(out=invcols[:, 16 * h:16 * h + 8],
                                    in_=invcols[:, 16 * h + 8:16 * h + 16],
                                    func=AF.Exp, scale=-0.5)
            sc_inv_h[h] = c("sc")

        def part3(sc_swap, ld_he1, gp_all, sc_mid):
            @E("sync", "ld", inc_n=16, waits=[("ld", c("ld"))])
            def _(e): return e.dma_start(out=b_bcast[:],
                                         in_=b_row_h.broadcast_to((128, D)))
            ld_b = c("ld")

            last_x1 = {0: 0, 1: 0}
            last_dma = {0: 0, 1: 0}
            last_stt = {0: 0, 1: 0}
            st_l1 = {"l1": 0, "oa": 0}
            oring = [k_m, u_m, csbuf]
            oring_st = [0, 0, 0]
            oabuf = [rbuf, k_m1]
            last_sttx = {0: 0, 1: 0}
            ld_pre = {}
            pendB, pendC = [], []

            def stageB(ctx, mid_emit=None):
                ig, h, i = ctx["ig"], ctx["h"], ctx["i"]
                cb = ctx["cb"]
                for j in range(8):
                    @E("tensor", "mm",
                       waits=([("sc", max(ctx["sc_ap"], psTbg["sc"]))]
                              if j == 0 else ()))
                    def _(e, j=j, cb=cb):
                        return e.transpose(psTb[:, j * 128:(j + 1) * 128],
                                           cb[:, j * 128:(j + 1) * 128],
                                           ident_t[:])
                last_tp[ig % 2] = mmt = c("mm")
                @E("scalar", "sc", waits=[("mm", mmt)])
                def _(e, ig=ig):
                    return e.copy(out=lnT3[:, :, ig * 128:(ig + 1) * 128],
                                  in_=psTb3[:])
                psTbg["sc"] = st_l1["l1"] = sc_l1 = c("sc")
                if mid_emit is not None:
                    mid_emit()
                first = True
                for m in range(8):
                    for j in range(2):
                        w = [("sc", sc_l1), ("ve", st_l1["oa"])] if first else []
                        first = False
                        @E("tensor", "mm", waits=w)
                        def _(e, m=m, j=j, h=h, i=i):
                            return e.matmul(psV[:, j * 512:(j + 1) * 512],
                                            lnT_mt(m, h * H + i * 128, 128),
                                            wmov(slotB, m, j),
                                            start=(m == 0), stop=(m == 7))
                ctx["mm_ffn"] = c("mm")
                pendC.append(ctx)

            def stageC(ctx):
                ig, par = ctx["ig"], ctx["par"]
                x1t, oa, cols = ctx["x1t"], ctx["oa"], ctx["cols"]
                ot = oring[ig % 3]
                @E("vector", "ve", waits=[("mm", ctx["mm_ffn"]),
                                          ("sc", ctx["sc_inv"]),
                                          ("gp", gp_all)])
                def _(e, x1t=x1t, oa=oa, cols=cols):
                    return e.scalar_tensor_tensor(oa[:], psV[:], cols[:, 1:2],
                                                  x1t[:], ALU.mult, ALU.add)
                st_l1["oa"] = c("ve")
                @E("vector", "ve", waits=[("st" + str(ig % 3), oring_st[ig % 3]),
                                          ("ld", ld_b), ("gp", gp_all)])
                def _(e, ot=ot, oa=oa):
                    return e.tensor_add(ot[:], oa[:], b_bcast[:])
                last_x1[par] = ve_out = c("ve")
                sts = "st" + str(ig % 3)
                @E("sync", sts, inc_n=16, waits=[("ve", ve_out)])
                def _(e, ig=ig, ot=ot):
                    return e.dma_start(out=out_h[ig * 128:(ig + 1) * 128, :],
                                       in_=ot[:])
                oring_st[ig % 3] = c(sts)

            wo_mm = {}

            def emit_wo(ig):
                if ig > 15:
                    return
                h, i = ig // 8, ig % 8
                psWo = psK if ig % 2 == 0 else psR
                wpar = ig % 2
                first = True
                for m in range(8):
                    for j in range(2):
                        w = ([("ve", last_stt[wpar]), ("sc", sc_mid),
                              ("sc", sc_swap)] if first else [])
                        first = False
                        @E("tensor", "mm", waits=w)
                        def _(e, m=m, j=j, h=h, i=i, psWo=psWo):
                            return e.matmul(psWo[:, j * 512:(j + 1) * 512],
                                            yA_mt(m, h * H + i * 128, 128),
                                            wmov(slotA, m, j),
                                            start=(m == 0), stop=(m == 7))
                wo_mm[ig] = c("mm")

            x1ring = [x1t0, x1t1, x1tc]
            last_apply_r = {0: 0, 1: 0, 2: last_applyx[2]}
            for h in range(2):
                for i in range(8):
                    ig = h * 8 + i
                    par = i % 2
                    xt = xt0 if par == 0 else xt1
                    x1t = x1ring[ig % 3]
                    mv = mvA if par == 0 else mvB
                    cols = colsA if par == 0 else colsB
                    cb = castbuf if ig % 2 == 0 else castbuf2
                    psWo = psK if ig % 2 == 0 else psR
                    wpar = ig % 2
                    if ig == 0:
                        emit_wo(0)
                        emit_wo(1)
                    else:
                        stageB(pendB.pop(0), mid_emit=lambda ig=ig: emit_wo(ig + 1))
                    mm_wo = wo_mm[ig]
                    for igp in ([ig, ig + 1] if ig == 0 else [ig + 1]):
                        if igp > 15:
                            continue
                        parp = igp % 8 % 2
                        xtp = xt0 if parp == 0 else xt1
                        xsp = "x0" if parp == 0 else "x1"
                        @E("sync", xsp, inc_n=16,
                           waits=[("ve", last_sttx[parp]),
                                  ("sc", sc_swap)])
                        def _(e, igp=igp, xtp=xtp):
                            return e.dma_start(
                                out=xtp[:],
                                in_=x_own[igp * 128:(igp + 1) * 128, :])
                        ld_pre[igp] = (xsp, c(xsp))
                    ld_x = ld_pre[ig]
                    @E("vector", "ve",
                       waits=[("mm", mm_wo), (ld_x[0], ld_x[1]),
                              ("ld", ld_he1), ("sc", sc_inv_h[h]),
                              ("sc", last_apply_r[ig % 3])])
                    def _(e, h=h, i=i, xt=xt, x1t=x1t, psWo=psWo):
                        return e.scalar_tensor_tensor(
                            x1t[:], psWo[:], invcols[:, 16 * h + i:16 * h + i + 1],
                            xt[:], ALU.mult, ALU.add)
                    last_stt[wpar] = last_sttx[par] = c("ve")
                    @E("vector", "ve", waits=[("sc", last_ln[par])])
                    def _(e, x1t=x1t): return e.bn_stats(bn0[:, 0:6], x1t[:, 0:512])
                    @E("vector", "ve")
                    def _(e, x1t=x1t): return e.bn_stats(bn0[:, 6:12], x1t[:, 512:1024])
                    @E("vector", "ve", waits=[("ve", c("ve"))])
                    def _(e, mv=mv): return e.bn_aggr(mv[:], bn0[:])
                    last_bn[par] = ve_mv = c("ve")
                    @E("vector", "ve", waits=[("ve", ve_mv)])
                    def _(e, mv=mv, cols=cols):
                        return e.tensor_scalar_mul(cols[:, 2:3], mv[:, 0:1], -1.0)
                    ve_negm = c("ve")
                    @E("scalar", "sc", waits=[("ve", ve_negm),
                                              ("mm", last_tp[ig % 2])])
                    def _(e, x1t=x1t, cols=cols, cb=cb):
                        return e.activation(out=cb[:], in_=x1t[:],
                                            func=AF.Identity,
                                            bias=cols[:, 2:3], scale=1.0)
                    last_apply_r[ig % 3] = sc_ap = c("sc")
                    @E("scalar", "sc", waits=[("ve", ve_mv)])
                    def _(e, mv=mv, cols=cols):
                        return e.activation(out=cols[:, 0:1], in_=mv[:, 1:2],
                                            func=AF.Ln, bias=small[:, 3:4])
                    last_ln[par] = c("sc")
                    @E("scalar", "sc", waits=[("sc", c("sc"))])
                    def _(e, cols=cols):
                        return e.activation(out=cols[:, 1:2], in_=cols[:, 0:1],
                                            func=AF.Exp, scale=-0.5)
                    sc_inv_i = c("sc")
                    pendB.append(dict(ig=ig, h=h, i=i, par=par, x1t=x1t,
                                      oa=oabuf[ig % 2], cb=cb, sc_ap=sc_ap,
                                      sc_inv=sc_inv_i, cols=cols))
                    if pendC:
                        stageC(pendC.pop(0))
            while pendB:
                stageB(pendB.pop(0))
                if pendC:
                    stageC(pendC.pop(0))
            while pendC:
                stageC(pendC.pop(0))

        dbg("small_pro", small[:])
        rdy_pre = []
        rounds = ([(Wk_h, slotA, rc) for rc in range(8)]
                  + [(Wv_h, slotB, rc) for rc in range(8)]
                  + [(Wr_h, slotC, rc) for rc in range(8)])
        slot_done = {}
        ri = 0
        stages_pro = (wstage, k_m1)
        cbufs_pro = (castbuf, y2A[:, 0:1024])
        lnq = []
        pend1 = []

        def push_chunk(ctx, rdy):
            pend1.append(ctx)
            if len(pend1) == 2:
                pc = pend1.pop(0)
                pc["finish"]()
                lnq.append(pc)
            if len(lnq) == 2:
                ln_chunk_p2(lnq.pop(0), rdy)

        def flush_chunks(rdy):
            while pend1:
                pc = pend1.pop(0)
                pc["finish"]()
                lnq.append(pc)
                if len(lnq) == 2:
                    ln_chunk_p2(lnq.pop(0), rdy)
            while lnq:
                ln_chunk_p2(lnq.pop(0), rdy)

        for s in range(16):
            push_chunk(ln_chunk_p1(x_pre, s), rdy_pre)
            for _ in range(2):
                if ri < len(rounds):
                    srch, slot, rc = rounds[ri]
                    slot_done[id(slot)] = weight_round(
                        srch, slot, rc, stages_pro[ri % 2], cbufs_pro[ri % 2])
                    ri += 1
        while ri < len(rounds):
            srch, slot, rc = rounds[ri]
            slot_done[id(slot)] = weight_round(
                srch, slot, rc, stages_pro[ri % 2], cbufs_pro[ri % 2])
            ri += 1
        flush_chunks(rdy_pre)
        dbg("lnTpre", lnT[:])

        @E("sync", "ld", inc_n=16,
           waits=[("ve", wstate["cast"].get(id(wstage), 0)),
                  ("ld", c("ld")), ("w0", c("w0")), ("w1", c("w1"))])
        def _(e): return e.dma_start(out=wstage[0:4, 0:100], in_=ld_h)
        @E("vector", "ve", waits=[("ld", c("ld")),
                                  ("mm", wstate["tp"].get(id(castbuf), 0))])
        def _(e): return e.tensor_copy(castbuf[0:4, 0:100], wstage[0:4, 0:100])
        @E("tensor", "mm", waits=[("ve", c("ve")), ("sc", c("sc"))])
        def _(e): return e.transpose(psTb[0:100, 0:4], castbuf[0:4, 0:100],
                                     ident_t[0:4, 0:4])
        @E("scalar", "sc", waits=[("mm", c("mm"))])
        def _(e): return e.copy(out=ldT[:], in_=psTb[0:100, 0:4])

        @E("sync", "ld", inc_n=16, waits=[("ve", c("ve")), ("ld", c("ld"))])
        def _(e): return e.dma_start(out=wstage[0:50, 0:4], in_=lu_h)
        @E("vector", "ve", waits=[("ld", c("ld")), ("sc", c("sc"))])
        def _(e): return e.tensor_copy(castbuf[0:50, 0:4], wstage[0:50, 0:4])
        @E("tensor", "mm", waits=[("ve", c("ve")), ("sc", c("sc"))])
        def _(e): return e.transpose(psTb[0:4, 0:50], castbuf[0:50, 0:4],
                                     ident_t[0:50, 0:50])
        @E("scalar", "sc", waits=[("mm", c("mm"))])
        def _(e): return e.copy(out=luT[:], in_=psTb[0:4, 0:50])

        for rc in range(8):
            @E("sync", "ld", inc_n=16, waits=[("ve", c("ve")), ("sc", c("sc")),
                                              ("ld", c("ld"))])
            def _(e, rc=rc):
                return e.dma_start(out=wstage[:, 0:50],
                                   in_=ua_h[rc * 128:(rc + 1) * 128, :])
            @E("vector", "ve", waits=[("ld", c("ld")), ("mm", c("mm"))])
            def _(e): return e.tensor_copy(castbuf[:, 0:50], wstage[:, 0:50])
            @E("tensor", "mm", waits=[("ve", c("ve")), ("sc", c("sc"))])
            def _(e): return e.transpose(psTb[0:50, 0:128], castbuf[:, 0:50],
                                         ident_t[:])
            @E("scalar", "sc", waits=[("mm", c("mm"))])
            def _(e, rc=rc):
                return e.copy(out=uaT[:, rc * 128:(rc + 1) * 128],
                              in_=psTb[0:50, 0:128])

        @E("sync", "ld", inc_n=16, waits=[("ve", c("ve")), ("ld", c("ld"))])
        def _(e): return e.dma_start(out=wstage[0:100, :], in_=da_h)
        @E("vector", "ve", waits=[("ld", c("ld")), ("sc", c("sc")), ("mm", c("mm"))])
        def _(e): return e.tensor_copy(castbuf[0:100, :], wstage[0:100, :])
        ve_da = c("ve")
        for j in range(2):
            @E("tensor", "mm", waits=([("ve", ve_da), ("sc", c("sc"))] if j == 0 else ()))
            def _(e, j=j):
                return e.matmul(psR[0:4, j * 512:(j + 1) * 512], ldT[:],
                                castbuf[0:100, j * 512:(j + 1) * 512],
                                start=True, stop=True)
        @E("scalar", "sc", waits=[("mm", c("mm"))])
        def _(e): return e.copy(out=m1[:], in_=psR[0:4, 0:1024])
        for j in range(2):
            @E("tensor", "mm", waits=[("sc", c("sc"))])
            def _(e, j=j):
                return e.matmul(psR[0:50, j * 512:(j + 1) * 512], luT[:],
                                m1[:, j * 512:(j + 1) * 512], start=True, stop=True)
        @E("scalar", "sc", waits=[("mm", c("mm"))])
        def _(e): return e.copy(out=m2[:], in_=psR[0:50, 0:1024])
        wstate["cast"][id(wstage)] = max(wstate["cast"].get(id(wstage), 0), c("ve"))
        wstate["tp"][id(castbuf)] = max(wstate["tp"].get(id(castbuf), 0), c("mm"))

        trk = {"sck": [0, 0], "kv": [0, 0], "gpu": [0, 0], "min": [0, 0],
               "red": [0, 0], "last_sck": slot_done.get(id(slotC), 0),
               "last_kv": 0,
               "wkd": slot_done.get(id(slotA), 0),
               "wvd": slot_done.get(id(slotB), 0),
               "wrd": slot_done.get(id(slotC), 0)}
        rdy_own = []
        for m in range(8):
            prefix_iter(0, m, trk, rdy_pre)
        for m in range(8):
            prefix_iter(1, m, trk, rdy_pre)
            for i2 in ([2 * m, 2 * m + 1] if m < 4 else []):
                push_chunk(ln_chunk_p1(x_own, i2,
                                       tag=("own" if i2 == 0 else "")), rdy_own)
        flush_chunks(rdy_own)
        carry_combine()
        dbg("small_prefix", small[:])

        trk.update({"state": [0, 0], "yst": [0, 0],
                    "th": [0, 0], "last_th": 0, "y2": 0, "mm_stats": 0,
                    "y2l": [0] * 8})
        for m in range(8):
            mid_iter(0, m, trk, rdy_own)
            for i2 in ([8 + 2 * m, 9 + 2 * m] if m < 4 else []):
                push_chunk(ln_chunk_p1(x_own, i2, use_castbuf=True), rdy_own)
        flush_chunks(rdy_own)
        mid_half_end(0, trk)
        stages_sw = (wstage, xt1)
        cbufs_sw = (castbuf, castbuf2)
        wstate["tp"][id(castbuf)] = max(wstate["tp"].get(id(castbuf), 0),
                                        last_tp[0])
        wstate["tp"][id(castbuf2)] = max(wstate["tp"].get(id(castbuf2), 0),
                                         last_tp[1])
        trk["y2l"] = [0] * 8
        swl = []
        for m in range(8):
            def rounds(m=m):
                weight_round(Wo_h, slotA, m, stages_sw[0], cbufs_sw[0],
                             scale_lnw=True)
                m3v = []
                sw = weight_round(Wf_h, slotB, m, stages_sw[1], cbufs_sw[1],
                                  m3_ps=psV, m3_w=[("ve", trk["last_kv"])],
                                  out_ve=m3v,
                                  dma_w=[("sc", last_apply[1]),
                                         ("ve", last_bn[1])])
                trk["last_kv"] = max(trk["last_kv"], m3v[0])
                swl.append(sw)
            mid_iter(1, m, trk, rdy_own, post_clamp=rounds)
        sc_swap = swl[-1]
        mid_half_end(1, trk)
        ld_he1 = c("ld")
        dbg("small_mid", small[:])
        dbg("lnTown", lnT[:])
        dbg("yA", yA[:])
        last_tp[0] = last_tp[1] = c("mm")
        part3(sc_swap, ld_he1, c("gp"),
              max(trk["last_sck"], trk["last_th"]))
        dbg("invcols", invcols[:])

        for s in ("st0", "st1", "st2"):
            fv = c(s)
            @E("sync", waits=[])
            def _(e, s=s, fv=fv): return e.wait_ge(SEM[s], fv)

        _OWN = {"vector": s_ve, "gpsimd": s_gp}

        def replay(key):
            own = _OWN.get(key)

            def body(eng):
                done = 0
                for fn, waits, inc, inc_n in p.streams[key]:
                    if own is not None and done > 0:
                        eng.wait_ge(own, done)
                    for (sem, v) in waits:
                        eng.wait_ge(sem, v)
                    ins = fn(eng)
                    if inc is not None:
                        ins.then_inc(inc, inc_n)
                        if inc is own:
                            done += inc_n
            return body

        block.sync(replay("sync"))
        block.tensor(replay("tensor"))
        block.vector(replay("vector"))
        block.scalar(replay("scalar"))
        block.gpsimd(replay("gpsimd"))

    return nc


_nc_cache = None


def kernel(**inputs):
    global _nc_cache
    x = np.ascontiguousarray(np.asarray(inputs["L0_attn_x"], dtype=np.float32))
    Wr = np.ascontiguousarray(np.asarray(inputs["L0_attn_Wr"], dtype=np.float32))
    Wk = np.ascontiguousarray(np.asarray(inputs["L0_attn_Wk"], dtype=np.float32))
    Wv = np.ascontiguousarray(np.asarray(inputs["L0_attn_Wv"], dtype=np.float32))
    Wo = np.ascontiguousarray(np.asarray(inputs["L0_attn_Wo"], dtype=np.float32))
    Wf = np.ascontiguousarray(np.asarray(inputs["L0_ffn_W"], dtype=np.float32))
    decay = np.asarray(inputs["L0_attn_decay"], dtype=np.float32)
    lnw = np.asarray(inputs["L0_attn_ln_w"], dtype=np.float32)
    bf = np.asarray(inputs["L0_ffn_b"], dtype=np.float32)
    ld = np.ascontiguousarray(np.asarray(inputs["L0_ffn_lora_down"], dtype=np.float32))
    lu = np.ascontiguousarray(np.asarray(inputs["L0_ffn_lora_up"], dtype=np.float32))
    da = np.ascontiguousarray(np.asarray(inputs["L0_ffn_down_aux"], dtype=np.float32))
    ua = np.ascontiguousarray(np.asarray(inputs["L0_ffn_up_aux"], dtype=np.float32))

    decay8 = np.ascontiguousarray(decay.reshape(8, 128).T)
    lnw8 = np.ascontiguousarray(lnw.reshape(8, 128).T)
    b_row = np.ascontiguousarray(bf.reshape(1, D))
    ident = np.eye(128, dtype=ml_dtypes.bfloat16)

    if _nc_cache is None:
        _nc_cache = build_nc()
    nc = _nc_cache

    in_maps = []
    for cid in range(8):
        b, h = cid // 2, cid % 2
        in_maps.append({
            "x_own": np.ascontiguousarray(x[b, h * HALF:(h + 1) * HALF]),
            "x_pre": np.ascontiguousarray(x[b, 0:HALF]),
            "Wr": Wr, "Wk": Wk, "Wv": Wv, "Wo": Wo, "Wf": Wf,
            "decay8": decay8, "lnw8": lnw8, "b_row": b_row,
            "lora_down": ld, "lora_up": lu, "down_aux": da, "up_aux": ua,
            "mask_col": np.full((128, 1), float(h), np.float32),
            "t0_col": np.full((128, 1), float(h * HALF), np.float32),
            "ident": ident,
        })

    res = run_bass_kernel_spmd(nc, in_maps, core_ids=list(range(8)))
    out = np.empty((B, T, D), np.float32)
    for cid in range(8):
        b, h = cid // 2, cid % 2
        out[b, h * HALF:(h + 1) * HALF] = res.results[cid]["out"]
    return out
